# version 06
# baseline (speedup 1.0000x reference)
import math
from contextlib import ExitStack
import numpy as np
import ml_dtypes
import concourse.bass as bass
import concourse.mybir as mybir
from concourse.ap import AP
from concourse.bass_utils import run_bass_kernel_spmd

F32 = mybir.dt.float32
BF16 = mybir.dt.bfloat16
AF = mybir.ActivationFunctionType
ALU = mybir.AluOpType
AX = mybir.AxisListType
BF = ml_dtypes.bfloat16
NEG = -1.0e30

D = 1024
S = 4096
B = 4
TOK = 2048
NC = 8
FC = D // 128
EPS = 1e-5


class Sched:
    ENGINES = ("sync", "scalar", "tensor", "vector", "gpsimd")

    def __init__(self, nc):
        self.nc = nc
        self.ops = {e: [] for e in self.ENGINES}
        self.counts = {}
        self.chain = {}

    def enable_chain(self, **eng_sems):
        self.chain.update(eng_sems)

    def val(self, sem):
        return self.counts.get(id(sem), 0)

    def _bump(self, sem, n):
        v = self.counts.get(id(sem), 0) + n
        self.counts[id(sem)] = v
        return v

    def call(self, engine, method, *args, wait=(), inc=None, inc_n=1, **kwargs):
        wait = list(wait)
        cs = self.chain.get(engine)
        if cs is not None and method != "dma_start":
            wait.append((cs, self.val(cs)))
            if inc is None:
                inc = cs
            else:
                assert inc is cs, f"chained engine {engine} op with inc != chain sem"
        tgt = self._bump(inc, inc_n) if inc is not None else None
        waits = tuple((s, v) for (s, v) in wait if v is not None)

        def thunk(eng, method=method, args=args, kwargs=kwargs, waits=waits,
                  inc=inc, inc_n=inc_n):
            for (s, v) in waits:
                eng.wait_ge(s, v)
            ins = getattr(eng, method)(*args, **kwargs)
            if inc is not None:
                ins.then_inc(inc, inc_n)
        self.ops[engine].append(thunk)
        return tgt

    def dma(self, engine, out_ap, in_ap, sem, wait=()):
        return self.call(engine, "dma_start", wait=wait, inc=sem, inc_n=16,
                         out=out_ap, in_=in_ap)

    def raw(self, engine, fn, wait=(), inc=None, inc_n=1):
        waits = tuple((s, v) for (s, v) in wait if v is not None)
        tgt = self._bump(inc, inc_n) if inc is not None else None

        def thunk(eng, fn=fn, waits=waits, inc=inc, inc_n=inc_n):
            for (s, v) in waits:
                eng.wait_ge(s, v)
            ins = fn(eng)
            if inc is not None:
                ins.then_inc(inc, inc_n)
        self.ops[engine].append(thunk)
        return tgt

    def barrier_wait(self, engine, sem, val):
        def thunk(eng, sem=sem, val=val):
            eng.wait_ge(sem, val)
        self.ops[engine].append(thunk)

    def run(self, block):
        for e in self.ENGINES:
            def body(eng, ops=self.ops[e]):
                for f in ops:
                    f(eng)
            getattr(block, e)(body)






class Mega:

    def __init__(self):
        self.nc = bass.Bass("TRN2", target_bir_lowering=False)
        self.sch = Sched(self.nc)
        self.ctx = ExitStack()
        self._sems = {}
        self.psum = [self.ctx.enter_context(
            self.nc.psum_tensor(f"mps{i}", [128, 512], F32)) for i in range(8)]
        self.consts = {}
        self.inmap = {}

    def sem(self, name):
        if name not in self._sems:
            self._sems[name] = self.ctx.enter_context(self.nc.semaphore(name))
        return self._sems[name]

    def sems(self, *names):
        return [self.sem(n) for n in names]

    def barrier(self):
        vals = [(s, self.sch.val(s)) for s in self._sems.values()
                if self.sch.val(s) > 0]
        for e in Sched.ENGINES:
            for (s, v) in vals:
                self.sch.barrier_wait(e, s, v)

    def inp(self, name, shape, dtype, data_fn):
        ap = self.nc.dram_tensor(name, shape, dtype, kind="ExternalInput").ap()
        self.inmap[name] = data_fn
        return ap

    def scratch(self, name, shape, dtype):
        return self.nc.dram_tensor(name, shape, dtype, kind="Internal").ap()




class BlockEnv:

    def __init__(self, mega, pfx):
        self.mega = mega
        self.pfx = pfx
        if mega is None:
            self.nc = bass.Bass("TRN2", target_bir_lowering=False)
            self.sch = Sched(self.nc)
            self._psum_i = 0
        else:
            self.nc = mega.nc
            self.sch = mega.sch
            self._psum_i = 0
        self.ctx = ExitStack()
        self._local_sems = {}

    def nm(self, name):
        return name if self.mega is None else self.pfx + name

    def dram_in(self, name, shape, dtype):
        if self.mega is None:
            return self.nc.dram_tensor(name, shape, dtype, kind="ExternalInput").ap()
        return self.nc.dram_tensor(self.pfx + name, shape, dtype,
                                   kind="ExternalInput").ap()

    def dram_out(self, name, shape, dtype):
        if self.mega is None:
            return self.nc.dram_tensor(name, shape, dtype, kind="ExternalOutput").ap()
        return self.mega.scratch(self.pfx + name, shape, dtype)

    def dram_int(self, name, shape, dtype):
        nm = name if self.mega is None else self.pfx + name
        return self.nc.dram_tensor(nm, shape, dtype, kind="Internal").ap()

    def sb(self, name, shape, dtype):
        return self.ctx.enter_context(self.nc.sbuf_tensor(name if self.mega is None
                                                          else self.pfx + name,
                                                          shape, dtype))

    def ps(self):
        if self.mega is None:
            h = self.ctx.enter_context(
                self.nc.psum_tensor(f"ps{self._psum_i}", [128, 512], F32))
        else:
            h = self.mega.psum[self._psum_i]
        self._psum_i += 1
        return h

    def sem(self, name):
        if self.mega is not None:
            return self.mega.sem(name)
        if name not in self._local_sems:
            self._local_sems[name] = self.ctx.enter_context(self.nc.semaphore(name))
        return self._local_sems[name]

    def sem_list(self, name, n):
        return [self.sem(f"{name}{i}") for i in range(n)]

    def start(self):
        if self.mega is None:
            self.sch.enable_chain(vector=self.sem("s_dv"), scalar=self.sem("s_ev"))
        else:
            self.mega.sch.chain = {"vector": self.sem("s_dv"),
                                   "scalar": self.sem("s_ev")}

    def finish(self):
        if self.mega is None:
            with self.nc.Block() as block:
                self.sch.run(block)
            self.ctx.close()
            return self.nc
        self.ctx.close()
        self.mega.barrier()
        return None


def emit_ln_fm(sch, env, ntok):
    nc = sch.nc
    xn = env["xn"]
    ps_sum, ps_sq = env["ps_sum"], env["ps_sq"]
    ps_ba, ps_bb = env["ps_ba"], env["ps_bb"]
    s_pe, s_ev, s_dv = env["s_pe"], env["s_ev"], env["s_dv"]
    stat, stat2, stat3 = env["stat"], env["stat2"], env["stat3"]
    xsq = env["xsq"]
    first_wait = list(env.get("xn_ready", []))
    dv_apply_done = env.get("ln_prev_dv")
    prev_bcast_pe = env.get("ln_prev_bcast_pe")
    xsq_pe = env.get("ln_xsq_pe", [None, None])

    if "chunk_fn" in env:
        def xnc(c, cols, _f=env["chunk_fn"]):
            return _f(c)[:, cols]
    else:
        def xnc(c, cols):
            return xn[:, c * ntok:(c + 1) * ntok][:, cols]

    NTC = ntok // 512
    for tc in range(NTC):
        cols = slice(tc * 512, (tc + 1) * 512)
        for c in range(FC):
            slot = c % 2
            xq = xsq[:, slot * 512:slot * 512 + 512]
            w = list(first_wait) if tc == 0 and c == 0 else []
            w.append((s_pe, xsq_pe[slot]))
            ev = sch.call("scalar", "activation", xq, xnc(c, cols), AF.Square,
                          wait=w, inc=s_ev)
            sch.call("tensor", "matmul", ps_sum[0:1, 0:512], env["onescb"][:, 0:1],
                     xnc(c, cols), start=(c == 0), stop=(c == FC - 1),
                     wait=(list(first_wait) if tc == 0 and c == 0 else [])
                     + ([(s_dv, dv_apply_done)] if c == 0 else []))
            xsq_pe[slot] = sch.call("tensor", "matmul", ps_sq[0:1, 0:512],
                                    env["onescb"][:, 0:1], xq,
                                    start=(c == 0), stop=(c == FC - 1),
                                    wait=[(s_ev, ev)], inc=s_pe)
        pe_stats = sch.val(s_pe)
        r_mean = stat[0:1, :]
        r_var = stat2[0:1, :]
        dv = sch.call("vector", "tensor_scalar_mul", r_mean, ps_sum[0:1, 0:512],
                      1.0 / D, wait=[(s_pe, pe_stats), (s_pe, prev_bcast_pe)],
                      inc=s_dv)
        dv = sch.call("vector", "tensor_scalar_mul", r_var, ps_sq[0:1, 0:512],
                      1.0 / D, wait=[(s_dv, dv)], inc=s_dv)
        dv = sch.call("vector", "tensor_mul", stat3[0:1, :], r_mean, r_mean,
                      wait=[(s_dv, dv)], inc=s_dv)
        dv = sch.call("vector", "tensor_sub", r_var, r_var, stat3[0:1, :],
                      wait=[(s_dv, dv)], inc=s_dv)
        ev = sch.call("scalar", "activation", stat3[0:1, :], r_var, AF.Sqrt,
                      bias=env["epsc"][0:1, 0:1], wait=[(s_dv, dv)], inc=s_ev)
        dv = sch.call("vector", "reciprocal", r_var, stat3[0:1, :],
                      wait=[(s_ev, ev)], inc=s_dv)
        dv2 = sch.call("vector", "scalar_tensor_tensor", r_mean, r_mean, -1.0,
                       r_var, ALU.mult, ALU.mult, wait=[(s_dv, dv)], inc=s_dv)
        sch.call("tensor", "matmul", ps_ba[:, 0:512], env["ones1f"][0:1, :],
                 r_var, start=True, stop=True,
                 wait=[(s_dv, dv2), (s_dv, dv_apply_done)])
        pe_b = sch.call("tensor", "matmul", ps_bb[:, 0:512], env["ones1f"][0:1, :],
                        r_mean, start=True, stop=True, inc=s_pe)
        prev_bcast_pe = pe_b
        for c in range(FC):
            sch.call("vector", "tensor_mul", xnc(c, cols), xnc(c, cols),
                     ps_ba[:, 0:512], wait=([(s_pe, pe_b)] if c == 0 else []))
            nd = sch.call("vector", "tensor_add", xnc(c, cols), xnc(c, cols),
                          ps_bb[:, 0:512], inc=(s_dv if c == FC - 1 else None))
            dv_apply_done = nd or dv_apply_done
    env["ln_prev_dv"] = dv_apply_done
    env["ln_prev_bcast_pe"] = prev_bcast_pe
    env["ln_xsq_pe"] = xsq_pe
    return dv_apply_done


def fmajor(a):
    return np.ascontiguousarray(np.asarray(a, dtype=np.float32).T)


def _ident128():
    return np.eye(128, dtype=BF)


def _ones_row(dtype=np.float32):
    return np.ones((1, 128), dtype=dtype)


def causal_cmask(qc_w=256):
    j = np.arange(128)[:, None]
    i = np.arange(qc_w)[None, :]
    out = np.stack([np.where(j + 0 <= i, 0.0, NEG), np.where(j + 128 <= i, 0.0, NEG)])
    return out.astype(BF)


def chunk_cols(x_1d, p=128):
    n = np.asarray(x_1d).shape[0] // p
    return np.ascontiguousarray(np.asarray(x_1d, np.float32).reshape(n, p).T)


L0_QC = 256
L0_NQC = TOK // L0_QC
NKT = S // 128


def build_l0_attn(mega=None, io=None):
    E = BlockEnv(mega, "l0a_")
    nc = E.nc
    if mega is None:
        xf = nc.dram_tensor("xf", [D, S], F32, kind="ExternalInput").ap()
        xo_in = nc.dram_tensor("xo_in", [D, TOK], F32, kind="ExternalInput").ap()
        xo = nc.dram_tensor("xo", [D, TOK], F32, kind="ExternalOutput").ap()
    else:
        xf = io["xf"]
        xo_in = io["xo_in"]
        xo = io["xo"]
    wqT = E.dram_in("wqT", [D, D], F32)
    wkT = E.dram_in("wkT", [D, D], F32)
    wvT = E.dram_in("wvT", [D, D], F32)
    bqc = E.dram_in("bqc", [128, FC], F32)
    bvc = E.dram_in("bvc", [128, FC], F32)
    kmask = E.dram_in("kmask", [128, NKT], F32)
    cmask = E.dram_in("cmask", [2, 128, L0_QC], BF16)
    ident_in = E.dram_in("ident_in", [128, 128], BF16)
    onesb_in = E.dram_in("onesb_in", [1, 128], BF16)
    onesf_in = E.dram_in("onesf_in", [1, 128], F32)
    onescb_in = E.dram_in("onescb_in", [128, 1], BF16)
    onescf_in = E.dram_in("onescf_in", [128, 1], F32)
    epsc_in = E.dram_in("epsc_in", [1, 1], F32)
    ksc = E.dram_int("ksc", [FC, 128, S], BF16)
    vsc = E.dram_int("vsc", [NKT, 128, D], BF16)

    if True:
        _e = E.ctx.enter_context
        xn = E.sb("xn", [128, FC * S], BF16)
        kst = E.sb("kst", [128, 4 * D], BF16)
        vst = E.sb("vst", [128, 4 * D], BF16)
        kevac = E.sb("kevac", [128, 2 * 512], BF16)
        vevac = E.sb("vevac", [128, 2 * D], BF16)
        qq = E.sb("qq", [128, FC * TOK], BF16)
        wq = E.sb("wq", [128, FC * D], BF16)
        wk = E.sb("wk", [128, FC * D], BF16)
        wv = E.sb("wv", [128, FC * D], BF16)
        xsq = E.sb("xsq", [128, 2 * 512], BF16)
        stat = E.sb("stat", [1, 512], F32)
        stat2 = E.sb("stat2", [1, 512], F32)
        stat3 = E.sb("stat3", [1, 512], F32)
        bqv = E.sb("bqv", [128, 2 * FC], F32)
        kms = E.sb("kms", [128, NKT], F32)
        cms = E.sb("cms", [128, 2 * L0_QC], BF16)
        ident = E.sb("ident", [128, 128], BF16)
        ones1 = E.sb("ones1", [1, 128], BF16)
        ones1f = E.sb("ones1f", [1, 128], F32)
        onescb = E.sb("onescb", [128, 1], BF16)
        onescf = E.sb("onescf", [128, 1], F32)
        epsc = E.sb("epsc", [1, 1], F32)
        expt = E.sb("expt", [128, 4 * L0_QC], BF16)
        zacc = E.sb("zacc", [128, L0_QC], F32)
        rz = E.sb("rz", [1, L0_QC], F32)
        rzb = E.sb("rzb", [128, L0_QC], F32)
        xdm = E.sb("xdm", [128, FC * L0_QC], F32)
        xob = E.sb("xob", [128, FC * L0_QC], F32)
        uu = E.sb("uu", [128, FC * L0_QC], F32)
        ps_a = E.ps()
        ps_b = E.ps()
        pc0 = E.ps()
        pc1 = E.ps()
        pc2 = E.ps()
        pc3 = E.ps()
        ps_st = E.ps()
        ps_z = E.ps()
        s_ld = E.sem("s_ld")
        s_lg = E.sem("s_lg")
        s_pe = E.sem("s_pe")
        s_ev = E.sem("s_ev")
        s_dv = E.sem("s_dv")
        s_xd = E.sem("s_xd")
        s_kv = E.sem_list("s_kv", 4)
        s_sc = E.sem_list("s_sc", 4)
        s_st = E.sem("s_st")
        sch = E.sch
        E.start()
        PC = (pc0, pc1, pc2, pc3)

        sch.dma("sync", ident[:], ident_in[:, :], s_ld)
        sch.dma("sync", ones1[:], onesb_in[:, :], s_ld)
        sch.dma("sync", ones1f[:], onesf_in[:, :], s_ld)
        sch.dma("sync", onescb[:], onescb_in[:, :], s_ld)
        sch.dma("sync", onescf[:], onescf_in[:, :], s_ld)
        sch.dma("sync", epsc[:], epsc_in[:, :], s_ld)
        sch.dma("sync", cms[:, 0:L0_QC], cmask[0, :, :], s_ld)
        sch.dma("sync", cms[:, L0_QC:], cmask[1, :, :], s_ld)
        sch.dma("sync", kms[:], kmask[:, :], s_ld)
        sch.dma("sync", bqv[:, 0:FC], bqc[:, :], s_ld)
        sch.dma("sync", bqv[:, FC:], bvc[:, :], s_ld)
        for (wt, wsrc) in ((wq, wqT), (wk, wkT), (wv, wvT)):
            for c in range(FC):
                sch.dma("gpsimd", wt[:, c * D:(c + 1) * D],
                        wsrc[c * 128:(c + 1) * 128, :], s_lg)
        for c in range(FC):
            sch.dma("gpsimd", xn[:, c * S:(c + 1) * S],
                    xf[c * 128:(c + 1) * 128, :], s_lg)
        all_ld = sch.val(s_ld)
        all_lg = sch.val(s_lg)

        def xnc(c, cols):
            return xn[:, c * S:(c + 1) * S][:, cols]

        NTC = S // 512
        dv_apply_done = None
        prev_bcast_pe = None
        xsq_pe = [None, None]
        for tc in range(NTC):
            cols = slice(tc * 512, (tc + 1) * 512)
            for c in range(FC):
                slot = c % 2
                xq = xsq[:, slot * 512:slot * 512 + 512]
                w = []
                if tc == 0 and c == 0:
                    w.append((s_ld, all_ld)); w.append((s_lg, all_lg))
                w.append((s_pe, xsq_pe[slot]))
                ev = sch.call("scalar", "activation", xq, xnc(c, cols), AF.Square,
                              wait=w, inc=s_ev)
                sch.call("tensor", "matmul", ps_st[0:1, 0:512], onescb[:, 0:1],
                         xnc(c, cols), start=(c == 0), stop=(c == FC - 1),
                         wait=([(s_ld, all_ld), (s_lg, all_lg)] if tc == 0 and c == 0 else [])
                         + ([(s_dv, dv_apply_done)] if c == 0 else []))
                xsq_pe[slot] = sch.call("tensor", "matmul", ps_z[0:1, 0:512],
                                        onescb[:, 0:1], xq,
                                        start=(c == 0), stop=(c == FC - 1),
                                        wait=[(s_ev, ev)], inc=s_pe)
            pe_stats = sch.val(s_pe)
            r_mean = stat[0:1, :]
            r_var = stat2[0:1, :]
            sch.call("vector", "tensor_scalar_mul", r_mean, ps_st[0:1, 0:512],
                     1.0 / D, wait=[(s_pe, pe_stats), (s_pe, prev_bcast_pe)])
            sch.call("vector", "tensor_scalar_mul", r_var, ps_z[0:1, 0:512], 1.0 / D)
            sch.call("vector", "tensor_mul", stat3[0:1, :], r_mean, r_mean)
            dv = sch.call("vector", "tensor_sub", r_var, r_var, stat3[0:1, :],
                          inc=s_dv)
            ev = sch.call("scalar", "activation", stat3[0:1, :], r_var, AF.Sqrt,
                          bias=epsc[0:1, 0:1], wait=[(s_dv, dv)], inc=s_ev)
            dv = sch.call("vector", "reciprocal", r_var, stat3[0:1, :],
                          wait=[(s_ev, ev)], inc=s_dv)
            dv2 = sch.call("vector", "scalar_tensor_tensor", r_mean, r_mean, -1.0,
                           r_var, ALU.mult, ALU.mult, wait=[(s_dv, dv)], inc=s_dv)
            sch.call("tensor", "matmul", ps_a[:, 0:512], ones1f[0:1, :],
                     r_var, start=True, stop=True,
                     wait=[(s_dv, dv2), (s_dv, dv_apply_done)])
            pe_b = sch.call("tensor", "matmul", ps_b[:, 0:512], ones1f[0:1, :],
                            r_mean, start=True, stop=True, inc=s_pe)
            prev_bcast_pe = pe_b
            for c in range(FC):
                sch.call("vector", "tensor_mul", xnc(c, cols), xnc(c, cols),
                         ps_a[:, 0:512], wait=([(s_pe, pe_b)] if c == 0 else []))
                dv_apply_done = sch.call("vector", "tensor_add", xnc(c, cols),
                                         xnc(c, cols), ps_b[:, 0:512],
                                         inc=(s_dv if c == FC - 1 else None)) \
                    or dv_apply_done
        ln_done_dv = dv_apply_done

        pp = [ps_a, ps_b]
        pi = 0
        prev_ev = [(s_ev, None), (s_ev, None)]
        kev_dma = [None, None]
        vev_dma = [None, None]

        NTB = S // 512
        for tb in range(NTB):
            cols = slice(tb * 512, (tb + 1) * 512)
            for fo in range(FC):
                bank = pp[pi % 2]
                slot = pi % 2
                pi += 1
                w = [(s_dv, ln_done_dv), prev_ev[slot]]
                for c in range(FC):
                    sch.call("tensor", "matmul", bank[:, 0:512],
                             wk[:, c * D + fo * 128: c * D + fo * 128 + 128],
                             xnc(c, cols), start=(c == 0), stop=(c == FC - 1),
                             wait=(w if c == 0 else []),
                             inc=(s_pe if c == FC - 1 else None))
                pe = sch.val(s_pe)
                kslot = (tb * FC + fo) % 2
                ev = sch.call("scalar", "activation",
                              kevac[:, kslot * 512:(kslot + 1) * 512],
                              bank[:, 0:512], AF.Copy,
                              wait=[(s_pe, pe), (s_sc[kslot], kev_dma[kslot])], inc=s_ev)
                prev_ev[slot] = (s_ev, ev)
                kev_dma[kslot] = sch.dma("gpsimd", ksc[fo, :, cols],
                                         kevac[:, kslot * 512:(kslot + 1) * 512],
                                         s_sc[kslot], wait=[(s_ev, ev)])
            if tb >= 4:
                for fo in range(FC):
                    bank = pp[pi % 2]
                    slot = pi % 2
                    pi += 1
                    w = [(s_dv, ln_done_dv), prev_ev[slot]]
                    for c in range(FC):
                        sch.call("tensor", "matmul", bank[:, 0:512],
                                 wq[:, c * D + fo * 128: c * D + fo * 128 + 128],
                                 xnc(c, cols), start=(c == 0), stop=(c == FC - 1),
                                 wait=(w if c == 0 else []),
                                 inc=(s_pe if c == FC - 1 else None))
                    pe = sch.val(s_pe)
                    qdv = sch.call(
                        "vector", "tensor_scalar",
                        qq[:, fo * TOK + (tb - 4) * 512: fo * TOK + (tb - 3) * 512],
                        bank[:, 0:512], bqv[:, fo:fo + 1], None, ALU.add,
                        wait=[(s_pe, pe)], inc=s_dv)
                    prev_ev[slot] = (s_dv, qdv)
            for tt in range(4):
                kt = tb * 4 + tt
                tcol = slice(kt * 128, (kt + 1) * 128)
                vslot = kt % 2
                for half in range(2):
                    bank = pp[pi % 2]
                    slot = pi % 2
                    pi += 1
                    w = [(s_dv, ln_done_dv), prev_ev[slot]]
                    for c in range(FC):
                        sch.call("tensor", "matmul", bank[:, 0:512],
                                 xnc(c, tcol),
                                 wv[:, c * D + half * 512: c * D + (half + 1) * 512],
                                 start=(c == 0), stop=(c == FC - 1),
                                 wait=(w if c == 0 else []),
                                 inc=(s_pe if c == FC - 1 else None))
                    pe = sch.val(s_pe)
                    ev = sch.call("scalar", "activation",
                                  vevac[:, vslot * D + half * 512: vslot * D + (half + 1) * 512],
                                  bank[:, 0:512], AF.Copy,
                                  wait=[(s_pe, pe)]
                                  + ([(s_sc[2 + vslot], vev_dma[vslot])] if half == 0 else []),
                                  inc=s_ev)
                    prev_ev[slot] = (s_ev, ev)
                vev_dma[vslot] = sch.dma("gpsimd", vsc[kt, :, :],
                                         vevac[:, vslot * D:(vslot + 1) * D],
                                         s_sc[2 + vslot], wait=[(s_ev, ev)])
        scr_done = [(x, sch.val(x)) for x in s_sc]
        proj_all_ev = sch.val(s_ev)
        proj_all_dv = sch.val(s_dv)

        sp = [ps_a, ps_b]
        sprev = [(s_ev, proj_all_ev), (s_ev, proj_all_ev)]
        kv_slot_pe = [None, None, None, None]
        zacc_dv = [None, None, None, None]
        kv_ld = [None, None, None, None]
        ctx_read_dv = None
        si = 0
        for qc in range(L0_NQC):
            def qcols(c, qc=qc):
                return slice(c * TOK + qc * L0_QC, c * TOK + (qc + 1) * L0_QC)
            n_kt = 16 + 2 * (qc + 1)
            for c in range(FC):
                sch.dma("sync", xdm[:, c * L0_QC:(c + 1) * L0_QC],
                        xo_in[c * 128:(c + 1) * 128, qc * L0_QC:(qc + 1) * L0_QC],
                        s_xd, wait=([(s_dv, ctx_read_dv)] if c == 0 else []))
            xd_ld = sch.val(s_xd)
            for c in range(FC):
                sch.call("vector", "tensor_scalar", xob[:, c * L0_QC:(c + 1) * L0_QC],
                         xdm[:, c * L0_QC:(c + 1) * L0_QC],
                         bqv[:, FC + c:FC + c + 1], None, ALU.add,
                         wait=([(s_xd, xd_ld), (s_dv, ctx_read_dv)] if c == 0 else []))
            last_zacc_dv = None
            for ki in range(n_kt):
                kt = ki
                eslot = ki % 4
                lw = [(s_pe, kv_slot_pe[eslot])]
                if qc == 0 and ki == 0:
                    lw += scr_done
                ksrc = AP(tensor=ksc.tensor, offset=kt * 128,
                          ap=[[S, 128], [128 * S, FC], [1, 128]])
                sch.dma("gpsimd",
                        kst[:, eslot * D:(eslot + 1) * D].rearrange("p (f t) -> p f t", f=FC),
                        ksrc, s_kv[eslot], wait=lw)
                kv_ld[eslot] = sch.dma("gpsimd", vst[:, eslot * D:(eslot + 1) * D],
                                       vsc[kt, :, :], s_kv[eslot])
                bank = sp[si % 2]
                slot = si % 2
                si += 1
                dvar = None
                if kt == 16 + 2 * qc:
                    dvar = 0
                elif kt == 16 + 2 * qc + 1:
                    dvar = 1
                w = [sprev[slot], (s_kv[eslot], kv_ld[eslot])]
                if qc == 0 and ki == 0:
                    w.append((s_dv, proj_all_dv))
                pe_sc = None
                for c in range(FC):
                    last_qk = (c == FC - 1) and dvar is None
                    pe_sc = sch.call("tensor", "matmul", bank[:, 0:L0_QC],
                                     kst[:, eslot * D + c * 128: eslot * D + (c + 1) * 128],
                                     qq[:, qcols(c)], start=(c == 0), stop=last_qk,
                                     wait=(w if c == 0 else []),
                                     inc=(s_pe if last_qk else None)) or pe_sc
                if dvar is not None:
                    pe_sc = sch.call("tensor", "matmul", bank[:, 0:L0_QC], ident[:],
                                     cms[:, dvar * L0_QC:(dvar + 1) * L0_QC],
                                     start=False, stop=True, inc=s_pe)
                et = expt[:, eslot * L0_QC:(eslot + 1) * L0_QC]
                ev = sch.call("scalar", "activation", et, bank[:, 0:L0_QC], AF.Exp,
                              scale=1.0 / 32.0, bias=kms[:, kt:kt + 1],
                              wait=[(s_pe, pe_sc), (s_dv, zacc_dv[eslot])],
                              inc=s_ev)
                sprev[slot] = (s_ev, ev)
                zw = [(s_ev, ev)]
                if ki == 0:
                    zw.append((s_dv, ctx_read_dv))
                    last_zacc_dv = sch.call("vector", "tensor_copy", zacc[:], et,
                                            wait=zw, inc=s_dv)
                else:
                    last_zacc_dv = sch.call("vector", "tensor_add", zacc[:],
                                            zacc[:], et, wait=zw, inc=s_dv)
                zacc_dv[eslot] = last_zacc_dv
                avw = [(s_ev, ev)]
                if ki == 0:
                    avw.append((s_dv, ctx_read_dv))
                pe_av = None
                for fc in range(FC):
                    bankc = PC[fc // 2]
                    reg = slice((fc % 2) * L0_QC, (fc % 2 + 1) * L0_QC)
                    pe_av = sch.call("tensor", "matmul", bankc[:, reg],
                                     vst[:, eslot * D + fc * 128: eslot * D + (fc + 1) * 128],
                                     et, start=(ki == 0), stop=(ki == n_kt - 1),
                                     wait=(avw if fc == 0 else []),
                                     inc=(s_pe if fc == FC - 1 else None)) or pe_av
                kv_slot_pe[eslot] = pe_av
            av_done_pe = sch.val(s_pe)
            pe_z = sch.call("tensor", "matmul", ps_z[0:1, 0:L0_QC], onescf[:, 0:1],
                            zacc[:], start=True, stop=True,
                            wait=[(s_dv, last_zacc_dv)], inc=s_pe)
            dv_rz = sch.call("vector", "reciprocal", rz[0:1, :], ps_z[0:1, 0:L0_QC],
                             wait=[(s_pe, pe_z)], inc=s_dv)
            pe_rb = sch.call("tensor", "matmul", ps_z[:, L0_QC:2 * L0_QC],
                             ones1f[0:1, :], rz[0:1, :], start=True, stop=True,
                             wait=[(s_dv, dv_rz)], inc=s_pe)
            ev_rb = sch.call("scalar", "activation", rzb[:], ps_z[:, L0_QC:2 * L0_QC],
                             AF.Copy, wait=[(s_pe, pe_rb)], inc=s_ev)
            for c in range(FC):
                bankc = PC[c // 2]
                reg = slice((c % 2) * L0_QC, (c % 2 + 1) * L0_QC)
                uc = uu[:, c * L0_QC:(c + 1) * L0_QC]
                sch.call("vector", "tensor_mul", uc, bankc[:, reg], rzb[:],
                         wait=([(s_ev, ev_rb), (s_pe, av_done_pe)] if c == 0 else []))
                ctx_read_dv = sch.call("vector", "tensor_add", uc, uc,
                                       xob[:, c * L0_QC:(c + 1) * L0_QC],
                                       inc=(s_dv if c == FC - 1 else None)) \
                    or ctx_read_dv
            for c in range(FC):
                sch.dma("sync", xo[c * 128:(c + 1) * 128, qc * L0_QC:(qc + 1) * L0_QC],
                        uu[:, c * L0_QC:(c + 1) * L0_QC], s_st,
                        wait=([(s_dv, ctx_read_dv)] if c == 0 else []))
        st_all = sch.val(s_st)
        sch.barrier_wait("sync", s_st, st_all)
    return E.finish()


def host_l0_attn(x, W_qkv, b_qkv):
    wq = np.ascontiguousarray(np.asarray(W_qkv, np.float32)[0:D, :].T)
    wk = np.ascontiguousarray(np.asarray(W_qkv, np.float32)[D:2 * D, :].T)
    wv = np.ascontiguousarray(np.asarray(W_qkv, np.float32)[2 * D:3 * D, :].T)
    b_qkv = np.asarray(b_qkv, np.float32)
    bq = chunk_cols(b_qkv[0:D])
    bv = chunk_cols(b_qkv[2 * D:3 * D])
    ident = _ident128()
    onesb = _ones_row(BF)
    onesf = _ones_row(np.float32)
    cmask = causal_cmask(L0_QC)
    in_maps = []
    for core in range(NC):
        b, h = core // 2, core % 2
        own = x[b, h * TOK:(h + 1) * TOK]
        other = x[b, (1 - h) * TOK:(2 - h) * TOK]
        xcat = np.concatenate([other, own], axis=0)
        km = np.zeros((128, NKT), np.float32)
        if h == 0:
            km[:, 0:16] = NEG
        in_maps.append({
            "xf": fmajor(xcat), "xo_in": fmajor(own),
            "wqT": wq, "wkT": wk, "wvT": wv,
            "bqc": bq, "bvc": bv, "kmask": km,
            "cmask": cmask, "ident_in": ident, "onesb_in": onesb, "onesf_in": onesf,
            "onescb_in": np.ascontiguousarray(onesb.T), "onescf_in": np.ascontiguousarray(onesf.T),
            "epsc_in": np.full((1, 1), EPS, np.float32),
        })
    return in_maps


def assemble_fm(results, key="xo"):
    out = np.empty((B, S, D), np.float32)
    for core in range(NC):
        b, h = core // 2, core % 2
        out[b, h * TOK:(h + 1) * TOK] = np.asarray(results[core][key]).T
    return out





TCH = 512
NTCH = TOK // TCH
NE = 8


def build_l0_ffn(debug=False, mega=None, io=None):
    E = BlockEnv(mega, "l0f_")
    nc = E.nc
    if mega is None:
        xi = nc.dram_tensor("xi", [D, TOK], F32, kind="ExternalInput").ap()
        xo = nc.dram_tensor("xo", [D, TOK], F32, kind="ExternalOutput").ap()
    else:
        xi = io["xi"]
        xo = io["xo"]
    rw_in = E.dram_in("rw_in", [D, NE], F32)
    gup_in = E.dram_in("gup_in", [NE, D, 2 * D], F32)
    gub_in = E.dram_in("gub_in", [1, NE * 2 * D], F32)
    dwn_in = E.dram_in("dwn_in", [NE, D, D], F32)
    dwb_in = E.dram_in("dwb_in", [1, NE * D], F32)
    onesb_in = E.dram_in("onesb_in", [1, 128], BF16)
    onesf_in = E.dram_in("onesf_in", [1, 128], F32)
    onescb_in = E.dram_in("onescb_in", [128, 1], BF16)
    onest_in = E.dram_in("onest_in", [1, TCH], BF16)
    identf_in = E.dram_in("identf_in", [128, 128], F32)
    epsc_in = E.dram_in("epsc_in", [1, 1], F32)
    if debug:
        dbg_lg = nc.dram_tensor("dbg_lg", [128, 128], F32, kind="ExternalOutput").ap()
        dbg_ww = nc.dram_tensor("dbg_ww", [128, 128], F32, kind="ExternalOutput").ap()
        dbg_gate = nc.dram_tensor("dbg_gate", [128, TCH], F32, kind="ExternalOutput").ap()
        dbg_act = nc.dram_tensor("dbg_act", [128, TCH], BF16, kind="ExternalOutput").ap()
        dbg_acc = nc.dram_tensor("dbg_acc", [128, D], F32, kind="ExternalOutput").ap()
        dbg_xn = nc.dram_tensor("dbg_xn", [128, TOK], F32, kind="ExternalOutput").ap()
        dbg_m = nc.dram_tensor("dbg_m", [128, 8], F32, kind="ExternalOutput").ap()
        dbg_eq = nc.dram_tensor("dbg_eq", [128, 16], F32, kind="ExternalOutput").ap()

    with ExitStack() as ctx:
        _e = ctx.enter_context
        xn = E.sb("xn", [128, FC * TOK], BF16)

        gu = E.sb("gu", [128, 2 * FC * 2 * D], BF16)
        dw = E.sb("dw", [128, FC * D], BF16)
        gatef = E.sb("gatef", [128, FC * TCH], F32)
        upf = E.sb("upf", [128, FC * TCH], BF16)
        act = E.sb("act", [128, FC * TCH], BF16)
        slu = E.sb("slu", [128, 2 * TCH], F32)
        acc = E.sb("acc", [128, 4 * D], F32)
        xs = E.sb("xs", [128, FC * TCH], F32)
        rw = E.sb("rw", [128, FC * NE], BF16)
        lg = E.sb("lg", [128, 16 * NE], F32)
        ww = E.sb("ww", [128, 16 * NE], F32)
        m1c = E.sb("m1c", [128, 8], F32)
        eqb = E.sb("eqb", [128, 2 * NE], F32)
        gub = E.sb("gub", [1, 2 * 2 * D], BF16)
        dwb = E.sb("dwb", [1, 2 * D], BF16)
        onesb = E.sb("onesb", [1, 128], BF16)
        ones1f = E.sb("ones1f", [1, 128], F32)
        onescb = E.sb("onescb", [128, 1], BF16)
        onest = E.sb("onest", [1, TCH], BF16)
        identf = E.sb("identf", [128, 128], F32)
        epsc = E.sb("epsc", [1, 1], F32)
        stat = E.sb("stat", [1, 512], F32)
        stat2 = E.sb("stat2", [1, 512], F32)
        stat3 = E.sb("stat3", [1, 512], F32)
        xsq = E.sb("xsq", [128, 2 * 512], BF16)
        ps_a = E.ps()
        ps_b = E.ps()
        ps_c = E.ps()
        ps_d = E.ps()
        ps_sum = E.ps()
        ps_sq = E.ps()
        ps_ba = E.ps()
        ps_bb = E.ps()
        s_ld = E.sem("s_ld")
        s_lg = E.sem("s_lg")
        s_pe = E.sem("s_pe")
        s_ev = E.sem("s_ev")
        s_dv = E.sem("s_dv")
        s_wl = E.sem_list("s_wl", 2)
        s_xs = E.sem("s_xs")
        s_st = E.sem("s_st")

        sch = E.sch
        E.start()
        sch.dma("sync", onesb[:], onesb_in[:, :], s_lg)
        sch.dma("sync", ones1f[:], onesf_in[:, :], s_ld)
        sch.dma("sync", onescb[:], onescb_in[:, :], s_ld)
        sch.dma("sync", onest[:], onest_in[:, :], s_ld)
        sch.dma("sync", identf[:], identf_in[:, :], s_ld)
        sch.dma("sync", epsc[:], epsc_in[:, :], s_ld)

        for c in range(FC):
            sch.dma("gpsimd", rw[:, c * NE:(c + 1) * NE],
                    rw_in[c * 128:(c + 1) * 128, :], s_lg)
        for c in range(FC):
            sch.dma("gpsimd", xn[:, c * TOK:(c + 1) * TOK],
                    xi[c * 128:(c + 1) * 128, :], s_lg)
        all_ld = sch.val(s_ld)
        all_lg = sch.val(s_lg)

        env = {"xn": xn, "onescb": onescb, "ones1f": ones1f, "epsc": epsc,
               "stat": stat, "stat2": stat2, "stat3": stat3, "xsq": xsq,
               "ps_sum": ps_sum, "ps_sq": ps_sq, "ps_ba": ps_ba, "ps_bb": ps_bb,
               "s_pe": s_pe, "s_ev": s_ev, "s_dv": s_dv,
               "xn_ready": [(s_ld, all_ld), (s_lg, all_lg)]}
        ln_dv = emit_ln_fm(sch, env, TOK)

        def xnc(c, cols):
            return xn[:, c * TOK:(c + 1) * TOK][:, cols]

        for ti in range(16):
            tcol = slice(ti * 128, (ti + 1) * 128)
            for c in range(FC):
                sch.call("tensor", "matmul", ps_bb[:, ti * NE:(ti + 1) * NE],
                         xnc(c, tcol), rw[:, c * NE:(c + 1) * NE],
                         start=(c == 0), stop=(c == FC - 1),
                         wait=([(s_dv, ln_dv)] if ti == 0 and c == 0 else []),
                         inc=(s_pe if ti == 15 and c == FC - 1 else None))
        pe_lg = sch.val(s_pe)
        dv = sch.call("vector", "tensor_copy", lg[:], ps_bb[:, 0:16 * NE],
                      wait=[(s_pe, pe_lg)], inc=s_dv)
        w_dv = None
        for ti in range(16):
            l8 = lg[:, ti * NE:(ti + 1) * NE]
            m1 = m1c[:, 0:1]; m2 = m1c[:, 1:2]; dd = m1c[:, 2:3]
            s1 = m1c[:, 3:4]; s2 = m1c[:, 4:5]
            eq1 = eqb[:, 0:NE]; eq2 = eqb[:, NE:2 * NE]
            c_ = sch.call("vector", "reduce_max", m1, l8, AX.X,
                          wait=([(s_dv, dv)] if ti == 0 else []), inc=s_dv)
            c_ = sch.call("vector", "tensor_scalar", eq1, l8, m1, None,
                          ALU.is_equal, wait=[(s_dv, c_)], inc=s_dv)
            c_ = sch.call("vector", "scalar_tensor_tensor", eq2, eq1, NEG, l8,
                          ALU.mult, ALU.add, wait=[(s_dv, c_)], inc=s_dv)
            c_ = sch.call("vector", "reduce_max", m2, eq2, AX.X,
                          wait=[(s_dv, c_)], inc=s_dv)
            c_ = sch.call("vector", "tensor_sub", dd, m1, m2,
                          wait=[(s_dv, c_)], inc=s_dv)
            dv2 = sch.call("vector", "tensor_scalar", eq2, l8, m2, None,
                           ALU.is_equal, wait=[(s_dv, c_)], inc=s_dv)
            ev = sch.call("scalar", "activation", s1, dd, AF.Sigmoid,
                          wait=[(s_dv, dv2)], inc=s_ev)
            c_ = sch.call("vector", "tensor_scalar", s2, s1, -1.0, 1.0, ALU.mult,
                          ALU.add, wait=[(s_ev, ev)], inc=s_dv)
            c_ = sch.call("vector", "tensor_scalar", eq1, eq1, s1, None, ALU.mult,
                          wait=[(s_dv, c_)], inc=s_dv)
            w_dv = sch.call("vector", "scalar_tensor_tensor",
                            ww[:, ti * NE:(ti + 1) * NE], eq2, s2, eq1,
                            ALU.mult, ALU.add, wait=[(s_dv, c_)], inc=s_dv)

        pp = [ps_a, ps_b]
        dp = [ps_c, ps_d]
        pi = 0
        di = 0
        gu_ld = [None, None]
        gu_pe = [None, None]
        dw_ld = [None, None]
        dw_pe = [None, None]
        prev_gu_ev = [(s_dv, None), (s_dv, None)]
        prev_dn_dv = [None, None]
        acc_tr_pe = None
        prev_st = None
        st_all = None
        for tc in range(NTCH):
            cols = slice(tc * TCH, (tc + 1) * TCH)
            for c in range(FC):
                sch.dma("sync", xs[:, c * TCH:(c + 1) * TCH],
                        xi[c * 128:(c + 1) * 128, cols], s_xs,
                        wait=([(s_st, prev_st)] if c == 0 else []))
            xs_ld = sch.val(s_xs)
            for e in range(NE):
                gslot = (tc * NE + e) % 2
                gu_ld[gslot] = sch.dma(
                    "gpsimd", gu[:, gslot * FC * 2 * D:(gslot + 1) * FC * 2 * D]
                    .rearrange("p (c f) -> p c f", c=FC),
                    gup_in[e, :, :].rearrange("(c p) f -> p c f", p=128),
                    s_wl[gslot], wait=[(s_pe, gu_pe[gslot])])
                dw_ld[gslot] = sch.dma(
                    "gpsimd", dw[:, :].rearrange("p (c f) -> p c f", c=FC),
                    dwn_in[e, :, :].rearrange("(c p) f -> p c f", p=128),
                    s_wl[gslot], wait=[(s_pe, dw_pe[0]), (s_pe, dw_pe[1])])
                sch.dma("gpsimd", gub[0:1, gslot * 2 * D:(gslot + 1) * 2 * D],
                        gub_in[0:1, e * 2 * D:(e + 1) * 2 * D], s_wl[gslot])
                full = sch.dma("gpsimd", dwb[0:1, gslot * D:(gslot + 1) * D],
                               dwb_in[0:1, e * D:(e + 1) * D], s_wl[gslot])
                gu_ld[gslot] = full
                dw_ld[gslot] = full
                for side in range(2):
                    dst = gatef if side == 0 else upf
                    for gf in range(FC):
                        bank = pp[pi % 2]
                        slot = pi % 2
                        pi += 1
                        w = [(s_wl[gslot], gu_ld[gslot]), prev_gu_ev[slot],
                             (s_ld, all_ld), (s_lg, all_lg)]
                        if tc == 0 and e == 0 and side == 0 and gf == 0:
                            w.append((s_dv, w_dv))
                        gub_base = gslot * FC * 2 * D
                        for c in range(FC):
                            gu3 = gu[:, gub_base:gub_base + FC * 2 * D] \
                                .rearrange("p (cc f) -> p cc f", cc=FC)
                            sch.call("tensor", "matmul", bank[:, 0:TCH],
                                     gu3[:, c, side * D + gf * 128: side * D + (gf + 1) * 128]
                                     if False else gu3[:, c, :][:, side * D + gf * 128: side * D + (gf + 1) * 128],
                                     xnc(c, cols), start=(c == 0), stop=False,
                                     wait=(w if c == 0 else []))
                        gu_pe[gslot] = sch.call(
                            "tensor", "matmul", bank[:, 0:TCH],
                            gub[0:1, gslot * 2 * D + side * D + gf * 128:
                                gslot * 2 * D + side * D + (gf + 1) * 128],
                            onest[0:1, :], start=False, stop=True, inc=s_pe)
                        pe = sch.val(s_pe)
                        if side == 0:
                            prev_gu_ev[slot] = (s_dv, sch.call(
                                "vector", "tensor_scalar",
                                dst[:, gf * TCH:(gf + 1) * TCH], bank[:, 0:TCH],
                                7.0, None, ALU.min, wait=[(s_pe, pe)], inc=s_dv))
                        else:
                            prev_gu_ev[slot] = (s_dv, sch.call(
                                "vector", "tensor_scalar",
                                dst[:, gf * TCH:(gf + 1) * TCH], bank[:, 0:TCH],
                                7.0, -7.0, ALU.min, ALU.max,
                                wait=[(s_pe, pe)], inc=s_dv))
                gu_all_dv = sch.val(s_dv)
                act_dv = None
                slu_dv = [None, None]
                for gf in range(FC):
                    gch = gatef[:, gf * TCH:(gf + 1) * TCH]
                    uch = upf[:, gf * TCH:(gf + 1) * TCH]
                    ach = act[:, gf * TCH:(gf + 1) * TCH]
                    sch_ = slu[:, (gf % 2) * TCH:(gf % 2 + 1) * TCH]
                    ev = sch.call("scalar", "activation", sch_, gch, AF.Silu,
                                  scale=1.702,
                                  wait=([(s_dv, gu_all_dv)] if gf == 0 else [])
                                  + [(s_dv, slu_dv[gf % 2])],
                                  inc=s_ev)
                    sch.call("vector", "tensor_scalar", ach, uch,
                             1.0 / 1.702, 1.0 / 1.702, ALU.mult, ALU.add)
                    act_dv = sch.call("vector", "tensor_mul", ach, ach, sch_,
                                      wait=[(s_ev, ev)], inc=s_dv)
                    slu_dv[gf % 2] = act_dv
                if debug and tc == 0 and e == 0:
                    sch.dma("sync", dbg_gate[:, :], gatef[:, 0:TCH], s_st,
                            wait=[(s_dv, act_dv)])
                    sch.dma("sync", dbg_act[:, :], act[:, 0:TCH], s_st,
                            wait=[(s_dv, act_dv)])
                for tt in range(4):
                    trow = slice(tt * 128, (tt + 1) * 128)
                    wcol = ww[:, (tc * 4 + tt) * NE + e:(tc * 4 + tt) * NE + e + 1]
                    for half in range(2):
                        bank = dp[di % 2]
                        slot = di % 2
                        di += 1
                        w = [(s_wl[gslot], dw_ld[gslot]), (s_dv, act_dv),
                             (s_dv, prev_dn_dv[slot])]
                        for c in range(FC):
                            sch.call("tensor", "matmul", bank[:, 0:512],
                                     act[:, c * TCH:(c + 1) * TCH][:, trow],
                                     dw[:, :].rearrange("p (cc f) -> p cc f", cc=FC)
                                     [:, c, half * 512:(half + 1) * 512],
                                     start=(c == 0), stop=False,
                                     wait=(w if c == 0 else []))
                        dw_pe[gslot] = sch.call(
                            "tensor", "matmul", bank[:, 0:512],
                            onesb[0:1, 0:128],
                            dwb[0:1, gslot * D + half * 512: gslot * D + (half + 1) * 512],
                            start=False, stop=True, inc=s_pe)
                        pe = sch.val(s_pe)
                        aslice = acc[:, tt * D + half * 512: tt * D + (half + 1) * 512]
                        if e == 0:
                            prev_dn_dv[slot] = sch.call(
                                "vector", "tensor_scalar", aslice, bank[:, 0:512],
                                wcol, None, ALU.mult,
                                wait=[(s_pe, pe), (s_pe, acc_tr_pe)], inc=s_dv)
                        else:
                            prev_dn_dv[slot] = sch.call(
                                "vector", "scalar_tensor_tensor", aslice,
                                bank[:, 0:512], wcol, aslice, ALU.mult, ALU.add,
                                wait=[(s_pe, pe)], inc=s_dv)
            if debug and tc == 0:
                sch.dma("sync", dbg_acc[:, :], acc[:, 0:D], s_st,
                        wait=[(s_dv, sch.val(s_dv))])
            acc_dv = sch.val(s_dv)
            uev = None
            for tt in range(4):
                for fo in range(FC):
                    pe_t = sch.call("tensor", "matmul", ps_ba[:, 0:128],
                                    acc[:, tt * D + fo * 128: tt * D + (fo + 1) * 128],
                                    identf[:, :], start=True, stop=True,
                                    wait=[(s_dv, acc_dv), (s_dv, uev)], inc=s_pe)
                    xsl = xs[:, fo * TCH + tt * 128: fo * TCH + (tt + 1) * 128]
                    uev = sch.call("vector", "tensor_add", xsl, ps_ba[:, 0:128],
                                   xsl, wait=[(s_pe, pe_t), (s_xs, xs_ld)], inc=s_dv)
            for c in range(FC):
                sch.dma("sync", xo[c * 128:(c + 1) * 128, cols],
                        xs[:, c * TCH:(c + 1) * TCH], s_st,
                        wait=([(s_dv, uev)] if c == 0 else []))
            st_all = sch.val(s_st)
            prev_st = st_all
            acc_tr_pe = sch.val(s_pe)
        if debug:
            sch.dma("sync", dbg_lg[:, :], lg[:], s_st, wait=[(s_dv, w_dv)])
            sch.dma("sync", dbg_m[:, :], m1c[:], s_st, wait=[(s_dv, w_dv)])
            sch.dma("sync", dbg_eq[:, :], eqb[:], s_st, wait=[(s_dv, w_dv)])
            sch.dma("sync", dbg_ww[:, :], ww[:], s_st, wait=[(s_dv, w_dv)])
            sch.dma("gpsimd", dbg_xn[:, :], xn[:, 0:TOK], s_st, wait=[(s_dv, w_dv)])
        st_all = sch.val(s_st)
        sch.barrier_wait("sync", s_st, st_all)
    return E.finish()


def host_l0_ffn(rw, gup, gub, dwn, dwb):
    rw = np.ascontiguousarray(np.asarray(rw, np.float32))
    gup = np.asarray(gup, np.float32)
    gup_r = np.ascontiguousarray(
        np.concatenate([gup[:, :, 0::2], gup[:, :, 1::2]], axis=2))
    gub = np.asarray(gub, np.float32)
    gub_r = np.concatenate([gub[:, 0::2], gub[:, 1::2]], axis=1).reshape(1, -1)
    dwn = np.ascontiguousarray(np.asarray(dwn, np.float32))
    dwb = np.asarray(dwb, np.float32).reshape(1, -1)
    onest = np.ones((1, TCH), BF)
    return {
        "rw_in": rw, "gup_in": gup_r, "gub_in": np.ascontiguousarray(gub_r),
        "dwn_in": dwn, "dwb_in": np.ascontiguousarray(dwb),
        "onesb_in": _ones_row(BF), "onesf_in": _ones_row(np.float32),
        "onescb_in": np.ascontiguousarray(_ones_row(BF).T),
        "onest_in": onest, "identf_in": np.eye(128, dtype=np.float32),
        "epsc_in": np.full((1, 1), EPS, np.float32),
    }


def build_l2_ffn(mega=None, io=None):
    E = BlockEnv(mega, "l2f_")
    nc = E.nc
    if mega is None:
        xi = nc.dram_tensor("xi", [D, TOK], F32, kind="ExternalInput").ap()
        xh = nc.dram_tensor("xh", [D, 1], F32, kind="ExternalInput").ap()
        xo = nc.dram_tensor("xo", [D, TOK], F32, kind="ExternalOutput").ap()
    else:
        xi = io["xi"]; xh = io.get("xh"); xo = io["xo"]
    wrT = E.dram_in("wrT", [D, D], F32)
    wkT = E.dram_in("wkT", [D, D], F32)
    wvT = E.dram_in("wvT", [D, D], F32)
    mur = E.dram_in("mur", [128, FC], F32)
    muk = E.dram_in("muk", [128, FC], F32)
    epsc_in = E.dram_in("epsc_in", [1, 1], F32)
    onescb_in = E.dram_in("onescb_in", [128, 1], BF16)
    onesf_in = E.dram_in("onesf_in", [1, 128], F32)

    TP = TOK + 1
    if True:
        _e = E.ctx.enter_context
        xs = E.sb("xs", [128, FC * 512], F32)
        xn = E.sb("xn", [128, FC * TP], BF16)
        xm = E.sb("xm", [128, FC * TOK], BF16)
        rr = E.sb("rr", [128, FC * TOK], BF16)
        ka = E.sb("ka", [128, FC * TOK], BF16)
        kr = E.sb("kr", [128, 2 * 512], BF16)
        utmp = E.sb("utmp", [128, 512], F32)
        wr = E.sb("wr", [128, FC * D], BF16)
        wk = E.sb("wk", [128, FC * D], BF16)
        wv = wr
        muc = E.sb("muc", [128, 2 * FC], F32)
        stat = E.sb("stat", [1, 512], F32)
        stat2 = E.sb("stat2", [1, 512], F32)
        stat3 = E.sb("stat3", [1, 512], F32)
        xsq = E.sb("xsq", [128, 2 * 512], BF16)
        epsc = E.sb("epsc", [1, 1], F32)
        onescb = E.sb("onescb", [128, 1], BF16)
        ones1f = E.sb("ones1f", [1, 128], F32)
        ps_a = E.ps()
        ps_b = E.ps()
        ps_c = E.ps()
        ps_d = E.ps()
        s_ld = E.sem("s_ld")
        s_lg = E.sem("s_lg")
        s_pe = E.sem("s_pe")
        s_ev = E.sem("s_ev")
        s_dv = E.sem("s_dv")
        s_xs = E.sem("s_xs")
        s_st = E.sem("s_st")
        sch = E.sch
        E.start()

        sch.dma("sync", muc[:, 0:FC], mur[:, :], s_ld)
        sch.dma("sync", muc[:, FC:], muk[:, :], s_ld)
        sch.dma("sync", epsc[:], epsc_in[:, :], s_ld)
        sch.dma("sync", onescb[:], onescb_in[:, :], s_ld)
        sch.dma("sync", ones1f[:], onesf_in[:, :], s_ld)
        for c in range(FC):
            if io is not None and "xh_load" in io:
                io["xh_load"](sch, c, xn[:, c * TP:c * TP + 1], s_lg)
            else:
                sch.dma("gpsimd", xn[:, c * TP:c * TP + 1],
                        xh[c * 128:(c + 1) * 128, :], s_lg)
            sch.dma("gpsimd", xn[:, c * TP + 1:(c + 1) * TP],
                    xi[c * 128:(c + 1) * 128, :], s_lg)
        for (wt, wsrc) in ((wr, wrT), (wk, wkT)):
            for c in range(FC):
                sch.dma("gpsimd", wt[:, c * D:(c + 1) * D],
                        wsrc[c * 128:(c + 1) * 128, :], s_lg)
        all_ld = sch.val(s_ld)
        all_lg = sch.val(s_lg)

        env = {"xn": xn, "onescb": onescb, "ones1f": ones1f, "epsc": epsc,
               "stat": stat, "stat2": stat2, "stat3": stat3, "xsq": xsq,
               "ps_sum": ps_a, "ps_sq": ps_b, "ps_ba": ps_c, "ps_bb": ps_d,
               "s_pe": s_pe, "s_ev": s_ev, "s_dv": s_dv,
               "xn_ready": [(s_ld, all_ld), (s_lg, all_lg)],
               "chunk_fn": lambda c: xn[:, c * TP + 1:(c + 1) * TP]}
        ln_dv = emit_ln_fm(sch, env, TOK)
        def hcol(c):
            return xn[:, c * TP:c * TP + 1]
        for c in range(FC):
            sch.call("tensor", "matmul", ps_a[0:1, 0:1], onescb[:, 0:1], hcol(c),
                     start=(c == 0), stop=(c == FC - 1),
                     wait=([(s_dv, ln_dv)] if c == 0 else []))
            sch.call("tensor", "matmul", ps_b[0:1, 0:1], hcol(c), hcol(c),
                     start=(c == 0), stop=(c == FC - 1),
                     inc=(s_pe if c == FC - 1 else None))
        hpe = sch.val(s_pe)
        hr = stat3[0:1, 0:4]
        d_ = sch.call("vector", "tensor_scalar_mul", hr[0:1, 0:1], ps_a[0:1, 0:1],
                      1.0 / D, wait=[(s_pe, hpe)], inc=s_dv)
        d_ = sch.call("vector", "tensor_scalar_mul", hr[0:1, 1:2], ps_b[0:1, 0:1],
                      1.0 / D, wait=[(s_dv, d_)], inc=s_dv)
        d_ = sch.call("vector", "tensor_mul", hr[0:1, 2:3], hr[0:1, 0:1],
                      hr[0:1, 0:1], wait=[(s_dv, d_)], inc=s_dv)
        d_ = sch.call("vector", "tensor_sub", hr[0:1, 1:2], hr[0:1, 1:2],
                      hr[0:1, 2:3], wait=[(s_dv, d_)], inc=s_dv)
        e_ = sch.call("scalar", "activation", hr[0:1, 2:3], hr[0:1, 1:2], AF.Sqrt,
                      bias=epsc[0:1, 0:1], wait=[(s_dv, d_)], inc=s_ev)
        d_ = sch.call("vector", "reciprocal", hr[0:1, 1:2], hr[0:1, 2:3],
                      wait=[(s_ev, e_)], inc=s_dv)
        d_ = sch.call("vector", "scalar_tensor_tensor", hr[0:1, 0:1], hr[0:1, 0:1],
                      -1.0, hr[0:1, 1:2], ALU.mult, ALU.mult,
                      wait=[(s_dv, d_)], inc=s_dv)
        sch.call("tensor", "matmul", ps_a[:, 0:1], ones1f[0:1, :], hr[0:1, 1:2],
                 start=True, stop=True, wait=[(s_dv, d_)])
        hbpe = sch.call("tensor", "matmul", ps_b[:, 0:1], ones1f[0:1, :],
                        hr[0:1, 0:1], start=True, stop=True, inc=s_pe)
        hdv = None
        for c in range(FC):
            d_ = sch.call("vector", "tensor_mul", hcol(c), hcol(c), ps_a[:, 0:1],
                          wait=([(s_pe, hbpe)] if c == 0 else [(s_dv, hdv)]),
                          inc=s_dv)
            hdv = sch.call("vector", "tensor_add", hcol(c), hcol(c), ps_b[:, 0:1],
                           wait=[(s_dv, d_)], inc=s_dv)
        ln_all_dv = hdv

        def mix2(mcol_off, wait_pe):
            dvs = None
            for c in range(FC):
                xcur = xn[:, c * TP + 1:(c + 1) * TP]
                xprv = xn[:, c * TP:(c + 1) * TP - 1]
                mcol = muc[:, mcol_off + c:mcol_off + c + 1]
                o = xm[:, c * TOK:(c + 1) * TOK]
                w = [(s_dv, ln_all_dv)] if c == 0 else []
                if wait_pe is not None and c == 0:
                    w.append((s_pe, wait_pe))
                d1 = sch.call("vector", "tensor_sub", o, xcur, xprv, wait=w,
                              inc=s_dv)
                d2 = sch.call("vector", "tensor_scalar", o, o, mcol, None,
                              ALU.mult, wait=[(s_dv, d1)], inc=s_dv)
                dvs = sch.call("vector", "tensor_add", o, o, xprv,
                               wait=[(s_dv, d2)], inc=s_dv)
            return dvs

        pp = [ps_a, ps_b]
        pi = 0
        prev_ev = [(s_ev, None), (s_ev, None)]

        def proj(dst_buf, wsb, act_fn, mix_dv):
            nonlocal pi
            evs = None
            for tb in range(TOK // 512):
                cols = slice(tb * 512, (tb + 1) * 512)
                for fo in range(FC):
                    bank = pp[pi % 2]
                    slot = pi % 2
                    pi += 1
                    w = [(s_dv, mix_dv), prev_ev[slot]]
                    for c in range(FC):
                        sch.call("tensor", "matmul", bank[:, 0:512],
                                 wsb[:, c * D + fo * 128: c * D + fo * 128 + 128],
                                 xm[:, c * TOK:(c + 1) * TOK][:, cols],
                                 start=(c == 0), stop=(c == FC - 1),
                                 wait=(w if c == 0 else []),
                                 inc=(s_pe if c == FC - 1 else None))
                    pe = sch.val(s_pe)
                    evs = sch.call("scalar", "activation",
                                   dst_buf[:, fo * TOK:(fo + 1) * TOK][:, cols],
                                   bank[:, 0:512], act_fn, wait=[(s_pe, pe)],
                                   inc=s_ev)
                    prev_ev[slot] = (s_ev, evs)
            return evs

        mdv = mix2(0, None)
        proj(rr, wr, AF.Sigmoid, mdv)
        r_pe = sch.val(s_pe)
        for c in range(FC):
            sch.dma("gpsimd", wv[:, c * D:(c + 1) * D],
                    wvT[c * 128:(c + 1) * 128, :], s_lg,
                    wait=([(s_pe, r_pe)] if c == 0 else []))
        wv_ld = sch.val(s_lg)
        sch.barrier_wait("tensor", s_lg, wv_ld)
        mdv = mix2(FC, r_pe)
        evs = None
        for tb in range(TOK // 512):
            cols = slice(tb * 512, (tb + 1) * 512)
            for fo in range(FC):
                bank = pp[pi % 2]
                slot = pi % 2
                pi += 1
                w = [(s_dv, mdv), prev_ev[slot]]
                for c in range(FC):
                    sch.call("tensor", "matmul", bank[:, 0:512],
                             wk[:, c * D + fo * 128: c * D + fo * 128 + 128],
                             xm[:, c * TOK:(c + 1) * TOK][:, cols],
                             start=(c == 0), stop=(c == FC - 1),
                             wait=(w if c == 0 else []),
                             inc=(s_pe if c == FC - 1 else None))
                pe = sch.val(s_pe)
                kslot = (tb * FC + fo) % 2
                ev1 = sch.call("scalar", "activation", kr[:, kslot * 512:(kslot + 1) * 512],
                               bank[:, 0:512], AF.Relu, wait=[(s_pe, pe)], inc=s_ev)
                prev_ev[slot] = (s_ev, ev1)
                evs = sch.call("vector", "tensor_mul",
                               ka[:, fo * TOK:(fo + 1) * TOK][:, cols],
                               kr[:, kslot * 512:(kslot + 1) * 512],
                               kr[:, kslot * 512:(kslot + 1) * 512],
                               wait=[(s_ev, ev1)], inc=s_dv)
        ka_dv = sch.val(s_dv)
        prev_st = None
        for tb in range(TOK // 512):
            cols = slice(tb * 512, (tb + 1) * 512)
            for c in range(FC):
                sch.dma("sync", xs[:, c * 512:(c + 1) * 512],
                        xi[c * 128:(c + 1) * 128, cols], s_xs,
                        wait=([(s_st, prev_st)] if c == 0 else []))
            xs_ld = sch.val(s_xs)
            for fo in range(FC):
                bank = pp[pi % 2]
                slot = pi % 2
                pi += 1
                w = [(s_dv, ka_dv), prev_ev[slot]]
                for c in range(FC):
                    sch.call("tensor", "matmul", bank[:, 0:512],
                             wv[:, c * D + fo * 128: c * D + fo * 128 + 128],
                             ka[:, c * TOK:(c + 1) * TOK][:, cols],
                             start=(c == 0), stop=(c == FC - 1),
                             wait=(w if c == 0 else []),
                             inc=(s_pe if c == FC - 1 else None))
                pe = sch.val(s_pe)
                xcur = xs[:, fo * 512:(fo + 1) * 512]
                d1 = sch.call("vector", "tensor_mul", utmp[:, 0:512],
                              bank[:, 0:512],
                              rr[:, fo * TOK:(fo + 1) * TOK][:, cols],
                              wait=[(s_pe, pe)] + ([(s_xs, xs_ld)] if fo == 0 else []),
                              inc=s_dv)
                prev_ev[slot] = (s_dv, d1)
                d2 = sch.call("vector", "tensor_add", xcur, utmp[:, 0:512], xcur,
                              wait=[(s_dv, d1)], inc=s_dv)
                sch.dma("sync", xo[fo * 128:(fo + 1) * 128, cols], xcur,
                        s_st, wait=[(s_dv, d2)])
            prev_st = sch.val(s_st)
        st_all = sch.val(s_st)
        sch.barrier_wait("sync", s_st, st_all)

    return E.finish()


def host_l2_ffn(W_r, W_k, W_v, mu_r, mu_k):
    return {
        "wrT": np.ascontiguousarray(np.asarray(W_r, np.float32).T),
        "wkT": np.ascontiguousarray(np.asarray(W_k, np.float32).T),
        "wvT": np.ascontiguousarray(np.asarray(W_v, np.float32).T),
        "mur": chunk_cols(mu_r), "muk": chunk_cols(mu_k),
        "epsc_in": np.full((1, 1), EPS, np.float32),
        "onescb_in": np.ones((128, 1), BF),
        "onesf_in": _ones_row(np.float32),
    }


def build_l3_attn(stop_after=None, mega=None, io=None):
    E = BlockEnv(mega, "l3a_")
    nc = E.nc
    HS = 64
    if mega is None:
        xf = nc.dram_tensor("xf", [D, S], F32, kind="ExternalInput").ap()
        xo_in = nc.dram_tensor("xo_in", [D, TOK], F32, kind="ExternalInput").ap()
        xo = nc.dram_tensor("xo", [D, TOK], F32, kind="ExternalOutput").ap()
    else:
        xf = io.get("xf"); xo_in = io["xo_in"]; xo = io["xo"]
    wqT = E.dram_in("wqT", [D, HS], F32)
    wkT = E.dram_in("wkT", [D, HS], F32)
    wvT = E.dram_in("wvT", [D, HS], F32)
    bqc = E.dram_in("bqc", [HS, 1], F32)
    bkc = E.dram_in("bkc", [HS, 1], F32)
    bvr = E.dram_in("bvr", [1, HS], BF16)
    pwT = E.dram_in("pwT", [D, D], F32)
    pbc = E.dram_in("pbc", [128, FC], F32)
    kvis = E.dram_in("kvis", [128, NKT], F32)
    cms01 = E.dram_in("cms01", [4, 128, 512], BF16)
    sel_in = E.dram_in("sel_in", [128, HS], BF16)
    onesb_in = E.dram_in("onesb_in", [1, 128], BF16)
    onesf_in = E.dram_in("onesf_in", [1, 128], F32)
    onescb_in = E.dram_in("onescb_in", [128, 1], BF16)
    epsc_in = E.dram_in("epsc_in", [1, 1], F32)

    NQ3 = 4
    if True:
        _e = E.ctx.enter_context
        xn = E.sb("xn", [128, FC * S], BF16)
        qq = E.sb("qq", [HS, TOK], BF16)
        kk = E.sb("kk", [HS, S], BF16)
        vv = E.sb("vv", [128, NKT * HS], BF16)
        wq3 = E.sb("wq3", [128, FC * 3 * HS], BF16)
        pw = E.sb("pw", [128, FC * D], BF16)
        weff = E.sb("weff", [HS, D], BF16)
        sel = E.sb("sel", [128, HS], BF16)
        wei = E.sb("wei", [128, 4 * 512], BF16)
        outh = E.sb("outh", [HS, 512], BF16)
        bq3 = E.sb("bq3", [HS, 2], F32)
        bvr_s = E.sb("bvr_s", [1, HS], BF16)
        pbc_s = E.sb("pbc_s", [128, FC], F32)
        kvi = E.sb("kvi", [128, NKT], F32)
        cm1 = E.sb("cm1", [128, 4 * 512], BF16)
        onesb = E.sb("onesb", [1, 128], BF16)
        ones1f = E.sb("ones1f", [1, 128], F32)
        onescb = E.sb("onescb", [128, 1], BF16)
        epsc = E.sb("epsc", [1, 1], F32)
        stat = E.sb("stat", [1, 512], F32)
        stat2 = E.sb("stat2", [1, 512], F32)
        stat3 = E.sb("stat3", [1, 512], F32)
        xsq = E.sb("xsq", [128, 2 * 512], BF16)
        xs = E.sb("xs", [128, FC * 512], F32)
        utm = E.sb("utm", [128, 512], F32)
        ps_a = E.ps()
        ps_b = E.ps()
        ps_c = E.ps()
        ps_d = E.ps()
        ps_o = E.ps()
        s_ld = E.sem("s_ld")
        s_lg = E.sem("s_lg")
        s_pe = E.sem("s_pe")
        s_ev = E.sem("s_ev")
        s_dv = E.sem("s_dv")
        s_xs = E.sem("s_xs")
        s_st = E.sem("s_st")
        sch = E.sch
        E.start()

        for (dst, src) in ((onesb, onesb_in), (ones1f, onesf_in), (onescb, onescb_in),
                           (epsc, epsc_in), (sel, sel_in), (bvr_s, bvr),
                           (pbc_s, pbc), (kvi, kvis)):
            sch.dma("sync", dst[:], src[:, :], s_ld)
        for v_ in range(4):
            sch.dma("sync", cm1[:, v_ * 512:(v_ + 1) * 512], cms01[v_, :, :], s_ld)
        sch.dma("gpsimd", bq3[:, 0:1], bqc[:, :], s_lg)
        sch.dma("gpsimd", bq3[:, 1:2], bkc[:, :], s_lg)
        for c in range(FC):
            sch.dma("gpsimd", wq3[:, c * 3 * HS:c * 3 * HS + HS],
                    wqT[c * 128:(c + 1) * 128, :], s_lg)
            sch.dma("gpsimd", wq3[:, c * 3 * HS + HS:c * 3 * HS + 2 * HS],
                    wkT[c * 128:(c + 1) * 128, :], s_lg)
            sch.dma("gpsimd", wq3[:, c * 3 * HS + 2 * HS:c * 3 * HS + 3 * HS],
                    wvT[c * 128:(c + 1) * 128, :], s_lg)
            sch.dma("gpsimd", pw[:, c * D:(c + 1) * D],
                    pwT[c * 128:(c + 1) * 128, :], s_lg)
        for c in range(FC):
            if io is not None and "xf_load" in io:
                io["xf_load"](sch, c, xn[:, c * S:(c + 1) * S], s_lg)
            else:
                sch.dma("gpsimd", xn[:, c * S:(c + 1) * S],
                        xf[c * 128:(c + 1) * 128, :], s_lg)
        all_ld = sch.val(s_ld)
        all_lg = sch.val(s_lg)

        env = {"xn": xn, "onescb": onescb, "ones1f": ones1f, "epsc": epsc,
               "stat": stat, "stat2": stat2, "stat3": stat3, "xsq": xsq,
               "ps_sum": ps_a, "ps_sq": ps_b, "ps_ba": ps_c, "ps_bb": ps_d,
               "s_pe": s_pe, "s_ev": s_ev, "s_dv": s_dv,
               "xn_ready": [(s_ld, all_ld), (s_lg, all_lg)]}
        ln_dv = emit_ln_fm(sch, env, S)

        def xnc(c, cols):
            return xn[:, c * S:(c + 1) * S][:, cols]

        def _finish():
            d = sch.call("vector", "tensor_copy", xs[:, 0:512], xn[:, 0:512],
                         wait=[(s_dv, sch.val(s_dv))], inc=s_dv)
            for c in range(FC):
                sch.dma("sync", xo[c * 128:(c + 1) * 128, 0:512], xs[:, 0:512],
                        s_st, wait=[(s_dv, d)])
            sch.barrier_wait("sync", s_st, sch.val(s_st))
            with nc.Block() as block:
                sch.run(block)
            return nc

        if stop_after == "ln":
            return _finish()

        weff_ev0 = None
        for half in range(2):
            for c in range(FC):
                w0 = []
                if half == 0 and c == 0:
                    w0 = [(s_ld, all_ld), (s_lg, all_lg), (s_dv, ln_dv)]
                if half == 1 and c == 0:
                    w0 = [(s_ev, weff_ev0)]
                sch.call("tensor", "matmul", ps_o[0:HS, 0:512], sel[:],
                         pw[:, c * D + half * 512: c * D + (half + 1) * 512],
                         start=(c == 0), stop=(c == FC - 1),
                         wait=w0, inc=(s_pe if c == FC - 1 else None))
            pe = sch.val(s_pe)
            weff_ev0 = sch.call("scalar", "activation",
                                weff[:, half * 512:(half + 1) * 512],
                                ps_o[0:HS, 0:512], AF.Copy, wait=[(s_pe, pe)],
                                inc=s_ev)
        weff_ev = sch.val(s_ev)
        if stop_after == "weff":
            return _finish()

        pp = [ps_a, ps_b]
        pi = 0
        prev = [(s_ev, None), (s_ev, None)]
        for tb in range(S // 512):
            cols = slice(tb * 512, (tb + 1) * 512)
            for (which, woff) in (("k", HS), ("q", 0)):
                if which == "q" and tb < 4:
                    continue
                bank = pp[pi % 2]
                slot = pi % 2
                pi += 1
                w = [(s_dv, ln_dv), prev[slot]]
                for c in range(FC):
                    sch.call("tensor", "matmul", bank[0:HS, 0:512],
                             wq3[:, c * 3 * HS + woff:c * 3 * HS + woff + HS],
                             xnc(c, cols), start=(c == 0), stop=(c == FC - 1),
                             wait=(w if c == 0 else []),
                             inc=(s_pe if c == FC - 1 else None))
                pe = sch.val(s_pe)
                if which == "q":
                    dst = qq[:, (tb - 4) * 512:(tb - 3) * 512]
                    bcol = bq3[:, 0:1]
                else:
                    dst = kk[:, cols]
                    bcol = bq3[:, 1:2]
                dv = sch.call("vector", "tensor_scalar", dst, bank[0:HS, 0:512],
                              bcol, None, ALU.add, wait=[(s_pe, pe)], inc=s_dv)
                prev[slot] = (s_dv, dv)
        for kt in range(NKT):
            tcol = slice(kt * 128, (kt + 1) * 128)
            bank = pp[pi % 2]
            slot = pi % 2
            pi += 1
            w = [(s_dv, ln_dv), prev[slot]]
            for c in range(FC):
                sch.call("tensor", "matmul", bank[:, 0:HS],
                         xnc(c, tcol), wq3[:, c * 3 * HS + 2 * HS:c * 3 * HS + 3 * HS],
                         start=(c == 0), stop=False, wait=(w if c == 0 else []))
            sch.call("tensor", "matmul", bank[:, 0:HS], onesb[0:1, :],
                     bvr_s[0:1, :], start=False, stop=True, inc=s_pe)
            pe = sch.val(s_pe)
            dv = sch.call("vector", "tensor_scalar", vv[:, kt * HS:(kt + 1) * HS],
                          bank[:, 0:HS], kvi[:, kt:kt + 1], None, ALU.mult,
                          wait=[(s_pe, pe)], inc=s_dv)
            prev[slot] = (s_dv, dv)
        proj_dv = sch.val(s_dv)
        if stop_after == "proj":
            return _finish()

        sprev = [(s_dv, proj_dv), (s_dv, proj_dv), (s_dv, proj_dv), (s_dv, proj_dv)]
        prev_st = None
        ep_d1 = None
        for qc in range(NQ3):
            qcols = slice(qc * 512, (qc + 1) * 512)
            n_kt = 16 + 4 * (qc + 1)
            for c in range(FC):
                sch.dma("sync", xs[:, c * 512:(c + 1) * 512],
                        xo_in[c * 128:(c + 1) * 128, qcols], s_xs,
                        wait=([(s_st, prev_st)] if c == 0 else []))
            xs_ld = sch.val(s_xs)
            outh_dv = None
            for ki in range(n_kt):
                kt = ki
                wslot = ki % 4
                wt = wei[:, wslot * 512:(wslot + 1) * 512]
                w = [sprev[wslot], (s_dv, proj_dv)]
                pe_sc = sch.call("tensor", "matmul", ps_c[:, 0:512],
                                 kk[:, kt * 128:(kt + 1) * 128], qq[:, qcols],
                                 start=True, stop=True, wait=w, inc=s_pe)
                ev = sch.call("scalar", "activation", wt, ps_c[:, 0:512], AF.Copy,
                              wait=[(s_pe, pe_sc)], inc=s_ev)
                lvar = kt - 16 - 4 * qc
                dep = (s_ev, ev)
                if 0 <= lvar < 4:
                    dvm = sch.call("vector", "tensor_mul", wt, wt,
                                   cm1[:, lvar * 512:(lvar + 1) * 512],
                                   wait=[(s_ev, ev)], inc=s_dv)
                    dep = (s_dv, dvm)
                pe_av = sch.call("tensor", "matmul", ps_o[0:HS, 0:512],
                                 vv[:, kt * HS:(kt + 1) * HS], wt,
                                 start=(ki == 0), stop=(ki == n_kt - 1),
                                 wait=[dep], inc=s_pe)
                sprev[wslot] = (s_pe, pe_av)
            ev_oh = sch.call("scalar", "activation", outh[:], ps_o[0:HS, 0:512],
                             AF.Copy, wait=[(s_pe, sch.val(s_pe)), (s_ev, weff_ev)],
                             inc=s_ev)
            for fo in range(FC):
                pe = sch.call("tensor", "matmul", ps_d[:, 0:512],
                              weff[:, fo * 128:(fo + 1) * 128], outh[:],
                              start=True, stop=True,
                              wait=[(s_ev, ev_oh), (s_dv, ep_d1)], inc=s_pe)
                d1 = sch.call("vector", "tensor_scalar", utm[:, 0:512],
                              ps_d[:, 0:512], pbc_s[:, fo:fo + 1], None, ALU.add,
                              wait=[(s_pe, pe)] + ([(s_xs, xs_ld)] if fo == 0 else []),
                              inc=s_dv)
                xsl = xs[:, fo * 512:(fo + 1) * 512]
                ep_d1 = d1
                d2 = sch.call("vector", "tensor_add", xsl, utm[:, 0:512], xsl,
                              wait=[(s_dv, d1)], inc=s_dv)
                sch.dma("sync", xo[fo * 128:(fo + 1) * 128, qcols], xsl, s_st,
                        wait=[(s_dv, d2)])
            prev_st = sch.val(s_st)
        sch.barrier_wait("sync", s_st, sch.val(s_st))
    return E.finish()


def host_l3_attn(qkv_w, qkv_b, proj_w, proj_b):
    HS = 64
    qkv_w = np.asarray(qkv_w, np.float32)
    qkv_b = np.asarray(qkv_b, np.float32)
    sel = np.tile(np.eye(HS, dtype=BF), (2, 1))
    j = np.arange(128)[:, None]
    i = np.arange(512)[None, :]
    cms01 = np.stack([(j + v * 128 <= i).astype(BF) for v in range(4)])
    shared = {
        "wqT": np.ascontiguousarray(qkv_w[0:HS].T),
        "wkT": np.ascontiguousarray(qkv_w[HS:2 * HS].T),
        "wvT": np.ascontiguousarray(qkv_w[2 * HS:3 * HS].T),
        "bqc": qkv_b[0:HS].reshape(HS, 1).copy(),
        "bkc": qkv_b[HS:2 * HS].reshape(HS, 1).copy(),
        "bvr": qkv_b[2 * HS:3 * HS].reshape(1, HS).astype(BF),
        "pwT": np.ascontiguousarray(np.asarray(proj_w, np.float32).T),
        "pbc": chunk_cols(proj_b),
        "cms01": cms01, "sel_in": np.ascontiguousarray(sel),
        "onesb_in": _ones_row(BF), "onesf_in": _ones_row(np.float32),
        "onescb_in": np.ones((128, 1), BF),
        "epsc_in": np.full((1, 1), EPS, np.float32),
    }
    return shared


def host_l3_attn_core(x, core):
    b, h = core // 2, core % 2
    own = x[b, h * TOK:(h + 1) * TOK]
    other = x[b, (1 - h) * TOK:(2 - h) * TOK]
    xcat = np.concatenate([other, own], axis=0)
    kv = np.ones((128, NKT), np.float32)
    if h == 0:
        kv[:, 0:16] = 0.0
    return {"xf": fmajor(xcat), "xo_in": fmajor(own), "kvis": kv}


def build_l2_attn(mega=None, io=None):
    E = BlockEnv(mega, "l2a_")
    nc = E.nc
    NHP = 8
    if mega is None:
        xf = nc.dram_tensor("xf", [D, S], F32, kind="ExternalInput").ap()
        xo_in = nc.dram_tensor("xo_in", [D, TOK], F32, kind="ExternalInput").ap()
        xo = nc.dram_tensor("xo", [D, TOK], F32, kind="ExternalOutput").ap()
    else:
        xf = io.get("xf"); xo_in = io["xo_in"]; xo = io["xo"]
    wqT = E.dram_in("wqT", [D, D], F32)
    wkT = E.dram_in("wkT", [D, D], F32)
    wvT = E.dram_in("wvT", [D, D], F32)
    wpT = E.dram_in("wpT", [D, D], F32)
    pbc = E.dram_in("pbc", [128, FC], F32)
    onesb_in = E.dram_in("onesb_in", [1, 128], BF16)
    onesf_in = E.dram_in("onesf_in", [1, 128], F32)
    onescb_in = E.dram_in("onescb_in", [128, 1], BF16)
    onescf_in = E.dram_in("onescf_in", [128, 1], F32)
    epsc_in = E.dram_in("epsc_in", [1, 1], F32)
    ksc = E.dram_int("ksc", [NHP, 128, S], BF16)
    vsc = E.dram_int("vsc", [NHP, NKT, 128, 128], BF16)

    if True:
        _e = E.ctx.enter_context
        qq = E.sb("qq", [128, NHP * TOK], BF16)
        cxs = E.sb("cxs", [128, FC * TOK], BF16)
        wp = E.sb("wp", [128, FC * D], BF16)
        pbs = E.sb("pbs", [128, FC], F32)
        onesb = E.sb("onesb", [1, 128], BF16)
        ones1f = E.sb("ones1f", [1, 128], F32)
        onescb = E.sb("onescb", [128, 1], BF16)
        onescf = E.sb("onescf", [128, 1], F32)
        epsc = E.sb("epsc", [1, 1], F32)
        stat = E.sb("stat", [1, 512], F32)
        stat2 = E.sb("stat2", [1, 512], F32)
        stat3 = E.sb("stat3", [1, 512], F32)
        xsq = E.sb("xsq", [128, 2 * 512], BF16)
        ps_a = E.ps()
        ps_b = E.ps()
        ps_c = E.ps()
        ps_d = E.ps()
        ps_e = E.ps()
        ps_f = E.ps()
        ps_z = E.ps()
        ps_g = E.ps()
        s_ld = E.sem("s_ld")
        s_lg = E.sem("s_lg")
        s_pe = E.sem("s_pe")
        s_ev = E.sem("s_ev")
        s_dv = E.sem("s_dv")
        s_sc = E.sem_list("s_sc", 4)
        s_kv = E.sem_list("s_kv", 2)
        s_xs = E.sem("s_xs")
        s_st = E.sem("s_st")
        sch = E.sch
        E.start()

        for (dst, src) in ((onesb, onesb_in), (ones1f, onesf_in), (onescb, onescb_in),
                           (onescf, onescf_in), (epsc, epsc_in), (pbs, pbc)):
            sch.dma("sync", dst[:], src[:, :], s_ld)
        for c in range(FC):
            sch.dma("gpsimd", wp[:, c * D:(c + 1) * D],
                    wpT[c * 128:(c + 1) * 128, :], s_lg)

        with ExitStack() as ctx2:
            _e2 = ctx2.enter_context
            xn = _e2(nc.sbuf_tensor(E.nm("xn"), [128, FC * S], BF16))
            wq = _e2(nc.sbuf_tensor(E.nm("wq"), [128, FC * D], BF16))
            wk = _e2(nc.sbuf_tensor(E.nm("wk"), [128, FC * D], BF16))
            wv = _e2(nc.sbuf_tensor(E.nm("wv"), [128, FC * D], BF16))
            kevac = _e2(nc.sbuf_tensor(E.nm("kevac"), [128, 2 * 512], BF16))
            vevac = _e2(nc.sbuf_tensor(E.nm("vevac"), [128, 2 * 128], BF16))
            for (wt, wsrc) in ((wq, wqT), (wk, wkT), (wv, wvT)):
                for c in range(FC):
                    sch.dma("gpsimd", wt[:, c * D:(c + 1) * D],
                            wsrc[c * 128:(c + 1) * 128, :], s_lg)
            for c in range(FC):
                if io is not None and "xf_load" in io:
                    io["xf_load"](sch, c, xn[:, c * S:(c + 1) * S], s_lg)
                else:
                    sch.dma("gpsimd", xn[:, c * S:(c + 1) * S],
                            xf[c * 128:(c + 1) * 128, :], s_lg)
            all_ld = sch.val(s_ld)
            all_lg = sch.val(s_lg)

            env = {"xn": xn, "onescb": onescb, "ones1f": ones1f, "epsc": epsc,
                   "stat": stat, "stat2": stat2, "stat3": stat3, "xsq": xsq,
                   "ps_sum": ps_a, "ps_sq": ps_b, "ps_ba": ps_c, "ps_bb": ps_d,
                   "s_pe": s_pe, "s_ev": s_ev, "s_dv": s_dv,
                   "xn_ready": [(s_ld, all_ld), (s_lg, all_lg)]}
            ln_dv = emit_ln_fm(sch, env, S)

            def xnc(c, cols):
                return xn[:, c * S:(c + 1) * S][:, cols]

            pp = [ps_a, ps_b]
            pi = 0
            prev = [(s_ev, None), (s_ev, None)]
            kev_dma = [None, None]
            vev_dma = [None, None]
            for tb in range(S // 512):
                cols = slice(tb * 512, (tb + 1) * 512)
                for fo in range(FC):
                    bank = pp[pi % 2]
                    slot = pi % 2
                    pi += 1
                    w = [(s_dv, ln_dv), prev[slot]]
                    for c in range(FC):
                        sch.call("tensor", "matmul", bank[:, 0:512],
                                 wk[:, c * D + fo * 128: c * D + fo * 128 + 128],
                                 xnc(c, cols), start=(c == 0), stop=(c == FC - 1),
                                 wait=(w if c == 0 else []),
                                 inc=(s_pe if c == FC - 1 else None))
                    pe = sch.val(s_pe)
                    kslot = (tb * FC + fo) % 2
                    ev = sch.call("scalar", "activation",
                                  kevac[:, kslot * 512:(kslot + 1) * 512],
                                  bank[:, 0:512], AF.Copy,
                                  wait=[(s_pe, pe), (s_sc[kslot], kev_dma[kslot])],
                                  inc=s_ev)
                    prev[slot] = (s_ev, ev)
                    kev_dma[kslot] = sch.dma("gpsimd", ksc[fo, :, cols],
                                             kevac[:, kslot * 512:(kslot + 1) * 512],
                                             s_sc[kslot], wait=[(s_ev, ev)])
                if tb >= 4:
                    for fo in range(FC):
                        bank = pp[pi % 2]
                        slot = pi % 2
                        pi += 1
                        w = [(s_dv, ln_dv), prev[slot]]
                        for c in range(FC):
                            sch.call("tensor", "matmul", bank[:, 0:512],
                                     wq[:, c * D + fo * 128: c * D + fo * 128 + 128],
                                     xnc(c, cols), start=(c == 0), stop=(c == FC - 1),
                                     wait=(w if c == 0 else []),
                                     inc=(s_pe if c == FC - 1 else None))
                        pe = sch.val(s_pe)
                        ev = sch.call("scalar", "activation",
                                      qq[:, fo * TOK + (tb - 4) * 512: fo * TOK + (tb - 3) * 512],
                                      bank[:, 0:512], AF.Copy, wait=[(s_pe, pe)],
                                      inc=s_ev)
                        prev[slot] = (s_ev, ev)
            for kt in range(NKT):
                tcol = slice(kt * 128, (kt + 1) * 128)
                for fo in range(FC):
                    bank = pp[pi % 2]
                    slot = pi % 2
                    pi += 1
                    w = [(s_dv, ln_dv), prev[slot]]
                    for c in range(FC):
                        sch.call("tensor", "matmul", bank[:, 0:128],
                                 xnc(c, tcol),
                                 wv[:, c * D + fo * 128: c * D + (fo + 1) * 128],
                                 start=(c == 0), stop=(c == FC - 1),
                                 wait=(w if c == 0 else []),
                                 inc=(s_pe if c == FC - 1 else None))
                    pe = sch.val(s_pe)
                    vslot = (kt * FC + fo) % 2
                    ev = sch.call("scalar", "activation",
                                  vevac[:, vslot * 128:(vslot + 1) * 128],
                                  bank[:, 0:128], AF.Copy,
                                  wait=[(s_pe, pe), (s_sc[2 + vslot], vev_dma[vslot])],
                                  inc=s_ev)
                    prev[slot] = (s_ev, ev)
                    vev_dma[vslot] = sch.dma("gpsimd", vsc[fo, kt, :, :],
                                             vevac[:, vslot * 128:(vslot + 1) * 128],
                                             s_sc[2 + vslot], wait=[(s_ev, ev)])
            proj_pe = sch.val(s_pe)
            proj_ev = sch.val(s_ev)
            scr_done = [(x, sch.val(x)) for x in s_sc]
        with ExitStack() as ctx3:
            _e3 = ctx3.enter_context
            kst = _e3(nc.sbuf_tensor(E.nm("kst"), [128, 2 * S], BF16))
            vst = _e3(nc.sbuf_tensor(E.nm("vst"), [128, 2 * NKT * 128], BF16))
            zacc = _e3(nc.sbuf_tensor(E.nm("zacc"), [128, 2 * 512], F32))
            rzr = _e3(nc.sbuf_tensor(E.nm("rzr"), [1, 2 * 512], F32))
            rzb = _e3(nc.sbuf_tensor(E.nm("rzb"), [128, 512], F32))
            expt = _e3(nc.sbuf_tensor(E.nm("expt"), [128, 8 * 512], BF16))
            kv_ld = [None, None]
            kv_pe = [None, None]
            sprev2 = [(s_ev, None), (s_ev, None)]
            prev_dvc = None
            prev_evz = None
            prev_d2 = None
            prev_pz2 = None
            QCW = 512
            NQC2 = TOK // QCW
            first = True
            for hp in range(NHP):
                hslot = hp % 2
                lw = [(s_pe, kv_pe[hslot])]
                if first:
                    lw += scr_done + [(s_ev, proj_ev)]
                sch.dma("gpsimd", kst[:, hslot * S:(hslot + 1) * S],
                        ksc[hp, :, :], s_kv[hslot], wait=lw)
                kv_ld[hslot] = sch.dma(
                    "gpsimd", vst[:, hslot * NKT * 128:(hslot + 1) * NKT * 128]
                    .rearrange("p (n t) -> p n t", n=NKT),
                    vsc[hp, :, :, :].rearrange("n p t -> p n t"), s_kv[hslot])
                first = False
                for qc in range(NQC2):
                    qcols = slice(qc * QCW, (qc + 1) * QCW)
                    e_dv = [None] * 8
                    for kt in range(NKT):
                        eslot = kt % 4
                        for hh in range(2):
                            sb_ = [ps_a, ps_b][hh]
                            et = expt[:, (eslot * 2 + hh) * 512:(eslot * 2 + hh + 1) * 512]
                            qkw = [sprev2[hh], (s_kv[hslot], kv_ld[hslot])]
                            pe_sc = sch.call(
                                "tensor", "matmul", sb_[:, 0:512],
                                kst[:, hslot * S + kt * 128: hslot * S + (kt + 1) * 128][hh * 64:(hh + 1) * 64, :],
                                qq[:, hp * TOK:(hp + 1) * TOK][hh * 64:(hh + 1) * 64, qcols],
                                start=True, stop=True,
                                tile_position=(hh * 64, 0),
                                wait=qkw, inc=s_pe)
                            ev = sch.call("scalar", "activation", et, sb_[:, 0:512],
                                          AF.Exp, scale=0.125,
                                          wait=[(s_pe, pe_sc),
                                                (s_dv, e_dv[eslot * 2 + hh])],
                                          inc=s_ev)
                            sprev2[hh] = (s_ev, ev)
                            zw = [(s_ev, ev)]
                            if kt == 0:
                                zw.append((s_pe, prev_pz2))
                                zdv = sch.call("vector", "tensor_copy",
                                               zacc[:, hh * 512:(hh + 1) * 512], et,
                                               wait=zw, inc=s_dv)
                            else:
                                zdv = sch.call("vector", "tensor_add",
                                               zacc[:, hh * 512:(hh + 1) * 512],
                                               zacc[:, hh * 512:(hh + 1) * 512], et,
                                               wait=zw, inc=s_dv)
                            e_dv[eslot * 2 + hh] = zdv
                            avw = [(s_ev, ev)]
                            if kt == 0:
                                avw.append((s_dv, prev_dvc))
                            ctxb = [ps_e, ps_f][hh]
                            sch.call("tensor", "matmul",
                                     ctxb[hh * 64:(hh + 1) * 64, 0:512],
                                     vst[:, hslot * NKT * 128 + kt * 128 + hh * 64:
                                         hslot * NKT * 128 + kt * 128 + (hh + 1) * 64],
                                     et, start=(kt == 0), stop=(kt == NKT - 1),
                                     tile_position=(0, hh * 64),
                                     wait=avw,
                                     inc=(s_pe if kt == NKT - 1 and hh == 1 else None))
                    av_pe = sch.val(s_pe)
                    z_dv = sch.val(s_dv)
                    pz = sch.call("tensor", "matmul", ps_z[0:1, 0:512], onescf[:, 0:1],
                                  zacc[:, 0:512], start=True, stop=True,
                                  wait=[(s_dv, z_dv), (s_ev, prev_evz)], inc=s_pe)
                    pz2 = sch.call("tensor", "matmul", ps_g[0:1, 0:512], onescf[:, 0:1],
                                   zacc[:, 512:1024], start=True, stop=True,
                                   wait=[(s_dv, prev_d2)], inc=s_pe)
                    prev_pz2 = pz2
                    d1 = sch.call("vector", "reciprocal", rzr[0:1, 0:512],
                                  ps_z[0:1, 0:512], wait=[(s_pe, pz)], inc=s_dv)
                    d2 = sch.call("vector", "reciprocal", rzr[0:1, 512:1024],
                                  ps_g[0:1, 0:512], wait=[(s_pe, pz2)], inc=s_dv)
                    prev_d2 = d2
                    bp1 = sch.call("tensor", "matmul", ps_z[0:64, 0:512],
                                   ones1f[0:1, 0:64], rzr[0:1, 0:512],
                                   start=True, stop=True,
                                   wait=[(s_dv, d1), (s_ev, prev_evz)], inc=s_pe)
                    bp2 = sch.call("tensor", "matmul", ps_z[64:128, 0:512],
                                   ones1f[0:1, 0:64], rzr[0:1, 512:1024],
                                   start=True, stop=True, wait=[(s_dv, d2)],
                                   tile_position=(0, 64), inc=s_pe)
                    evz = sch.call("scalar", "activation", rzb[:], ps_z[:, 0:512],
                                   AF.Copy, wait=[(s_pe, bp2)], inc=s_ev)
                    prev_evz = evz
                    cxsl = cxs[:, hp * TOK:(hp + 1) * TOK][:, qcols]
                    sch.call("vector", "tensor_mul", cxsl[0:64, :],
                             ps_e[0:64, 0:512], rzb[0:64, :],
                             wait=[(s_ev, evz), (s_pe, av_pe)], inc=s_dv)
                    dvc = sch.call("vector", "tensor_mul", cxsl[64:128, :],
                                   ps_f[64:128, 0:512], rzb[64:128, :], inc=s_dv)
                    prev_dvc = dvc
                kv_pe[hslot] = sch.val(s_pe)
            ctx_all_dv = sch.val(s_dv)
        with ExitStack() as ctx4:
            _e4 = ctx4.enter_context
            xs = _e4(nc.sbuf_tensor(E.nm("xs"), [128, FC * 512], F32))
            utm = _e4(nc.sbuf_tensor(E.nm("utm"), [128, 512], F32))
            pp2 = [ps_a, ps_b]
            pi2 = 0
            oprev = [(s_dv, None), (s_dv, None)]
            prev_st = None
            scope_gate = [(s_dv, ctx_all_dv), (s_pe, sch.val(s_pe)),
                          (s_kv[0], sch.val(s_kv[0])), (s_kv[1], sch.val(s_kv[1])),
                          (s_ev, sch.val(s_ev))]
            for qc in range(4):
                qcols = slice(qc * 512, (qc + 1) * 512)
                for c in range(FC):
                    sch.dma("sync", xs[:, c * 512:(c + 1) * 512],
                            xo_in[c * 128:(c + 1) * 128, qcols], s_xs,
                            wait=((scope_gate if qc == 0 else [(s_st, prev_st)]) if c == 0 else []))
                xs_ld = sch.val(s_xs)
                for fo in range(FC):
                    bank = pp2[pi2 % 2]
                    slot = pi2 % 2
                    pi2 += 1
                    w = [(s_dv, ctx_all_dv), oprev[slot]]
                    for c in range(FC):
                        sch.call("tensor", "matmul", bank[:, 0:512],
                                 wp[:, c * D + fo * 128: c * D + fo * 128 + 128],
                                 cxs[:, c * TOK:(c + 1) * TOK][:, qcols],
                                 start=(c == 0), stop=(c == FC - 1),
                                 wait=(w if c == 0 else []),
                                 inc=(s_pe if c == FC - 1 else None))
                    pe = sch.val(s_pe)
                    d1 = sch.call("vector", "tensor_scalar", utm[:, 0:512],
                                  bank[:, 0:512], pbs[:, fo:fo + 1], None, ALU.add,
                                  wait=[(s_pe, pe)] + ([(s_xs, xs_ld)] if fo == 0 else []),
                                  inc=s_dv)
                    oprev[slot] = (s_dv, d1)
                    xsl = xs[:, fo * 512:(fo + 1) * 512]
                    d2 = sch.call("vector", "tensor_add", xsl, utm[:, 0:512], xsl,
                                  wait=[(s_dv, d1)], inc=s_dv)
                    sch.dma("sync", xo[fo * 128:(fo + 1) * 128, qcols], xsl, s_st,
                            wait=[(s_dv, d2)])
                    if io is not None and "col_hook" in io and qc == 3:
                        io["col_hook"](sch, fo, xsl[:, 511:512])
                prev_st = sch.val(s_st)
            sch.barrier_wait("sync", s_st, sch.val(s_st))
    return E.finish()


def host_l2_attn(w_qkv, w_proj, b_proj):
    w_qkv = np.asarray(w_qkv, np.float32)
    return {
        "wqT": np.ascontiguousarray(w_qkv[0:D].T),
        "wkT": np.ascontiguousarray(w_qkv[D:2 * D].T),
        "wvT": np.ascontiguousarray(w_qkv[2 * D:3 * D].T),
        "wpT": np.ascontiguousarray(np.asarray(w_proj, np.float32).T),
        "pbc": chunk_cols(b_proj),
        "onesb_in": _ones_row(BF), "onesf_in": _ones_row(np.float32),
        "onescb_in": np.ones((128, 1), BF),
        "onescf_in": np.ones((128, 1), np.float32),
        "epsc_in": np.full((1, 1), EPS, np.float32),
    }


DFF = 4096
NFF = DFF // 128


def build_l3_ffn(mega=None, io=None):
    E = BlockEnv(mega, "l3f_")
    nc = E.nc
    if mega is None:
        xi = nc.dram_tensor("xi", [D, TOK], F32, kind="ExternalInput").ap()
        xo = nc.dram_tensor("xo", [D, TOK], F32, kind="ExternalOutput").ap()
    else:
        xi = io["xi"]; xo = io["xo"]
    rw1_in = E.dram_in("rw1_in", [D, DFF], F32)
    rb1_in = E.dram_in("rb1_in", [128, NFF], F32)
    rw2_in = E.dram_in("rw2_in", [DFF, NE], F32)
    rb2_in = E.dram_in("rb2_in", [1, NE], BF16)
    w1_in = E.dram_in("w1_in", [NE, D, DFF], F32)
    b1_in = E.dram_in("b1_in", [NE, 128, NFF], F32)
    w2_in = E.dram_in("w2_in", [NE, DFF, D], F32)
    b2_in = E.dram_in("b2_in", [1, NE * D], F32)
    onesb_in = E.dram_in("onesb_in", [1, 128], BF16)
    onesf_in = E.dram_in("onesf_in", [1, 128], F32)
    onescb_in = E.dram_in("onescb_in", [128, 1], BF16)
    identf_in = E.dram_in("identf_in", [128, 128], F32)
    epsc_in = E.dram_in("epsc_in", [1, 1], F32)

    if True:
        _e = E.ctx.enter_context
        xn = E.sb("xn", [128, FC * TOK], BF16)
        w1 = E.sb("w1", [128, FC * DFF], BF16)
        w2 = E.sb("w2", [128, NFF * D], BF16)
        TC3 = 256
        NTC3 = TOK // TC3
        hh = E.sb("hh", [128, NFF * TC3], BF16)
        acc = E.sb("acc", [128, 2 * D], F32)
        xs = E.sb("xs", [128, 2 * 128], F32)
        rw2 = E.sb("rw2", [128, NFF * NE], BF16)
        rb1c = E.sb("rb1c", [128, NFF], F32)
        b1c = E.sb("b1c", [128, 2 * NFF], F32)
        b2r = E.sb("b2r", [1, 2 * D], BF16)
        rb2r = E.sb("rb2r", [1, NE], BF16)
        lg = E.sb("lg", [128, 16 * NE], F32)
        ww = E.sb("ww", [128, 16 * NE], F32)
        m1c = E.sb("m1c", [128, 8], F32)
        eqb = E.sb("eqb", [128, 2 * NE], F32)
        onesb = E.sb("onesb", [1, 128], BF16)
        ones1f = E.sb("ones1f", [1, 128], F32)
        onescb = E.sb("onescb", [128, 1], BF16)
        identf = E.sb("identf", [128, 128], F32)
        epsc = E.sb("epsc", [1, 1], F32)
        stat = E.sb("stat", [1, 512], F32)
        stat2 = E.sb("stat2", [1, 512], F32)
        stat3 = E.sb("stat3", [1, 512], F32)
        xsq = E.sb("xsq", [128, 2 * 512], BF16)
        ps_a = E.ps()
        ps_b = E.ps()
        ps_c = E.ps()
        ps_d = E.ps()
        ps_e = E.ps()
        ps_f = E.ps()
        s_ld = E.sem("s_ld")
        s_lg = E.sem("s_lg")
        s_pe = E.sem("s_pe")
        s_ev = E.sem("s_ev")
        s_dv = E.sem("s_dv")
        s_w1 = E.sem("s_w1")
        s_w2 = E.sem("s_w2")
        s_xs = E.sem("s_xs")
        s_st = E.sem("s_st")
        sch = E.sch
        E.start()

        for (dst, src) in ((onesb, onesb_in), (ones1f, onesf_in), (onescb, onescb_in),
                           (identf, identf_in), (epsc, epsc_in), (rb1c, rb1_in),
                           (rb2r, rb2_in)):
            sch.dma("sync", dst[:], src[:, :], s_ld)
        for c in range(NFF):
            sch.dma("gpsimd", rw2[:, c * NE:(c + 1) * NE],
                    rw2_in[c * 128:(c + 1) * 128, :], s_lg)
        for c in range(FC):
            sch.dma("gpsimd", xn[:, c * TOK:(c + 1) * TOK],
                    xi[c * 128:(c + 1) * 128, :], s_lg)
        rw1_ld = sch.dma("gpsimd", w1[:, :].rearrange("p (c f) -> p c f", c=FC),
                         rw1_in[:, :].rearrange("(c p) f -> p c f", p=128), s_w1)
        all_ld = sch.val(s_ld)
        all_lg = sch.val(s_lg)

        env = {"xn": xn, "onescb": onescb, "ones1f": ones1f, "epsc": epsc,
               "stat": stat, "stat2": stat2, "stat3": stat3, "xsq": xsq,
               "ps_sum": ps_a, "ps_sq": ps_b, "ps_ba": ps_c, "ps_bb": ps_d,
               "s_pe": s_pe, "s_ev": s_ev, "s_dv": s_dv,
               "xn_ready": [(s_ld, all_ld), (s_lg, all_lg)]}
        ln_dv = emit_ln_fm(sch, env, TOK)

        def xnc(c, cols):
            return xn[:, c * TOK:(c + 1) * TOK][:, cols]

        pp = [ps_a, ps_b]
        pi = 0
        hprev = [(s_dv, None), (s_dv, None)]

        def h_phase(e_idx, tc, wsem, wld, bcol_fn, dst_dv_list):
            nonlocal pi
            cols = slice(tc * TC3, (tc + 1) * TC3)
            last = None
            for gf in range(NFF):
                bank = pp[pi % 2]
                slot = pi % 2
                pi += 1
                w = [(wsem, wld), hprev[slot], (s_dv, ln_dv)]
                for c in range(FC):
                    sch.call("tensor", "matmul", bank[:, 0:TC3],
                             w1[:, c * DFF + gf * 128: c * DFF + gf * 128 + 128],
                             xnc(c, cols), start=(c == 0), stop=(c == FC - 1),
                             wait=(w if c == 0 else []),
                             inc=(s_pe if c == FC - 1 else None))
                pe = sch.val(s_pe)
                d = sch.call("vector", "tensor_scalar", hh[:, gf * TC3:(gf + 1) * TC3],
                             bank[:, 0:TC3], bcol_fn(gf), 0.0, ALU.add, ALU.max,
                             wait=[(s_pe, pe)], inc=s_dv)
                hprev[slot] = (s_dv, d)
                last = d
            return last

        w_dv = None
        score_pe = None
        for tc in range(NTC3):
            hdv = h_phase(None, tc, s_w1, rw1_ld, lambda gf: rb1c[:, gf:gf + 1], None)
            for tt in range(2):
                trow = slice(tt * 128, (tt + 1) * 128)
                ti = tc * 2 + tt
                for c in range(NFF):
                    sch.call("tensor", "matmul", ps_e[:, ti * NE:(ti + 1) * NE],
                             hh[:, c * TC3:(c + 1) * TC3][:, trow],
                             rw2[:, c * NE:(c + 1) * NE],
                             start=(c == 0), stop=False,
                             wait=([(s_dv, hdv)] if c == 0 else []))
                score_pe = sch.call("tensor", "matmul",
                                    ps_e[:, ti * NE:(ti + 1) * NE],
                                    onesb[0:1, :], rb2r[0:1, :],
                                    start=False, stop=True, inc=s_pe)
        dv = sch.call("vector", "tensor_copy", lg[:], ps_e[:, 0:16 * NE],
                      wait=[(s_pe, score_pe)], inc=s_dv)
        for ti in range(16):
            l8 = lg[:, ti * NE:(ti + 1) * NE]
            m1 = m1c[:, 0:1]; m2 = m1c[:, 1:2]; dd = m1c[:, 2:3]
            s1 = m1c[:, 3:4]; s2 = m1c[:, 4:5]
            eq1 = eqb[:, 0:NE]; eq2 = eqb[:, NE:2 * NE]
            c_ = sch.call("vector", "reduce_max", m1, l8, AX.X, inc=s_dv)
            c_ = sch.call("vector", "tensor_scalar", eq1, l8, m1, None,
                          ALU.is_equal, inc=s_dv)
            c_ = sch.call("vector", "scalar_tensor_tensor", eq2, eq1, NEG, l8,
                          ALU.mult, ALU.add, inc=s_dv)
            c_ = sch.call("vector", "reduce_max", m2, eq2, AX.X, inc=s_dv)
            c_ = sch.call("vector", "tensor_sub", dd, m1, m2, inc=s_dv)
            dv2 = sch.call("vector", "tensor_scalar", eq2, l8, m2, None,
                           ALU.is_equal, inc=s_dv)
            ev = sch.call("scalar", "activation", s1, dd, AF.Sigmoid,
                          wait=[(s_dv, dv2)], inc=s_ev)
            c_ = sch.call("vector", "tensor_scalar", s2, s1, -1.0, 1.0, ALU.mult,
                          ALU.add, wait=[(s_ev, ev)], inc=s_dv)
            c_ = sch.call("vector", "tensor_scalar", eq1, eq1, s1, None, ALU.mult,
                          inc=s_dv)
            w_dv = sch.call("vector", "scalar_tensor_tensor",
                            ww[:, ti * NE:(ti + 1) * NE], eq2, s2, eq1,
                            ALU.mult, ALU.add, inc=s_dv)

        dp = [ps_c, ps_d]
        di = 0
        dn_prev = [(s_dv, None), (s_dv, None)]
        w1_pe = None
        w2_pe = [None, None]
        b1_dv = [None, None]
        b2_pe = [None, None]
        acc_tr_pe = None
        prev_st = None
        w1_ld = rw1_ld
        for tc in range(NTC3):
            for e in range(NE):
                bslot = e % 2
                w1_ld = sch.dma("gpsimd", w1[:, :].rearrange("p (c f) -> p c f", c=FC),
                                w1_in[e, :, :].rearrange("(c p) f -> p c f", p=128),
                                s_w1, wait=[(s_pe, w1_pe), (s_dv, w_dv)])
                w2_ld = sch.dma("gpsimd", w2[:, :].rearrange("p (c f) -> p c f", c=NFF),
                                w2_in[e, :, :].rearrange("(c p) f -> p c f", p=128),
                                s_w2, wait=[(s_pe, w2_pe[0]), (s_pe, w2_pe[1])])
                sch.dma("gpsimd", b1c[:, bslot * NFF:(bslot + 1) * NFF],
                        b1_in[e, :, :], s_w1, wait=[(s_dv, b1_dv[bslot])])
                b1_ld = sch.val(s_w1)
                b2_ld = sch.dma("gpsimd", b2r[0:1, bslot * D:(bslot + 1) * D],
                                b2_in[0:1, e * D:(e + 1) * D], s_w2,
                                wait=[(s_pe, b2_pe[bslot])])
                hdv = h_phase(e, tc, s_w1, b1_ld,
                              lambda gf, bslot=bslot: b1c[:, bslot * NFF + gf:bslot * NFF + gf + 1],
                              None)
                b1_dv[bslot] = hdv
                w1_pe = sch.val(s_pe)
                for tt in range(2):
                    trow = slice(tt * 128, (tt + 1) * 128)
                    wcol = ww[:, (tc * 2 + tt) * NE + e:(tc * 2 + tt) * NE + e + 1]
                    for half in range(2):
                        bank = dp[di % 2]
                        slot = di % 2
                        di += 1
                        w = [(s_w2, b2_ld), (s_dv, hdv), dn_prev[slot]]
                        for c in range(NFF):
                            sch.call("tensor", "matmul", bank[:, 0:512],
                                     hh[:, c * TC3:(c + 1) * TC3][:, trow],
                                     w2[:, c * D + half * 512: c * D + (half + 1) * 512],
                                     start=(c == 0), stop=False,
                                     wait=(w if c == 0 else []))
                        pe2 = sch.call("tensor", "matmul", bank[:, 0:512],
                                       onesb[0:1, 0:128],
                                       b2r[0:1, bslot * D + half * 512: bslot * D + (half + 1) * 512],
                                       start=False, stop=True, inc=s_pe)
                        w2_pe[slot] = pe2
                        b2_pe[bslot] = pe2
                        aslice = acc[:, tt * D + half * 512: tt * D + (half + 1) * 512]
                        if e == 0:
                            d = sch.call("vector", "tensor_scalar", aslice,
                                         bank[:, 0:512], wcol, None, ALU.mult,
                                         wait=[(s_pe, pe2), (s_pe, acc_tr_pe)],
                                         inc=s_dv)
                        else:
                            d = sch.call("vector", "scalar_tensor_tensor", aslice,
                                         bank[:, 0:512], wcol, aslice,
                                         ALU.mult, ALU.add,
                                         wait=[(s_pe, pe2)], inc=s_dv)
                        dn_prev[slot] = (s_dv, d)
            acc_dv = sch.val(s_dv)
            uev = None
            xss = None
            for tt in range(2):
                for fo in range(FC):
                    xslot = (tt * FC + fo) % 2
                    xss = sch.dma("sync", xs[:, xslot * 128:(xslot + 1) * 128],
                                  xi[fo * 128:(fo + 1) * 128,
                                     tc * TC3 + tt * 128: tc * TC3 + (tt + 1) * 128],
                                  s_xs, wait=[(s_st, prev_st)])
                    pe_t = sch.call("tensor", "matmul", ps_f[:, 0:128],
                                    acc[:, tt * D + fo * 128: tt * D + (fo + 1) * 128],
                                    identf[:, :], start=True, stop=True,
                                    wait=[(s_dv, acc_dv), (s_dv, uev)], inc=s_pe)
                    xsl = xs[:, xslot * 128:(xslot + 1) * 128]
                    uev = sch.call("vector", "tensor_add", xsl, ps_f[:, 0:128], xsl,
                                   wait=[(s_pe, pe_t), (s_xs, xss)], inc=s_dv)
                    sch.dma("sync", xo[fo * 128:(fo + 1) * 128,
                                       tc * TC3 + tt * 128: tc * TC3 + (tt + 1) * 128],
                            xsl, s_st, wait=[(s_dv, uev)])
                    prev_st = sch.val(s_st)
            acc_tr_pe = sch.val(s_pe)
        sch.barrier_wait("sync", s_st, sch.val(s_st))
    return E.finish()


def host_l3_ffn(rw1, rb1, rw2, rb2, W1, b1, W2, b2):
    b1 = np.asarray(b1, np.float32)
    b1c = np.stack([chunk_cols(b1[e]) for e in range(NE)])
    return {
        "rw1_in": np.ascontiguousarray(np.asarray(rw1, np.float32)),
        "rb1_in": chunk_cols(rb1),
        "rw2_in": np.ascontiguousarray(np.asarray(rw2, np.float32)),
        "rb2_in": np.asarray(rb2, np.float32).reshape(1, -1).astype(BF),
        "w1_in": np.ascontiguousarray(np.asarray(W1, np.float32)),
        "b1_in": np.ascontiguousarray(b1c),
        "w2_in": np.ascontiguousarray(np.asarray(W2, np.float32)),
        "b2_in": np.asarray(b2, np.float32).reshape(1, -1).copy(),
        "onesb_in": _ones_row(BF), "onesf_in": _ones_row(np.float32),
        "onescb_in": np.ones((128, 1), BF),
        "identf_in": np.eye(128, dtype=np.float32),
        "epsc_in": np.full((1, 1), EPS, np.float32),
    }


L1P = 640
WTAB = 3328
HS1 = 64


def build_l1_attn(mega=None, io=None, pfx="l1a_"):
    E = BlockEnv(mega, pfx)
    nc = E.nc
    if mega is None:
        xf = nc.dram_tensor("xf", [D, S], F32, kind="ExternalInput").ap()
        co = nc.dram_tensor("co", [256, S], F32, kind="ExternalOutput").ap()
    else:
        xf = io.get("xf"); co = io["co"]
    wqT = E.dram_in("wqT", [D, 256], F32)
    wkT = E.dram_in("wkT", [D, 256], F32)
    wvT = E.dram_in("wvT", [D, 256], F32)
    bqc = E.dram_in("bqc", [128, 2], F32)
    bkc = E.dram_in("bkc", [128, 2], F32)
    bvr = E.dram_in("bvr", [1, 256], BF16)
    ept2 = E.dram_in("ept2", [128, WTAB], F32)
    eptr2 = E.dram_in("eptr2", [128, WTAB], F32)
    onesb_in = E.dram_in("onesb_in", [1, 128], BF16)
    onesf_in = E.dram_in("onesf_in", [1, 128], F32)
    onescf_in = E.dram_in("onescf_in", [128, 1], F32)
    onescb_in = E.dram_in("onescb_in", [128, 1], BF16)
    onestb_in = E.dram_in("onestb_in", [1, 512], BF16)
    ident_in = E.dram_in("ident_in", [128, 128], BF16)
    epsc_in = E.dram_in("epsc_in", [1, 1], F32)
    qdsc = E.dram_int("qdsc", [4, S, WTAB], BF16)
    kdsc = E.dram_int("kdsc", [4, S, WTAB], BF16)

    NLC = S // 512
    NTC7 = 7
    if True:
        _e = E.ctx.enter_context
        xn = E.sb("xn", [128, FC * S], BF16)
        qhh = E.sb("qhh", [128, 2 * S], BF16)
        khh = E.sb("khh", [128, 2 * S], BF16)
        vhh = E.sb("vhh", [128, 2 * S], BF16)
        epb2 = E.sb("epb2", [128, WTAB], BF16)
        eprb2 = E.sb("eprb2", [128, WTAB], BF16)
        bq3 = E.sb("bq3", [128, 4], F32)
        bvrs = E.sb("bvrs", [1, 256], BF16)
        onesb = E.sb("onesb", [1, 128], BF16)
        ones1f = E.sb("ones1f", [1, 128], F32)
        onescf = E.sb("onescf", [128, 1], F32)
        onescb = E.sb("onescb", [128, 1], BF16)
        onestb = E.sb("onestb", [1, 512], BF16)
        ident = E.sb("ident", [128, 128], BF16)
        epsc = E.sb("epsc", [1, 1], F32)
        stat = E.sb("stat", [1, 512], F32)
        stat2 = E.sb("stat2", [1, 512], F32)
        stat3 = E.sb("stat3", [1, 512], F32)
        xsq = E.sb("xsq", [128, 2 * 512], BF16)
        ps_a = E.ps()
        ps_b = E.ps()
        ps_c = E.ps()
        ps_d = E.ps()
        ps_s = E.ps()
        ps_t = E.ps()
        ps_x = E.ps()
        ps_z = E.ps()
        s_ld = E.sem("s_ld")
        s_lg = E.sem("s_lg")
        s_pe = E.sem("s_pe")
        s_ev = E.sem("s_ev")
        s_dv = E.sem("s_dv")
        s_tb = E.sem_list("s_tb", 2)
        s_ga = E.sem("s_ga")
        s_gb = E.sem_list("s_gb", 4)
        s_st = E.sem("s_st")
        sch = E.sch
        E.start()

        sch.dma("sync", onesb[:], onesb_in[:, :], s_ld)
        sch.dma("sync", ones1f[:], onesf_in[:, :], s_ld)
        sch.dma("sync", onescf[:], onescf_in[:, :], s_ld)
        sch.dma("sync", onescb[:], onescb_in[:, :], s_ld)
        sch.dma("sync", onestb[:], onestb_in[:, :], s_ld)
        sch.dma("sync", ident[:], ident_in[:, :], s_ld)
        sch.dma("sync", epsc[:], epsc_in[:, :], s_ld)
        sch.dma("sync", bq3[:, 0:2], bqc[:, :], s_ld)
        sch.dma("sync", bq3[:, 2:4], bkc[:, :], s_ld)
        sch.dma("gpsimd", bvrs[:], bvr[:, :], s_lg)
        sch.dma("gpsimd", epb2[:], ept2[:, :], s_lg)
        sch.dma("gpsimd", eprb2[:], eptr2[:, :], s_lg)
        ctx2 = E.ctx.enter_context(ExitStack())
        _e2 = ctx2.enter_context
        wts = _e2(nc.sbuf_tensor(E.nm("wts"), [128, FC * 3 * 256], BF16))
        tevac = _e2(nc.sbuf_tensor(E.nm("tevac"), [128, 4 * WTAB], BF16))
        for (wdst, wsrc) in ((0, wqT), (1, wkT), (2, wvT)):
            for c in range(FC):
                sch.dma("gpsimd", wts[:, c * 768 + wdst * 256: c * 768 + wdst * 256 + 256],
                        wsrc[c * 128:(c + 1) * 128, :], s_lg)
        for c in range(FC):
            if io is not None and "xf_load" in io:
                io["xf_load"](sch, c, xn[:, c * S:(c + 1) * S], s_lg)
            else:
                sch.dma("gpsimd", xn[:, c * S:(c + 1) * S],
                        xf[c * 128:(c + 1) * 128, :], s_lg)
        all_ld = sch.val(s_ld)
        all_lg = sch.val(s_lg)

        env = {"xn": xn, "onescb": onescb, "ones1f": ones1f, "epsc": epsc,
               "stat": stat, "stat2": stat2, "stat3": stat3, "xsq": xsq,
               "ps_sum": ps_a, "ps_sq": ps_b, "ps_ba": ps_c, "ps_bb": ps_d,
               "s_pe": s_pe, "s_ev": s_ev, "s_dv": s_dv,
               "xn_ready": [(s_ld, all_ld), (s_lg, all_lg)]}
        ln_dv = emit_ln_fm(sch, env, S)

        def xnc(c, cols):
            return xn[:, c * S:(c + 1) * S][:, cols]

        pp = [ps_a, ps_b]
        pi = 0
        prev = [(s_ev, None), (s_ev, None)]
        for tb in range(S // 512):
            cols = slice(tb * 512, (tb + 1) * 512)
            for hp in range(2):
                for which in range(2):
                    bank = pp[pi % 2]
                    slot = pi % 2
                    pi += 1
                    w = [(s_dv, ln_dv), prev[slot]]
                    for c in range(FC):
                        sch.call("tensor", "matmul", bank[:, 0:512],
                                 wts[:, c * 768 + which * 256 + hp * 128:
                                     c * 768 + which * 256 + (hp + 1) * 128],
                                 xnc(c, cols), start=(c == 0), stop=(c == FC - 1),
                                 wait=(w if c == 0 else []),
                                 inc=(s_pe if c == FC - 1 else None))
                    pe = sch.val(s_pe)
                    dst = (qhh if which == 0 else khh)[:, hp * S:(hp + 1) * S][:, cols]
                    dv = sch.call("vector", "tensor_scalar", dst, bank[:, 0:512],
                                  bq3[:, which * 2 + hp:which * 2 + hp + 1], None,
                                  ALU.add, wait=[(s_pe, pe)], inc=s_dv)
                    prev[slot] = (s_dv, dv)
        for kt in range(NKT):
            tcol = slice(kt * 128, (kt + 1) * 128)
            for hp in range(2):
                bank = pp[pi % 2]
                slot = pi % 2
                pi += 1
                w = [(s_dv, ln_dv), prev[slot]]
                for c in range(FC):
                    sch.call("tensor", "matmul", bank[:, 0:128],
                             xnc(c, tcol),
                             wts[:, c * 768 + 512 + hp * 128: c * 768 + 512 + (hp + 1) * 128],
                             start=(c == 0), stop=False,
                             wait=(w if c == 0 else []))
                sch.call("tensor", "matmul", bank[:, 0:128], onesb[0:1, :],
                         bvrs[0:1, hp * 128:(hp + 1) * 128],
                         start=False, stop=True, inc=s_pe)
                pe = sch.val(s_pe)
                ev = sch.call("scalar", "activation",
                              vhh[:, (kt * 2 + hp) * 128:(kt * 2 + hp + 1) * 128],
                              bank[:, 0:128], AF.Copy, wait=[(s_pe, pe)], inc=s_ev)
                prev[slot] = (s_ev, ev)
        proj_dv = sch.val(s_dv)
        proj_ev = sch.val(s_ev)

        NTC7 = 7
        tprev = [(s_ev, None), (s_ev, None)]
        tev_dma = [None, None]
        bank_ev = None
        for hp in range(2):
            for (tbl, src, scr) in ((eprb2, qhh, qdsc), (epb2, khh, kdsc)):
                for lt in range(NKT):
                    tcol = slice(lt * 128, (lt + 1) * 128)
                    dslot = lt % 2
                    w0 = [(s_dv, proj_dv), (s_ev, proj_ev), tprev[dslot],
                          (s_ev, bank_ev), (s_tb[dslot], tev_dma[dslot])]
                    sA = tevac[:, (2 * dslot) * WTAB:(2 * dslot + 1) * WTAB]
                    sB = tevac[:, (2 * dslot + 1) * WTAB:(2 * dslot + 2) * WTAB]
                    last_ev = None
                    for ch in range(NTC7):
                        cw = 512 if ch < 6 else WTAB - 6 * 512
                        for hh in range(2):
                            bank = [ps_s, ps_t][hh]
                            wch = []
                            if ch == 0 and hh == 0:
                                wch = list(w0)
                            elif hh == 0 and last_ev is not None:
                                wch = [(s_ev, last_ev)]
                            sch.call("tensor", "matmul", bank[:, 0:cw],
                                     src[:, hp * S:(hp + 1) * S][:, tcol][hh * 64:(hh + 1) * 64, :],
                                     tbl[hh * 64:(hh + 1) * 64, ch * 512:ch * 512 + cw],
                                     start=True, stop=True,
                                     tile_position=(hh * 64, 0),
                                     wait=wch, inc=(s_pe if hh == 1 else None))
                        pe = sch.val(s_pe)
                        sch.call("scalar", "activation", sA[:, ch * 512:ch * 512 + cw],
                                 ps_s[:, 0:cw], AF.Copy, wait=[(s_pe, pe)], inc=s_ev)
                        last_ev = sch.call("scalar", "activation",
                                           sB[:, ch * 512:ch * 512 + cw],
                                           ps_t[:, 0:cw], AF.Copy, inc=s_ev)
                    tprev[dslot] = (s_ev, last_ev)
                    bank_ev = last_ev
                    sch.dma("gpsimd", scr[2 * hp, lt * 128:(lt + 1) * 128, :],
                            sA[:, :], s_tb[dslot], wait=[(s_ev, last_ev)])
                    tev_dma[dslot] = sch.dma(
                        "gpsimd", scr[2 * hp + 1, lt * 128:(lt + 1) * 128, :],
                        sB[:, :], s_tb[dslot])
        tab_done = [(s_tb[0], sch.val(s_tb[0])), (s_tb[1], sch.val(s_tb[1]))]
        ctx2.close()
        scope_gate = tab_done + [(s_pe, sch.val(s_pe)), (s_ev, sch.val(s_ev)),
                                 (s_dv, sch.val(s_dv))]
        gat = E.sb("gat", [128, 4 * 2816], BF16)
        gbt = E.sb("gbt", [128, 4 * 512], BF16)
        expt = E.sb("expt", [128, 4 * 512], BF16)
        zacc = E.sb("zacc", [128, 512], F32)
        rz = E.sb("rz", [1, 512], F32)
        rzb = E.sb("rzb", [128, 512], F32)
        cov = E.sb("cov", [64, 512], F32)
        qer = E.sb("qer", [1, 2 * S], BF16)
        ker = E.sb("ker", [1, 2 * S], BF16)

        sp = [ps_s, ps_t]
        si = 0
        sprev = [(s_ev, sch.val(s_ev)), (s_ev, sch.val(s_ev))]
        ga_pe = None
        gb_pe = [None] * 4
        gb_ld = [None] * 4
        e_cons = [None] * 4
        e_zdv = [None] * 4
        prev_pz = None
        prev_cov = None
        prev_cost = None
        prev_rzb_ev = None
        far_pe = None
        for hloc in range(4):
            hp, hh = hloc // 2, hloc % 2
            erow_dv = None
            for (dst, src) in ((qer, qhh), (ker, khh)):
                for side in range(2):
                    ecol = epb2[hh * 64:(hh + 1) * 64,
                                (0 if side == 0 else WTAB - 1):(1 if side == 0 else WTAB)]
                    for ch8 in range(8):
                        ccols = slice(ch8 * 512, (ch8 + 1) * 512)
                        pe = sch.call("tensor", "matmul", ps_z[0:1, 0:512], ecol,
                                      src[:, hp * S:(hp + 1) * S][:, ccols][hh * 64:(hh + 1) * 64, :],
                                      start=True, stop=True,
                                      tile_position=(hh * 64, 0),
                                      wait=[(s_dv, erow_dv)]
                                      + ([(s_pe, far_pe), (s_dv, prev_cov)]
                                         + (scope_gate if hloc == 0 else [])
                                         if ch8 == 0 and side == 0 and dst is qer else []),
                                      inc=s_pe)
                        erow_dv = sch.call(
                            "vector", "tensor_copy",
                            dst[0:1, side * S + ch8 * 512: side * S + (ch8 + 1) * 512],
                            ps_z[0:1, 0:512], wait=[(s_pe, pe)], inc=s_dv)
            erows_dv = sch.val(s_dv)
            for lc in range(NLC):
                l0c = lc * 512
                rt_lo = max(0, 4 * lc - 9)
                rt_hi = min(NKT - 1, 4 * lc + 12)
                rmin = rt_lo * 128
                wr = (rt_hi - rt_lo + 1) * 128
                ga_ld = None
                for j in range(4):
                    lt = 4 * lc + j
                    l0 = lt * 128
                    asrc = AP(tensor=qdsc.tensor,
                              offset=(2 * hp + hh) * S * WTAB + l0 * (WTAB - 1) + rmin + 1664,
                              ap=[[WTAB - 1, 128], [1, wr]])
                    ga_ld = sch.dma("gpsimd", gat[:, j * 2816:j * 2816 + wr], asrc,
                                    s_ga, wait=([(s_pe, ga_pe)] + tab_done if j == 0 else []))
                for rt in range(NKT):
                    r0 = rt * 128
                    bank = sp[si % 2]
                    slot = si % 2
                    si += 1
                    gathered = rt_lo <= rt <= rt_hi
                    w = [sprev[slot], (s_dv, erows_dv)]
                    pe_sc = sch.call("tensor", "matmul", bank[:, 0:512],
                                     khh[:, hp * S + r0: hp * S + r0 + 128][hh * 64:(hh + 1) * 64, :],
                                     qhh[:, hp * S + l0c: hp * S + l0c + 512][hh * 64:(hh + 1) * 64, :],
                                     start=True, stop=False,
                                     tile_position=(hh * 64, 0),
                                     wait=w, inc=None)
                    if gathered:
                        bslot = rt % 4
                        gb_ld[bslot] = sch.dma(
                            "gpsimd", gbt[:, bslot * 512:(bslot + 1) * 512],
                            AP(tensor=kdsc.tensor,
                               offset=(2 * hp + hh) * S * WTAB + r0 * (WTAB - 1) + l0c + 1663,
                               ap=[[WTAB - 1, 128], [1, 512]]),
                            s_gb[bslot], wait=[(s_pe, gb_pe[bslot])])
                        for j in range(4):
                            sch.call("tensor", "matmul", bank[:, j * 128:(j + 1) * 128],
                                     gat[:, j * 2816 + (rt - rt_lo) * 128:
                                         j * 2816 + (rt - rt_lo) * 128 + 128],
                                     ident[:], start=False, stop=False,
                                     wait=([(s_ga, ga_ld)] if j == 0 else []))
                        gb_pe[bslot] = sch.call(
                            "tensor", "matmul", bank[:, 0:512], ident[:],
                            gbt[:, bslot * 512:(bslot + 1) * 512],
                            start=False, stop=True,
                            wait=[(s_gb[bslot], gb_ld[bslot])], inc=s_pe)
                        pe_sc = gb_pe[bslot]
                        ga_pe = pe_sc
                    else:
                        side = 1 if rt < rt_lo else 0
                        sch.call("tensor", "matmul", bank[:, 0:512], onesb[0:1, :],
                                 qer[0:1, side * S + l0c: side * S + l0c + 512],
                                 start=False, stop=False)
                        pe_sc = sch.call("tensor", "matmul", bank[:, 0:512],
                                         ker[0:1, side * S + r0: side * S + r0 + 128],
                                         onestb[0:1, :], start=False, stop=True,
                                         inc=s_pe)
                        far_pe = pe_sc
                    eslot = si % 4
                    et = expt[:, eslot * 512:(eslot + 1) * 512]
                    ev = sch.call("scalar", "activation", et, bank[:, 0:512],
                                  AF.Exp, scale=0.125,
                                  wait=[(s_pe, pe_sc), (s_pe, e_cons[eslot]),
                                        (s_dv, e_zdv[eslot])], inc=s_ev)
                    sprev[slot] = (s_ev, ev)
                    zw = [(s_ev, ev)]
                    if rt == 0:
                        zw += [(s_pe, prev_pz)]
                        zdv = sch.call("vector", "tensor_copy", zacc[:], et,
                                       wait=zw, inc=s_dv)
                    else:
                        zdv = sch.call("vector", "tensor_add", zacc[:], zacc[:], et,
                                       wait=zw, inc=s_dv)
                    e_zdv[eslot] = zdv
                    avw = [(s_ev, ev)]
                    if rt == 0:
                        avw.append((s_dv, prev_cov))
                    e_cons[eslot] = sch.call(
                        "tensor", "matmul", ps_x[0:64, 0:512],
                        vhh[:, (rt * 2 + hp) * 128 + hh * 64:(rt * 2 + hp) * 128 + (hh + 1) * 64],
                        et, start=(rt == 0), stop=(rt == NKT - 1),
                        wait=avw, inc=s_pe)
                av_pe = sch.val(s_pe)
                last_zdv = sch.val(s_dv)
                pz = sch.call("tensor", "matmul", ps_z[0:1, 0:512], onescf[:, 0:1],
                              zacc[:], start=True, stop=True,
                              wait=[(s_dv, last_zdv), (s_dv, prev_cov)], inc=s_pe)
                prev_pz = pz
                d1 = sch.call("vector", "reciprocal", rz[0:1, :], ps_z[0:1, 0:512],
                              wait=[(s_pe, pz)], inc=s_dv)
                bp = sch.call("tensor", "matmul", ps_c[0:64, 0:512],
                              ones1f[0:1, 0:64], rz[0:1, :], start=True, stop=True,
                              wait=[(s_dv, d1), (s_ev, prev_rzb_ev)], inc=s_pe)
                evz = sch.call("scalar", "activation", rzb[0:64, :], ps_c[0:64, 0:512],
                               AF.Copy, wait=[(s_pe, bp)], inc=s_ev)
                prev_rzb_ev = evz
                cov_dv = sch.call("vector", "tensor_mul", cov[:], ps_x[0:64, 0:512],
                                  rzb[0:64, :],
                                  wait=[(s_ev, evz), (s_pe, av_pe), (s_st, prev_cost)],
                                  inc=s_dv)
                prev_cov = cov_dv
                prev_cost = sch.dma("sync", co[hloc * 64:(hloc + 1) * 64, l0c:l0c + 512],
                                    cov[:], s_st, wait=[(s_dv, cov_dv)])
        sch.barrier_wait("sync", s_st, sch.val(s_st))
    return E.finish()


def host_l1_attn(wq, wk, wv, bq, bk, bv, dist_emb):
    wqT = np.ascontiguousarray(np.asarray(wq, np.float32).T)
    wkT = np.ascontiguousarray(np.asarray(wk, np.float32).T)
    wvT = np.ascontiguousarray(np.asarray(wv, np.float32).T)
    bq = np.asarray(bq, np.float32)
    bk = np.asarray(bk, np.float32)
    bv = np.asarray(bv, np.float32)
    E = np.asarray(dist_emb, np.float32)
    idx = np.clip(np.arange(WTAB) - L1P, 0, 2046)
    ep = E[idx]
    ept2 = np.ascontiguousarray(np.vstack([ep.T, ep.T]))
    eptr2 = np.ascontiguousarray(ept2[:, ::-1])
    const = {
        "ept2": ept2, "eptr2": eptr2,
        "onesb_in": _ones_row(BF), "onesf_in": _ones_row(np.float32),
        "onescf_in": np.ones((128, 1), np.float32),
        "onescb_in": np.ones((128, 1), BF),
        "onestb_in": np.ones((1, 512), BF),
        "ident_in": _ident128(),
        "epsc_in": np.full((1, 1), EPS, np.float32),
    }

    def per(core, li):
        h2 = core % 2
        c0 = (h2 * 8 + li * 4) * 64
        m = dict(const)
        m["wqT"] = np.ascontiguousarray(wqT[:, c0:c0 + 256])
        m["wkT"] = np.ascontiguousarray(wkT[:, c0:c0 + 256])
        m["wvT"] = np.ascontiguousarray(wvT[:, c0:c0 + 256])
        m["bqc"] = chunk_cols(bq[c0:c0 + 256])
        m["bkc"] = chunk_cols(bk[c0:c0 + 256])
        m["bvr"] = bv[c0:c0 + 256].reshape(1, -1).astype(BF)
        return m
    return per


TV = 2560
TK = 2304
DI2 = 2816
FCA = DI2 // 128


def build_l1f_pre(mega=None, io=None):
    E = BlockEnv(mega, "l1fp_")
    nc = E.nc
    if mega is None:
        xi = nc.dram_tensor("xi", [D, TOK], F32, kind="ExternalInput").ap()
        ci = nc.dram_tensor("ci", [D, TOK], F32, kind="ExternalInput").ap()
        yo = nc.dram_tensor("yo", [D, TOK], F32, kind="ExternalOutput").ap()
    else:
        xi = io["xi"]; ci = io.get("ci"); yo = io["yo"]
    if True:
        _e = E.ctx.enter_context
        xa = E.sb("xa", [128, 2 * 512], F32)
        xb = E.sb("xb", [128, 2 * 512], F32)
        s_ld = E.sem("s_ld")
        s_dv = E.sem("s_dv")
        s_ev = E.sem("s_ev")
        s_st = E.sem("s_st")
        sch = E.sch
        E.start()
        prev_st = [None, None]
        prev_add = [None, None]
        for i in range(FC * 4):
            c = i // 4
            tb = i % 4
            cols = slice(tb * 512, (tb + 1) * 512)
            slot = i % 2
            w = [(s_st, prev_st[slot])]
            sch.dma("sync", xa[:, slot * 512:(slot + 1) * 512],
                    xi[c * 128:(c + 1) * 128, cols], s_ld, wait=w)
            if io is not None and "ci_load" in io:
                ldw = io["ci_load"](sch, c, tb, xb[:, slot * 512:(slot + 1) * 512],
                                    [(s_dv, prev_add[slot])])
            else:
                ldw = (s_ld, sch.dma("sync", xb[:, slot * 512:(slot + 1) * 512],
                                     ci[c * 128:(c + 1) * 128, cols], s_ld))
            d = sch.call("vector", "tensor_add", xa[:, slot * 512:(slot + 1) * 512],
                         xa[:, slot * 512:(slot + 1) * 512],
                         xb[:, slot * 512:(slot + 1) * 512],
                         wait=[(s_ld, sch.val(s_ld)), ldw], inc=s_dv)
            prev_add[slot] = d
            prev_st[slot] = sch.dma("sync", yo[c * 128:(c + 1) * 128, cols],
                                    xa[:, slot * 512:(slot + 1) * 512], s_st,
                                    wait=[(s_dv, d)])
        sch.barrier_wait("sync", s_st, sch.val(s_st))
    return E.finish()


def build_l1f_attn(ln_first=False, mega=None, io=None, pfx="l1fa_"):
    E = BlockEnv(mega, pfx)
    nc = E.nc
    if mega is None:
        xv = nc.dram_tensor("xv", [D, TV], F32, kind="ExternalInput").ap()
        xr = nc.dram_tensor("xr", [D, TOK], F32, kind="ExternalInput").ap()
        xo = nc.dram_tensor("xo", [D, TOK], F32, kind="ExternalOutput").ap()
    else:
        xv = io.get("xv"); xr = io["xr"]; xo = io["xo"]
    wqkv = E.dram_in("wqkv", [D, 3 * D], F32)
    woT = E.dram_in("woT", [D, D], F32)
    qs2 = E.dram_in("qs2", [128, 2], F32)
    ks2 = E.dram_in("ks2", [128, 2], F32)
    cst = E.dram_in("cst", [128, TK], F32)
    snt = E.dram_in("snt", [128, TK], F32)
    csti = E.dram_in("csti", [128, TK], F32)
    snti = E.dram_in("snti", [128, TK], F32)
    rotT = E.dram_in("rotT", [128, 128], BF16)
    wmask = E.dram_in("wmask", [4, 128, 256], BF16)
    kvis = E.dram_in("kvis", [128, 32], F32)
    onesb_in = E.dram_in("onesb_in", [1, 128], BF16)
    onesf_in = E.dram_in("onesf_in", [1, 128], F32)
    onescf_in = E.dram_in("onescf_in", [128, 1], F32)
    onescb_in = E.dram_in("onescb_in", [128, 1], BF16)
    ident_in = E.dram_in("ident_in", [128, 128], BF16)
    epsc_in = E.dram_in("epsc_in", [1, 1], F32)
    mum_in = E.dram_in("mum_in", [1, 1], F32)

    NCH = 5
    def chw(ch):
        return 512 if ch < 4 else 256
    if True:
        _e = E.ctx.enter_context
        xb = E.sb("xb", [128, FC * TV], BF16)
        wq3 = E.sb("wq3", [128, FC * 384], BF16)
        wo = E.sb("wo", [128, FC * D], BF16)
        cxs = E.sb("cxs", [128, FC * TOK], BF16)
        qp = E.sb("qp", [128, TK], BF16)
        kp = E.sb("kp", [128, TK], BF16)
        vp = E.sb("vp", [128, 18 * 128], BF16)
        qf = E.sb("qf", [128, TK], BF16)
        kf = E.sb("kf", [128, TK], BF16)
        sq2 = E.sb("sq2", [128, TK], BF16)
        ta = E.sb("ta", [128, TK], BF16)
        tb_ = E.sb("tb_", [128, TK], BF16)
        tc_ = E.sb("tc_", [128, TK], BF16)
        td_ = E.sb("td_", [128, TK], BF16)
        rot = E.sb("rot", [128, 128], BF16)
        nr = E.sb("nr", [1, 2048], F32)
        expt = E.sb("expt", [128, 4 * 256], BF16)
        zca = E.sb("zca", [128, 2 * 256], F32)
        rzr = E.sb("rzr", [1, 2 * 256], F32)
        rzb = E.sb("rzb", [128, 256], F32)
        msk = E.sb("msk", [128, 4 * 256], BF16)
        kvi = E.sb("kvi", [128, 32], F32)
        qsc = E.sb("qsc", [128, 4], F32)
        xs = E.sb("xs", [128, FC * 512], F32)
        utm = E.sb("utm", [128, 512], F32)
        onesb = E.sb("onesb", [1, 128], BF16)
        ones1f = E.sb("ones1f", [1, 128], F32)
        onescf = E.sb("onescf", [128, 1], F32)
        onescb = E.sb("onescb", [128, 1], BF16)
        ident = E.sb("ident", [128, 128], BF16)
        epsc = E.sb("epsc", [1, 1], F32)
        mum = E.sb("mum", [1, 1], F32)
        stat = E.sb("stat", [1, 512], F32)
        stat2 = E.sb("stat2", [1, 512], F32)
        stat3 = E.sb("stat3", [1, 512], F32)
        xsq = E.sb("xsq", [128, 2 * 512], BF16)
        ps_a = E.ps()
        ps_b = E.ps()
        ps_c = E.ps()
        ps_d = E.ps()
        ps_e = E.ps()
        ps_f = E.ps()
        ps_g = E.ps()
        ps_h = E.ps()
        s_ld = E.sem("s_ld")
        s_lg = E.sem("s_lg")
        s_wl = E.sem("s_wl")
        s_pe = E.sem("s_pe")
        s_ev = E.sem("s_ev")
        s_dv = E.sem("s_dv")
        s_xs = E.sem("s_xs")
        s_st = E.sem("s_st")
        sch = E.sch
        E.start()

        for (dst, src) in ((onesb, onesb_in), (ones1f, onesf_in), (onescf, onescf_in),
                           (onescb, onescb_in), (ident, ident_in), (epsc, epsc_in),
                           (mum, mum_in), (kvi, kvis), (rot, rotT),
                           (qsc[:, 0:2], qs2), (qsc[:, 2:4], ks2)):
            sch.dma("sync", dst if isinstance(dst, AP) else dst[:], src[:, :], s_ld)
        for v_ in range(4):
            sch.dma("sync", msk[:, v_ * 256:(v_ + 1) * 256], wmask[v_, :, :], s_ld)
        for c in range(FC):
            sch.dma("gpsimd", wo[:, c * D:(c + 1) * D],
                    woT[c * 128:(c + 1) * 128, :], s_lg)
        for (dst, src) in ((ta, cst), (tb_, snt), (tc_, csti), (td_, snti)):
            sch.dma("gpsimd", dst[:], src[:, :], s_lg)
        for c in range(FC):
            if io is not None and "xv_load" in io:
                io["xv_load"](sch, c, xb[:, c * TV:(c + 1) * TV], s_lg)
            else:
                sch.dma("gpsimd", xb[:, c * TV:(c + 1) * TV],
                        xv[c * 128:(c + 1) * 128, :], s_lg)
        all_ld = sch.val(s_ld)
        all_lg = sch.val(s_lg)
        tdv = None
        for (dst, sidx) in ((ta, 0), (tb_, 1), (tc_, 2), (td_, 3)):
            tdv = sch.call("vector", "tensor_scalar", dst[:], dst[:],
                           qsc[:, sidx:sidx + 1], None, ALU.mult,
                           wait=[(s_lg, all_lg), (s_ld, all_ld)], inc=s_dv)
        tab_dv = tdv

        ln_dv = None
        if ln_first:
            env = {"xn": xb, "onescb": onescb, "ones1f": ones1f, "epsc": epsc,
                   "stat": stat, "stat2": stat2, "stat3": stat3, "xsq": xsq,
                   "ps_sum": ps_a, "ps_sq": ps_b, "ps_ba": ps_c, "ps_bb": ps_d,
                   "s_pe": s_pe, "s_ev": s_ev, "s_dv": s_dv,
                   "xn_ready": [(s_ld, all_ld), (s_lg, all_lg)],
                   "chunk_fn": lambda c: xb[:, c * TV:(c + 1) * TV]}
            ln_dv = emit_ln_fm(sch, env, TV)

        def xsh(c, cols):
            off = 256 if c < 4 else 255
            return xb[:, c * TV + off: c * TV + off + TK][:, cols]

        wl_prev = None
        hp_pe = None
        pp = [ps_a, ps_b]
        pi = 0
        prev = [(s_ev, None), (s_ev, None)]
        sprev = [(s_ev, None), (s_ev, None)]
        e_cons = [None] * 4
        e_zdv = [None] * 4
        prev_ctx_dv = None
        prev_pz = None
        prev_rzb = None
        for hp in range(FC):
            wl_prev = sch.dma("gpsimd", wq3[:, :].rearrange("p (c x) -> p c x", c=FC),
                              wqkv[:, hp * 384:(hp + 1) * 384]
                              .rearrange("(c p) x -> p c x", p=128),
                              s_wl, wait=[(s_pe, hp_pe)])
            for (which, dst) in ((0, qp), (1, kp)):
                for ch in range(NCH):
                    cw = chw(ch)
                    cols = slice(ch * 512, ch * 512 + cw)
                    bank = pp[pi % 2]
                    slot = pi % 2
                    pi += 1
                    w = [(s_wl, wl_prev), prev[slot]]
                    if ln_first:
                        w.append((s_dv, ln_dv))
                    w.append((s_lg, all_lg))
                    for c in range(FC):
                        sch.call("tensor", "matmul", bank[:, 0:cw],
                                 wq3[:, c * 384 + which * 128: c * 384 + (which + 1) * 128],
                                 xsh(c, cols), start=(c == 0), stop=(c == FC - 1),
                                 wait=(w if c == 0 else []),
                                 inc=(s_pe if c == FC - 1 else None))
                    pe = sch.val(s_pe)
                    ev = sch.call("scalar", "activation", dst[:, cols], bank[:, 0:cw],
                                  AF.Copy, wait=[(s_pe, pe)], inc=s_ev)
                    prev[slot] = (s_ev, ev)
            for tt in range(18):
                tcols = slice(tt * 128, (tt + 1) * 128)
                bank = pp[pi % 2]
                slot = pi % 2
                pi += 1
                w = [(s_wl, wl_prev), prev[slot]]
                if ln_first:
                    w.append((s_dv, ln_dv))
                for c in range(FC):
                    sch.call("tensor", "matmul", bank[:, 0:128],
                             xsh(c, tcols),
                             wq3[:, c * 384 + 256: c * 384 + 384],
                             start=(c == 0), stop=(c == FC - 1),
                             wait=(w if c == 0 else []),
                             inc=(s_pe if c == FC - 1 else None))
                pe = sch.val(s_pe)
                ev = sch.call("scalar", "activation", vp[:, tt * 128:(tt + 1) * 128],
                              bank[:, 0:128], AF.Copy, wait=[(s_pe, pe)], inc=s_ev)
                prev[slot] = (s_ev, ev)
            pk_ev = sch.val(s_ev)
            for (src_, dst_, tA, tB) in ((qp, qf, ta, tb_), (kp, kf, tc_, td_)):
                sq_ev = sch.call("scalar", "activation", sq2[:, 0:TK], src_[:, 0:TK],
                                 AF.Square, wait=[(s_ev, pk_ev), (s_dv, sch.val(s_dv))],
                                 inc=s_ev)
                for ch in range(NCH):
                    cw = chw(ch)
                    cols = slice(ch * 512, ch * 512 + cw)
                    for hh in range(2):
                        pe = sch.call("tensor", "matmul", ps_g[0:1, 0:cw],
                                      onescb[hh * 64:(hh + 1) * 64, 0:1],
                                      sq2[hh * 64:(hh + 1) * 64, cols],
                                      start=True, stop=True,
                                      tile_position=(hh * 64, 0),
                                      wait=[(s_ev, sq_ev), (s_dv, sch.val(s_dv))],
                                      inc=s_pe)
                        pe_ = sch.val(s_pe)
                        rrow = nr[0:1, hh * 512: hh * 512 + cw]
                        trow = nr[0:1, 1024 + hh * 512: 1024 + hh * 512 + cw]
                        e2 = sch.call("scalar", "activation", trow, ps_g[0:1, 0:cw],
                                      AF.Sqrt, wait=[(s_pe, pe_)], inc=s_ev)
                        d2 = sch.call("vector", "tensor_scalar_max", trow, trow,
                                      1e-12, wait=[(s_ev, e2)], inc=s_dv)
                        d3 = sch.call("vector", "reciprocal", rrow, trow,
                                      wait=[(s_dv, d2)], inc=s_dv)
                        sch.call("tensor", "matmul",
                                 ps_f[hh * 64:(hh + 1) * 64, 0:cw],
                                 ones1f[0:1, 0:64], rrow, start=True, stop=True,
                                 tile_position=(0, hh * 64),
                                 wait=[(s_dv, d3)],
                                 inc=(s_pe if hh == 1 else None))
                    pe_b = sch.val(s_pe)
                    pe_r = sch.call("tensor", "matmul", ps_e[:, 0:cw], rot[:],
                                    src_[:, cols], start=True, stop=True,
                                    wait=[(s_dv, sch.val(s_dv))], inc=s_pe)
                    d1 = sch.call("vector", "tensor_mul", sq2[:, cols], src_[:, cols],
                                  tA[:, cols], wait=[(s_pe, pe_r)], inc=s_dv)
                    d2 = sch.call("vector", "tensor_tensor", dst_[:, cols],
                                  ps_e[:, 0:cw], tB[:, cols], ALU.mult,
                                  wait=[(s_dv, d1)], inc=s_dv)
                    d3 = sch.call("vector", "tensor_add", dst_[:, cols], dst_[:, cols],
                                  sq2[:, cols], wait=[(s_dv, d2)], inc=s_dv)
                    d4 = sch.call("vector", "tensor_mul", dst_[:, cols], dst_[:, cols],
                                  ps_f[:, 0:cw], wait=[(s_dv, d3), (s_pe, pe_b)],
                                  inc=s_dv)
            rot_dv = sch.val(s_dv)
            for w8 in range(8):
                q0 = 256 + w8 * 256
                k0 = w8 * 256
                for hh in range(2):
                    zinit = False
                    for kt4 in range(4):
                        bank = [ps_c, ps_d][si2 := (w8 * 8 + hh * 4 + kt4) % 2]
                        kcol = k0 + kt4 * 128
                        w = [sprev[si2], (s_dv, rot_dv)]
                        sch.call("tensor", "matmul", bank[:, 0:256],
                                 kf[hh * 64:(hh + 1) * 64, kcol:kcol + 128],
                                 qf[hh * 64:(hh + 1) * 64, q0:q0 + 256],
                                 start=True, stop=False,
                                 tile_position=(hh * 64, 0), wait=w)
                        pe_sc = sch.call("tensor", "matmul", bank[:, 0:256],
                                         ident[:], msk[:, kt4 * 256:(kt4 + 1) * 256],
                                         start=False, stop=True, inc=s_pe)
                        eslot = (w8 * 8 + hh * 4 + kt4) % 4
                        et = expt[:, eslot * 256:(eslot + 1) * 256]
                        ev = sch.call("scalar", "activation", et, bank[:, 0:256],
                                      AF.Exp, scale=8.0,
                                      bias=kvi[:, w8 * 4 + kt4:w8 * 4 + kt4 + 1],
                                      wait=[(s_pe, pe_sc), (s_pe, e_cons[eslot]),
                                            (s_dv, e_zdv[eslot])], inc=s_ev)
                        sprev[si2] = (s_ev, ev)
                        zsl = zca[:, hh * 256:(hh + 1) * 256]
                        if kt4 == 0:
                            zdv = sch.call("vector", "tensor_copy", zsl, et,
                                           wait=[(s_ev, ev), (s_pe, prev_pz)],
                                           inc=s_dv)
                        else:
                            zdv = sch.call("vector", "tensor_add", zsl, zsl, et,
                                           wait=[(s_ev, ev)], inc=s_dv)
                        e_zdv[eslot] = zdv
                        avw = [(s_ev, ev)]
                        if kt4 == 0:
                            avw.append((s_dv, prev_ctx_dv))
                        e_cons[eslot] = sch.call(
                            "tensor", "matmul",
                            [ps_e, ps_f][hh][hh * 64:(hh + 1) * 64, 0:256],
                            vp[:, (w8 + kt4 // 2) * 128 * 2:][:, 0:0] if False else
                            vp[:, (w8 * 2 + kt4) * 64 + hh * 64:][:, 0:0] if False else
                            vp[:, k0 // 128 * 128 + kt4 * 128: k0 // 128 * 128 + kt4 * 128][:, 0:0] if False else
                            vp[:, (k0 + kt4 * 128) // 128 * 128 + hh * 64:
                                (k0 + kt4 * 128) // 128 * 128 + hh * 64 + 64],
                            et, start=(kt4 == 0), stop=(kt4 == 3),
                            tile_position=(0, hh * 64),
                            wait=avw, inc=s_pe)
                    av_pe = sch.val(s_pe)
                    pz = sch.call("tensor", "matmul", ps_g[0:1, 0:256], onescf[:, 0:1],
                                  zca[:, hh * 256:(hh + 1) * 256], start=True,
                                  stop=True, wait=[(s_dv, e_zdv[(w8 * 8 + hh * 4 + 3) % 4])],
                                  inc=s_pe)
                    prev_pz = pz
                    d1 = sch.call("vector", "reciprocal", rzr[0:1, hh * 256:(hh + 1) * 256],
                                  ps_g[0:1, 0:256], wait=[(s_pe, pz)], inc=s_dv)
                    bp = sch.call("tensor", "matmul",
                                  ps_h[hh * 64:(hh + 1) * 64, 0:256],
                                  ones1f[0:1, 0:64],
                                  rzr[0:1, hh * 256:(hh + 1) * 256],
                                  start=True, stop=True, tile_position=(0, hh * 64),
                                  wait=[(s_dv, d1), (s_ev, prev_rzb)], inc=s_pe)
                    evz = sch.call("scalar", "activation", rzb[hh * 64:(hh + 1) * 64, 0:256],
                                   ps_h[hh * 64:(hh + 1) * 64, 0:256], AF.Copy,
                                   wait=[(s_pe, bp)], inc=s_ev)
                    prev_rzb = evz
                    ctx_dv = sch.call(
                        "vector", "tensor_mul",
                        cxs[:, hp * TOK + w8 * 256: hp * TOK + w8 * 256 + 256][hh * 64:(hh + 1) * 64, :],
                        [ps_e, ps_f][hh][hh * 64:(hh + 1) * 64, 0:256],
                        rzb[hh * 64:(hh + 1) * 64, 0:256],
                        wait=[(s_ev, evz), (s_pe, av_pe)], inc=s_dv)
                    prev_ctx_dv = ctx_dv
            hp_pe = sch.val(s_pe)
        ctx_all_dv = sch.val(s_dv)

        oprev = [(s_dv, None), (s_dv, None)]
        prev_st = None
        for qc in range(4):
            qcols = slice(qc * 512, (qc + 1) * 512)
            if not ln_first:
                for c in range(FC):
                    sch.dma("sync", xs[:, c * 512:(c + 1) * 512],
                            xr[c * 128:(c + 1) * 128, qcols], s_xs,
                            wait=([(s_st, prev_st)] if c == 0 else []))
            xs_ld = sch.val(s_xs)
            for fo in range(FC):
                bank = pp[pi % 2]
                slot = pi % 2
                pi += 1
                w = [(s_dv, ctx_all_dv), oprev[slot]]
                for c in range(FC):
                    sch.call("tensor", "matmul", bank[:, 0:512],
                             wo[:, c * D + fo * 128: c * D + fo * 128 + 128],
                             cxs[:, c * TOK:(c + 1) * TOK][:, qcols],
                             start=(c == 0), stop=(c == FC - 1),
                             wait=(w if c == 0 else []),
                             inc=(s_pe if c == FC - 1 else None))
                pe = sch.val(s_pe)
                if ln_first:
                    xres = xb[:, fo * TV + 512:(fo * TV) + 512 + TOK][:, qcols]
                    d2 = sch.call("vector", "tensor_add", utm[:, 0:512],
                                  bank[:, 0:512], xres,
                                  wait=[(s_pe, pe), (s_st, prev_st)], inc=s_dv)
                    oprev[slot] = (s_dv, d2)
                    prev_st = sch.dma("sync", xo[fo * 128:(fo + 1) * 128, qcols],
                                      utm[:, 0:512], s_st, wait=[(s_dv, d2)])
                    if io is not None and "col_hook" in io and qc == 3:
                        prev_st = io["col_hook"](sch, fo, utm[:, 511:512])
                else:
                    xsl = xs[:, fo * 512:(fo + 1) * 512]
                    d2 = sch.call("vector", "tensor_add", xsl, bank[:, 0:512], xsl,
                                  wait=[(s_pe, pe), (s_xs, xs_ld)], inc=s_dv)
                    oprev[slot] = (s_dv, d2)
                    prev_st = sch.dma("sync", xo[fo * 128:(fo + 1) * 128, qcols],
                                      xsl, s_st, wait=[(s_dv, d2)])
                    if io is not None and "col_hook" in io and qc == 3:
                        prev_st = io["col_hook"](sch, fo, xsl[:, 511:512])
        sch.barrier_wait("sync", s_st, sch.val(s_st))
    return E.finish()


def build_l1f_ffn(final=False, mega=None, io=None, pfx="l1ff_"):
    E = BlockEnv(mega, pfx)
    nc = E.nc
    TVF = TOK + 1
    if mega is None:
        xv = nc.dram_tensor("xv", [D, TVF], F32, kind="ExternalInput").ap()
        if final:
            ym = nc.dram_tensor("ym", [D, TOK], F32, kind="ExternalInput").ap()
        xo = nc.dram_tensor("xo", [D, TOK], F32, kind="ExternalOutput").ap()
    else:
        xv = io.get("xv"); xo = io["xo"]
        ym = io.get("ym")
    w1p_in = E.dram_in("w1p_in", [2, D, DI2], F32)
    b1r_in = E.dram_in("b1r_in", [1, 2 * DI2], BF16)
    w2p_in = E.dram_in("w2p_in", [DI2, D], F32)
    b2c_in = E.dram_in("b2c_in", [128, FC], F32)
    lng_in = E.dram_in("lng_in", [128, FC], F32)
    lnb_in = E.dram_in("lnb_in", [128, FC], F32)
    onesb_in = E.dram_in("onesb_in", [1, 128], BF16)
    onesf_in = E.dram_in("onesf_in", [1, 128], F32)
    onescb_in = E.dram_in("onescb_in", [128, 1], BF16)
    onest_in = E.dram_in("onest_in", [1, 512], BF16)
    epsc_in = E.dram_in("epsc_in", [1, 1], F32)

    if True:
        _e = E.ctx.enter_context
        xb = E.sb("xb", [128, FC * TVF], BF16)
        w1 = E.sb("w1", [128, FC * DI2], BF16)
        w2 = E.sb("w2", [128, FCA * D], BF16)
        ab = E.sb("ab", [128, FCA * 512], BF16)
        gst = E.sb("gst", [128, 2 * 512], BF16)
        b1r = E.sb("b1r", [1, 2 * DI2], BF16)
        b2c = E.sb("b2c", [128, FC], F32)
        lng = E.sb("lng", [128, FC], F32)
        lnb = E.sb("lnb", [128, FC], F32)
        xs = E.sb("xs", [128, FC * 512], F32)
        xs2 = E.sb("xs2", [128, FC * 512], F32) if final else None
        utm = E.sb("utm", [128, 512], F32)
        onesb = E.sb("onesb", [1, 128], BF16)
        ones1f = E.sb("ones1f", [1, 128], F32)
        onescb = E.sb("onescb", [128, 1], BF16)
        onest = E.sb("onest", [1, 512], BF16)
        epsc = E.sb("epsc", [1, 1], F32)
        stat = E.sb("stat", [1, 512], F32)
        stat2 = E.sb("stat2", [1, 512], F32)
        stat3 = E.sb("stat3", [1, 512], F32)
        xsq = E.sb("xsq", [128, 2 * 512], BF16)
        ps_a = E.ps()
        ps_b = E.ps()
        ps_c = E.ps()
        ps_d = E.ps()
        s_ld = E.sem("s_ld")
        s_lg = E.sem("s_lg")
        s_w1 = E.sem("s_w1")
        s_w2 = E.sem("s_w2")
        s_pe = E.sem("s_pe")
        s_ev = E.sem("s_ev")
        s_dv = E.sem("s_dv")
        s_xs = E.sem("s_xs")
        s_st = E.sem("s_st")
        sch = E.sch
        E.start()

        for (dst, src) in ((onesb, onesb_in), (ones1f, onesf_in), (onescb, onescb_in),
                           (onest, onest_in), (epsc, epsc_in), (b2c, b2c_in),
                           (lng, lng_in), (lnb, lnb_in)):
            sch.dma("sync", dst[:], src[:, :], s_ld)
        sch.dma("gpsimd", b1r[:], b1r_in[:, :], s_lg)
        for c in range(FC):
            if io is not None and "xv_load" in io:
                io["xv_load"](sch, c, xb[:, c * TVF:(c + 1) * TVF], s_lg)
            else:
                sch.dma("gpsimd", xb[:, c * TVF:(c + 1) * TVF],
                        xv[c * 128:(c + 1) * 128, :], s_lg)
        sch.dma("gpsimd", w2[:, :].rearrange("p (c f) -> p c f", c=FCA),
                w2p_in[:, :].rearrange("(c p) f -> p c f", p=128), s_w2)
        w2_ld = sch.val(s_w2)
        all_ld = sch.val(s_ld)
        all_lg = sch.val(s_lg)

        def chunk_shift(c):
            off = 1 if c < 4 else 0
            return xb[:, c * TVF + off: c * TVF + off + TOK]

        env = {"xn": xb, "onescb": onescb, "ones1f": ones1f, "epsc": epsc,
               "stat": stat, "stat2": stat2, "stat3": stat3, "xsq": xsq,
               "ps_sum": ps_a, "ps_sq": ps_b, "ps_ba": ps_c, "ps_bb": ps_d,
               "s_pe": s_pe, "s_ev": s_ev, "s_dv": s_dv,
               "xn_ready": [(s_ld, all_ld), (s_lg, all_lg)],
               "chunk_fn": chunk_shift}
        ln_dv = emit_ln_fm(sch, env, TOK)
        aff_dv = None
        for c in range(FC):
            aff_dv = sch.call("vector", "tensor_scalar", chunk_shift(c), chunk_shift(c),
                              lng[:, c:c + 1], lnb[:, c:c + 1], ALU.mult, ALU.add,
                              wait=([(s_dv, ln_dv), (s_ld, all_ld)] if c == 0 else []),
                              inc=s_dv)

        pp = [ps_a, ps_b]
        pi = 0
        prev = [(s_ev, None), (s_ev, None)]
        w1_pe = None
        prev_st = None
        g_dv = [None, None]
        for tc in range(4):
            cols = slice(tc * 512, (tc + 1) * 512)
            for side in range(2):
                w1_ld = sch.dma("gpsimd", w1[:, :].rearrange("p (c f) -> p c f", c=FC),
                                w1p_in[side, :, :].rearrange("(c p) f -> p c f", p=128),
                                s_w1, wait=[(s_pe, w1_pe)])
                for gf in range(FCA):
                    bank = pp[pi % 2]
                    slot = pi % 2
                    pi += 1
                    w = [(s_w1, w1_ld), prev[slot], (s_dv, aff_dv)]
                    for c in range(FC):
                        sch.call("tensor", "matmul", bank[:, 0:512],
                                 w1[:, c * DI2 + gf * 128: c * DI2 + gf * 128 + 128],
                                 chunk_shift(c)[:, cols], start=(c == 0), stop=False,
                                 wait=(w if c == 0 else []))
                    sch.call("tensor", "matmul", bank[:, 0:512],
                             b1r[0:1, side * DI2 + gf * 128: side * DI2 + gf * 128 + 128],
                             onest[0:1, :], start=False, stop=True, inc=s_pe)
                    pe = sch.val(s_pe)
                    if side == 0:
                        ev = sch.call("scalar", "activation",
                                      ab[:, gf * 512:(gf + 1) * 512], bank[:, 0:512],
                                      AF.Copy, wait=[(s_pe, pe)], inc=s_ev)
                        prev[slot] = (s_ev, ev)
                    else:
                        gslot = gf % 2
                        ev = sch.call("scalar", "activation",
                                      gst[:, gslot * 512:(gslot + 1) * 512],
                                      bank[:, 0:512], AF.Gelu,
                                      wait=[(s_pe, pe), (s_dv, g_dv[gslot])], inc=s_ev)
                        d = sch.call("vector", "tensor_mul",
                                     ab[:, gf * 512:(gf + 1) * 512],
                                     ab[:, gf * 512:(gf + 1) * 512],
                                     gst[:, gslot * 512:(gslot + 1) * 512],
                                     wait=[(s_ev, ev)], inc=s_dv)
                        g_dv[gslot] = d
                        prev[slot] = (s_dv, d)
                w1_pe = sch.val(s_pe)
            act_dv = sch.val(s_dv)
            for c in range(FC):
                if io is not None and "xvr" in io:
                    xsrc = io["xvr"][c * 128:(c + 1) * 128, tc * 512:(tc + 1) * 512]
                else:
                    xsrc = xv[c * 128:(c + 1) * 128, 1 + tc * 512:1 + (tc + 1) * 512]
                sch.dma("sync", xs[:, c * 512:(c + 1) * 512], xsrc,
                        s_xs, wait=([(s_st, prev_st)] if c == 0 else []))
                if final:
                    sch.dma("sync", xs2[:, c * 512:(c + 1) * 512],
                            ym[c * 128:(c + 1) * 128, cols], s_xs)
            xs_ld = sch.val(s_xs)
            for fo in range(FC):
                bank = pp[pi % 2]
                slot = pi % 2
                pi += 1
                w = [(s_dv, act_dv), prev[slot], (s_w2, w2_ld)]
                for c in range(FCA):
                    sch.call("tensor", "matmul", bank[:, 0:512],
                             w2[:, c * D + fo * 128: c * D + fo * 128 + 128],
                             ab[:, c * 512:(c + 1) * 512],
                             start=(c == 0), stop=(c == FCA - 1),
                             wait=(w if c == 0 else []),
                             inc=(s_pe if c == FCA - 1 else None))
                pe = sch.val(s_pe)
                d1 = sch.call("vector", "tensor_scalar", utm[:, 0:512],
                              bank[:, 0:512], b2c[:, fo:fo + 1], None, ALU.add,
                              wait=[(s_pe, pe)] + ([(s_xs, xs_ld)] if fo == 0 else []),
                              inc=s_dv)
                prev[slot] = (s_dv, d1)
                xsl = xs[:, fo * 512:(fo + 1) * 512]
                d2 = sch.call("vector", "tensor_add", xsl, utm[:, 0:512], xsl,
                              wait=[(s_dv, d1)], inc=s_dv)
                if final:
                    d2 = sch.call("vector", "tensor_add", xsl, xsl,
                                  xs2[:, fo * 512:(fo + 1) * 512],
                                  wait=[(s_dv, d2)], inc=s_dv)
                sch.dma("sync", xo[fo * 128:(fo + 1) * 128, cols], xsl, s_st,
                        wait=[(s_dv, d2)])
            prev_st = sch.val(s_st)
        sch.barrier_wait("sync", s_st, sch.val(s_st))
    return E.finish()


def host_l1f(Wqkv, q_scale, k_scale, Wo, ln_g, ln_b, W1, b1, W2, b2):
    DEPTH = 6
    HEADS = 16
    DH = 64
    WINDOW = 256
    DIM_INNER = 2730
    Wqkv = np.asarray(Wqkv, np.float32)
    Wo = np.asarray(Wo, np.float32)
    W1 = np.asarray(W1, np.float32)
    W2 = np.asarray(W2, np.float32)
    b1 = np.asarray(b1, np.float32)
    b2 = np.asarray(b2, np.float32)
    ln_g = np.asarray(ln_g, np.float32)
    ln_b = np.asarray(ln_b, np.float32)
    q_scale = np.asarray(q_scale, np.float32)
    k_scale = np.asarray(k_scale, np.float32)
    R = np.zeros((64, 64), np.float32)
    for d_ in range(32):
        R[d_, 32 + d_] = -1.0
        R[32 + d_, d_] = 1.0
    R2 = np.zeros((128, 128), np.float32)
    R2[0:64, 0:64] = R
    R2[64:128, 64:128] = R
    R2T = np.ascontiguousarray(R2.T).astype(BF)
    i_ = np.arange(256)[None, :]
    j_ = np.arange(128)[:, None]
    masks = []
    for kt4 in range(4):
        jj = kt4 * 128 + j_
        if kt4 < 2:
            keep = (i_ <= jj)
        else:
            keep = (i_ >= (jj - 256))
        masks.append(np.where(keep, 0.0, NEG))
    wmask = np.stack(masks).astype(BF)

    def rotary_tables(n=4096):
        half = np.arange(0, DH, 2, dtype=np.float32)
        inv_freq = 1.0 / 10000.0 ** (half / DH)
        t = np.arange(n, dtype=np.float32)
        freqs = t[:, None] * inv_freq[None, :]
        freqs = np.concatenate([freqs, freqs], -1)
        base_scale = (half + 0.4 * DH) / (1.4 * DH)
        power = (t - n // 2) / (WINDOW // 2)
        scale = base_scale[None, :] ** power[:, None]
        scale = np.concatenate([scale, scale], -1)
        return freqs, scale
    freqs, scale = rotary_tables()
    cosS = (np.cos(freqs) * scale).T
    sinS = (np.sin(freqs) * scale).T
    cosI = (np.cos(freqs) / scale).T
    sinI = (np.sin(freqs) / scale).T
    sigma = np.concatenate([np.arange(32, 64), np.arange(0, 32)])

    consts = {
        "rotT": R2T, "wmask": wmask,
        "onesb_in": _ones_row(BF), "onesf_in": _ones_row(np.float32),
        "onescf_in": np.ones((128, 1), np.float32),
        "onescb_in": np.ones((128, 1), BF),
        "ident_in": _ident128(),
        "epsc_in": np.full((1, 1), EPS, np.float32),
        "mum_in": np.full((1, 1), 1e-12, np.float32),
    }
    ffn_consts = {
        "onesb_in": _ones_row(BF), "onesf_in": _ones_row(np.float32),
        "onescb_in": np.ones((128, 1), BF),
        "onest_in": np.ones((1, 512), BF),
        "epsc_in": np.full((1, 1), EPS, np.float32),
    }

    def attn_layer_inputs(i, core):
        h2 = core % 2
        own0 = h2 * TOK
        trange = np.arange(own0 - 256, own0 + TOK)
        trange = np.clip(trange, 0, S - 1)
        m = dict(consts)
        Wq = Wqkv[i][:, 0:1024]
        Wk = Wqkv[i][:, 1024:2048]
        Wv = Wqkv[i][:, 2048:3072]
        blocks = []
        for hp in range(8):
            sl = slice(hp * 128, (hp + 1) * 128)
            blocks += [Wq[:, sl], Wk[:, sl], Wv[:, sl]]
        m["wqkv"] = np.ascontiguousarray(np.concatenate(blocks, axis=1))
        m["woT"] = np.ascontiguousarray(Wo[i])
        m["qs2"] = np.stack([np.tile(q_scale[i], 2), np.tile(q_scale[i][sigma], 2)], 1).copy()
        m["ks2"] = np.stack([np.tile(k_scale[i], 2), np.tile(k_scale[i][sigma], 2)], 1).copy()
        m["cst"] = np.ascontiguousarray(np.tile(cosS[:, trange], (2, 1)))
        m["snt"] = np.ascontiguousarray(np.tile(sinS[:, trange], (2, 1)))
        m["csti"] = np.ascontiguousarray(np.tile(cosI[:, trange], (2, 1)))
        m["snti"] = np.ascontiguousarray(np.tile(sinI[:, trange], (2, 1)))
        kv = np.zeros((128, 32), np.float32)
        if h2 == 0:
            kv[:, 0] = NEG
            kv[:, 1] = NEG
        m["kvis"] = kv
        return m

    def ffn_layer_inputs(i):
        m = dict(ffn_consts)
        a = W1[i][:, 0:DIM_INNER]
        g = W1[i][:, DIM_INNER:2 * DIM_INNER]
        pad = np.zeros((D, DI2 - DIM_INNER), np.float32)
        m["w1p_in"] = np.ascontiguousarray(np.stack([np.concatenate([a, pad], 1),
                                                     np.concatenate([g, pad], 1)]))
        b1a = np.concatenate([b1[i][0:DIM_INNER], np.zeros(DI2 - DIM_INNER, np.float32)])
        b1g = np.concatenate([b1[i][DIM_INNER:], np.zeros(DI2 - DIM_INNER, np.float32)])
        m["b1r_in"] = np.concatenate([b1a, b1g]).reshape(1, -1).astype(BF)
        m["w2p_in"] = np.ascontiguousarray(
            np.concatenate([W2[i], np.zeros((DI2 - DIM_INNER, D), np.float32)], 0))
        m["b2c_in"] = chunk_cols(b2[i])
        m["lng_in"] = chunk_cols(ln_g[i])
        m["lnb_in"] = chunk_cols(ln_b[i])
        return m
    return attn_layer_inputs, ffn_layer_inputs


def build_mega(upto=None):
    mega = Mega()
    nc = mega.nc
    sch = mega.sch
    out = nc.dram_tensor("out", [D, TOK], F32, kind="ExternalOutput").ap()
    cprm = nc.dram_tensor("cprm", [1, 64], mybir.dt.int32, kind="ExternalInput").ap()
    zcol = nc.dram_tensor("zcol", [D, 512], F32, kind="ExternalInput").ap()
    xc0 = mega.scratch("xc0", [D, TOK], F32)
    xc1 = mega.scratch("xc1", [D, TOK], F32)
    yo = mega.scratch("yo", [D, TOK], F32)
    co2 = mega.scratch("co2", [512, S], F32)
    agx = mega.scratch("agx", [2, D, TOK], F32)
    agc = mega.scratch("agc", [2, 512, S], F32)
    cna = mega.scratch("cna", [D, 1024], F32)
    aga = mega.scratch("aga", [2, D, 1024], F32)
    cnf = mega.scratch("cnf", [2, D], F32)
    agf = mega.scratch("agf", [2, 2, D], F32)
    ci_loc = mega.scratch("ci_loc", [D, TOK], F32)
    halo_loc = mega.scratch("halo_loc", [D, 512], F32)
    halo1_loc = mega.scratch("halo1_loc", [1, D], F32)
    xf_loc = mega.scratch("xf_loc", [D, S], F32)
    cprs = mega.ctx.enter_context(nc.sbuf_tensor("cprs", [1, 64], mybir.dt.int32))
    s_gl = mega.sem("s_gl")
    s_cc = mega.sem("s_cc")
    RG = [[0, 1], [2, 3], [4, 5], [6, 7]]
    regs = {}

    sch.dma("gpsimd", cprs[:], cprm[:, :], s_gl)
    sch.dma("gpsimd", cna[:, 512:1024], zcol[:, :], s_gl)
    sch.dma("gpsimd", cnf[1:2, 0:512], zcol[0:1, :], s_gl)
    sch.dma("gpsimd", cnf[1:2, 512:1024], zcol[1:2, :], s_gl)
    cpr_ld = sch.val(s_gl)

    def dyn_stage(k, dst, src_t, ap_dims, wait=()):
        def fn(g):
            rv = g.value_load(cprs[0:1, k:k + 1])
            src = AP(tensor=src_t.tensor, offset=rv, ap=ap_dims)
            ins = g.dma_start(out=dst, in_=src)
            g.free_register(rv.val)
            return ins
        return sch.raw("gpsimd", fn, wait=list(wait) + [(s_gl, cpr_ld)],
                       inc=s_gl, inc_n=16)

    def ag(src, dst, wait=()):
        def fn(g):
            return g.collective_compute("AllGather", mybir.AluOpType.bypass,
                                        RG, [src.opt()], [dst.opt()])
        return sch.raw("gpsimd", fn, wait=wait, inc=s_cc, inc_n=1)

    def col_hook(sch_, fo, tile):
        return sch_.dma("sync",
                        cnf[0:1, fo * 128:(fo + 1) * 128].rearrange("a b -> b a"),
                        tile, mega.sem("s_st"))

    def _dbg(state):
        v = sch.dma("gpsimd", out[:, :], state[:, :], s_gl)
        sch.barrier_wait("sync", s_gl, v)
        with nc.Block() as block:
            sch.run(block)
        mega.ctx.close()
        return nc

    l0a_xf = nc.dram_tensor("l0a_xf", [D, S], F32, kind="ExternalInput").ap()
    l0a_xr = nc.dram_tensor("l0a_xo_in", [D, TOK], F32, kind="ExternalInput").ap()
    build_l0_attn(mega=mega, io={"xf": l0a_xf, "xo_in": l0a_xr, "xo": xc0})
    if upto == "l0a":
        return _dbg(xc0)
    build_l0_ffn(mega=mega, io={"xi": xc0, "xo": xc1})
    if upto == "l0f":
        return _dbg(xc1)

    ag(xc1, agx)
    agx_cc = sch.val(s_cc)
    if upto in ("agx0", "agx1"):
        u = int(upto[-1])
        v = sch.dma("gpsimd", out[:, :], agx[u, :, :], s_gl,
                    wait=[(s_cc, agx_cc)])
        sch.barrier_wait("sync", s_gl, v)
        with nc.Block() as block:
            sch.run(block)
        mega.ctx.close()
        return nc

    def xf_interleave(sch_, c, dst, sem):
        sch_.dma("gpsimd", dst.rearrange("p (u v) -> p u v", u=2),
                 agx[:, c * 128:(c + 1) * 128, :].rearrange("u p v -> p u v"),
                 sem, wait=[(s_cc, agx_cc)])

    build_l1_attn(mega=mega, pfx="l1a0_",
                  io={"xf_load": xf_interleave, "co": co2[0:256, :]})
    build_l1_attn(mega=mega, pfx="l1a1_",
                  io={"xf_load": xf_interleave, "co": co2[256:512, :]})
    if upto == "co2":
        v = sch.dma("gpsimd", out[0:512, :], co2[:, 0:TOK], s_gl)
        v = sch.dma("gpsimd", out[512:1024, :], co2[:, TOK:S], s_gl)
        sch.barrier_wait("sync", s_gl, v)
        with nc.Block() as block:
            sch.run(block)
        mega.ctx.close()
        return nc
    ag(co2, agc)
    agc_cc = sch.val(s_cc)
    ci_stage = dyn_stage(0, ci_loc[:, :], agc, [[S, D], [1, TOK]],
                         wait=[(s_cc, agc_cc)])

    def ci_load(sch_, c, tb, dst, wait=()):
        v = sch_.dma("sync", dst,
                     ci_loc[c * 128:(c + 1) * 128, tb * 512:(tb + 1) * 512],
                     mega.sem("s_ld"),
                     wait=list(wait) + [(s_gl, ci_stage)])
        return (mega.sem("s_ld"), v)

    build_l1f_pre(mega=mega, io={"xi": xc1, "ci_load": ci_load, "yo": yo})
    if upto == "l1fp":
        return _dbg(yo)

    sA, tA, sF = yo, xc0, xc1
    for i in range(6):
        sch.dma("gpsimd", cna[:, 0:512], sA[:, TOK - 512:TOK], s_gl)
        ag(cna, aga, wait=[(s_gl, sch.val(s_gl))])
        a_cc = sch.val(s_cc)
        h_stage = dyn_stage(1, halo_loc[:, :], aga, [[1024, D], [1, 512]],
                            wait=[(s_cc, a_cc)])

        def xv_load(sch_, c, dst, sem, h_stage=h_stage, sA=sA):
            sch_.dma("gpsimd", dst[:, 0:512], halo_loc[c * 128:(c + 1) * 128, :],
                     sem, wait=[(s_gl, h_stage)])
            sch_.dma("gpsimd", dst[:, 512:TV], sA[c * 128:(c + 1) * 128, :], sem)

        build_l1f_attn(ln_first=(i == 0), mega=mega, pfx=f"l1fa{i}_",
                       io={"xv_load": xv_load, "xr": sA, "xo": tA,
                           "col_hook": col_hook})
        if upto == f"i{i}a":
            return _dbg(tA)
        ag(cnf, agf)
        f_cc = sch.val(s_cc)
        f_stage = dyn_stage(2, halo1_loc[0:1, :], agf, [[1, 1], [1, D]],
                            wait=[(s_cc, f_cc)])

        def xvf_load(sch_, c, dst, sem, f_stage=f_stage, tA=tA):
            sch_.dma("gpsimd", dst[:, 0:1],
                     halo1_loc[0:1, c * 128:(c + 1) * 128]
                     .rearrange("a b -> b a"), sem, wait=[(s_gl, f_stage)])
            sch_.dma("gpsimd", dst[:, 1:TOK + 1], tA[c * 128:(c + 1) * 128, :], sem)

        io_f = {"xv_load": xvf_load, "xvr": tA, "xo": sF}
        if i == 5:
            io_f["ym"] = yo
        build_l1f_ffn(final=(i == 5), mega=mega, pfx=f"l1ff{i}_", io=io_f)
        if upto == f"i{i}f":
            return _dbg(sF)
        sA, tA, sF = xc1, xc0, xc1

    ag(xc1, agx)
    x2_cc = sch.val(s_cc)
    dyn_stage(3, xf_loc[:, 0:TOK], agx, [[TOK, D], [1, TOK]],
              wait=[(s_cc, x2_cc)])
    x2_st = dyn_stage(4, xf_loc[:, TOK:S], agx, [[TOK, D], [1, TOK]])

    def xf_oo2(sch_, c, dst, sem, st=x2_st):
        sch_.dma("gpsimd", dst, xf_loc[c * 128:(c + 1) * 128, :], sem,
                 wait=[(s_gl, st)])

    build_l2_attn(mega=mega, io={"xf_load": xf_oo2, "xo_in": xc1, "xo": xc0,
                                 "col_hook": col_hook})
    if upto == "l2a":
        return _dbg(xc0)
    ag(cnf, agf)
    h2_cc = sch.val(s_cc)
    h2_st = dyn_stage(2, halo1_loc[0:1, :], agf, [[1, 1], [1, D]],
                      wait=[(s_cc, h2_cc)])

    def xh_load(sch_, c, dst, sem, st=h2_st):
        sch_.dma("gpsimd", dst,
                 halo1_loc[0:1, c * 128:(c + 1) * 128].rearrange("a b -> b a"),
                 sem, wait=[(s_gl, st)])

    build_l2_ffn(mega=mega, io={"xi": xc0, "xh_load": xh_load, "xo": xc1})
    if upto == "l2f":
        return _dbg(xc1)

    ag(xc1, agx)
    x3_cc = sch.val(s_cc)
    dyn_stage(3, xf_loc[:, 0:TOK], agx, [[TOK, D], [1, TOK]],
              wait=[(s_cc, x3_cc)])
    x3_st = dyn_stage(4, xf_loc[:, TOK:S], agx, [[TOK, D], [1, TOK]])

    def xf_oo3(sch_, c, dst, sem, st=x3_st):
        sch_.dma("gpsimd", dst, xf_loc[c * 128:(c + 1) * 128, :], sem,
                 wait=[(s_gl, st)])

    build_l3_attn(mega=mega, io={"xf_load": xf_oo3, "xo_in": xc1, "xo": xc0})
    if upto == "l3a":
        return _dbg(xc0)
    build_l3_ffn(mega=mega, io={"xi": xc0, "xo": out})
    with nc.Block() as block:
        sch.run(block)
    mega.ctx.close()
    return nc


MEGA_PHASES = ["l0a", "l0f", "l1a0", "l1a1", "l1fp",
               "i0a", "i0f", "i1a", "i1f", "i2a", "i2f",
               "i3a", "i3f", "i4a", "i4f", "i5a", "i5f",
               "l2a", "l2f", "l3a", "l3f"]


def mega_maps(x, inputs, upto=None):
    nlim = MEGA_PHASES.index(upto) + 1 if upto else len(MEGA_PHASES)
    ph = MEGA_PHASES[:nlim]
    maps = [dict() for _ in range(NC)]

    def put(pfx, core, m):
        for k, v in m.items():
            maps[core][pfx + k] = v

    for core, m in enumerate(host_l0_attn(x, inputs["L0_attn_W_qkv"],
                                          inputs["L0_attn_b_qkv"])):
        put("l0a_", core, m)
    if "l0f" in ph:
        sh = host_l0_ffn(inputs["L0_ffn_router_weight"], inputs["L0_ffn_gate_up_proj"],
                         inputs["L0_ffn_gate_up_proj_bias"], inputs["L0_ffn_down_proj"],
                         inputs["L0_ffn_down_proj_bias"])
        for core in range(NC):
            put("l0f_", core, sh)
    if "l1a0" in ph:
        per = host_l1_attn(inputs["L1_attn_wq"], inputs["L1_attn_wk"],
                           inputs["L1_attn_wv"], inputs["L1_attn_bq"],
                           inputs["L1_attn_bk"], inputs["L1_attn_bv"],
                           inputs["L1_attn_dist_emb"])
        for li in range(2):
            if f"l1a{li}" in ph:
                for core in range(NC):
                    put(f"l1a{li}_", core, per(core, li))
    if "i0a" in ph:
        att_in, ffn_in = host_l1f(inputs["L1_ffn_Wqkv"], inputs["L1_ffn_q_scale"],
                                  inputs["L1_ffn_k_scale"], inputs["L1_ffn_Wo"],
                                  inputs["L1_ffn_ln_g"], inputs["L1_ffn_ln_b"],
                                  inputs["L1_ffn_W1"], inputs["L1_ffn_b1"],
                                  inputs["L1_ffn_W2"], inputs["L1_ffn_b2"])
        for i in range(6):
            if f"i{i}a" in ph:
                for core in range(NC):
                    put(f"l1fa{i}_", core, att_in(i, core))
            if f"i{i}f" in ph:
                sh = ffn_in(i)
                for core in range(NC):
                    put(f"l1ff{i}_", core, sh)
    if "l2a" in ph:
        sh = host_l2_attn(inputs["L2_attn_w_qkv"], inputs["L2_attn_w_proj"],
                          inputs["L2_attn_b_proj"])
        for core in range(NC):
            put("l2a_", core, sh)
    if "l2f" in ph:
        sh = host_l2_ffn(inputs["L2_ffn_W_r"], inputs["L2_ffn_W_k"],
                         inputs["L2_ffn_W_v"], inputs["L2_ffn_mu_r"],
                         inputs["L2_ffn_mu_k"])
        for core in range(NC):
            put("l2f_", core, sh)
    if "l3a" in ph:
        sh = host_l3_attn(inputs["L3_attn_qkv_w"], inputs["L3_attn_qkv_b"],
                          inputs["L3_attn_proj_w"], inputs["L3_attn_proj_b"])
        for core in range(NC):
            put("l3a_", core, sh)
            h = core % 2
            kv = np.ones((128, NKT), np.float32)
            if h == 0:
                kv[:, 0:16] = 0.0
            maps[core]["l3a_kvis"] = kv
    if "l3f" in ph:
        sh = host_l3_ffn(inputs["L3_ffn_rw1"], inputs["L3_ffn_rb1"],
                         inputs["L3_ffn_rw2"], inputs["L3_ffn_rb2"],
                         inputs["L3_ffn_W1"], inputs["L3_ffn_b1"],
                         inputs["L3_ffn_W2"], inputs["L3_ffn_b2"])
        for core in range(NC):
            put("l3f_", core, sh)
    zc = np.zeros((D, 512), np.float32)
    for core in range(NC):
        h = core % 2
        t = np.zeros((1, 64), np.int32)
        t[0, 0] = h * TOK
        t[0, 1] = (1 - h) * 512
        t[0, 2] = (1 - h) * D
        t[0, 3] = (1 - h) * D * TOK
        t[0, 4] = h * D * TOK
        maps[core]["cprm"] = t
        maps[core]["zcol"] = zc
    return maps


def _run(nc, maps):
    return run_bass_kernel_spmd(nc, maps, core_ids=list(range(NC))).results


def kernel(**inputs):
    x = np.asarray(inputs["L0_attn_x"], np.float32)
    nc = build_mega()
    maps = mega_maps(x, inputs)
    res = _run(nc, maps)
    return assemble_fm(res, key="out")


def kernel_unfused(**inputs):
    x = np.asarray(inputs["L0_attn_x"], np.float32)

    res = _run(build_l0_attn(),
               host_l0_attn(x, inputs["L0_attn_W_qkv"], inputs["L0_attn_b_qkv"]))
    x = assemble_fm(res)

    shared = host_l0_ffn(inputs["L0_ffn_router_weight"], inputs["L0_ffn_gate_up_proj"],
                         inputs["L0_ffn_gate_up_proj_bias"], inputs["L0_ffn_down_proj"],
                         inputs["L0_ffn_down_proj_bias"])
    maps = []
    for core in range(NC):
        b, h = core // 2, core % 2
        m = dict(shared)
        m["xi"] = fmajor(x[b, h * TOK:(h + 1) * TOK])
        maps.append(m)
    x = assemble_fm(_run(build_l0_ffn(), maps))

    per = host_l1_attn(inputs["L1_attn_wq"], inputs["L1_attn_wk"], inputs["L1_attn_wv"],
                       inputs["L1_attn_bq"], inputs["L1_attn_bk"], inputs["L1_attn_bv"],
                       inputs["L1_attn_dist_emb"])
    nc1 = build_l1_attn()
    ctx = np.empty((B, S, D), np.float32)
    for li in range(2):
        maps = []
        for core in range(NC):
            b = core // 2
            m = per(core, li)
            m["xf"] = fmajor(x[b])
            maps.append(m)
        res = _run(nc1, maps)
        for core in range(NC):
            b, h2 = core // 2, core % 2
            c0 = (h2 * 8 + li * 4) * 64
            ctx[b, :, c0:c0 + 256] = np.asarray(res[core]["co"]).T

    maps = []
    for core in range(NC):
        b, h = core // 2, core % 2
        maps.append({"xi": fmajor(x[b, h * TOK:(h + 1) * TOK]),
                     "ci": fmajor(ctx[b, h * TOK:(h + 1) * TOK])})
    y = assemble_fm(_run(build_l1f_pre(), maps), key="yo")

    att_in, ffn_in = host_l1f(inputs["L1_ffn_Wqkv"], inputs["L1_ffn_q_scale"],
                              inputs["L1_ffn_k_scale"], inputs["L1_ffn_Wo"],
                              inputs["L1_ffn_ln_g"], inputs["L1_ffn_ln_b"],
                              inputs["L1_ffn_W1"], inputs["L1_ffn_b1"],
                              inputs["L1_ffn_W2"], inputs["L1_ffn_b2"])
    nc_a0 = build_l1f_attn(ln_first=True)
    nc_a = build_l1f_attn(ln_first=False)
    nc_f = build_l1f_ffn(final=False)
    nc_ff = build_l1f_ffn(final=True)
    state = y.copy()
    for i in range(6):
        maps = []
        for core in range(NC):
            b, h2 = core // 2, core % 2
            m = att_in(i, core)
            own0 = h2 * TOK
            xvv = np.zeros((TV, D), np.float32)
            lo = own0 - 512
            src0 = max(0, lo)
            xvv[src0 - lo:, :] = state[b, src0:own0 + TOK]
            m["xv"] = fmajor(xvv)
            m["xr"] = fmajor(state[b, own0:own0 + TOK])
            maps.append(m)
        res = _run(nc_a0 if i == 0 else nc_a, maps)
        for core in range(NC):
            b, h2 = core // 2, core % 2
            state[b, h2 * TOK:(h2 + 1) * TOK] = np.asarray(res[core]["xo"]).T
        maps = []
        for core in range(NC):
            b, h2 = core // 2, core % 2
            m = ffn_in(i)
            own0 = h2 * TOK
            xvv = np.zeros((TOK + 1, D), np.float32)
            if own0 > 0:
                xvv[0] = state[b, own0 - 1]
            xvv[1:] = state[b, own0:own0 + TOK]
            m["xv"] = fmajor(xvv)
            if i == 5:
                m["ym"] = fmajor(y[b, own0:own0 + TOK])
            maps.append(m)
        res = _run(nc_ff if i == 5 else nc_f, maps)
        for core in range(NC):
            b, h2 = core // 2, core % 2
            state[b, h2 * TOK:(h2 + 1) * TOK] = np.asarray(res[core]["xo"]).T
    x = state

    shared = host_l2_attn(inputs["L2_attn_w_qkv"], inputs["L2_attn_w_proj"],
                          inputs["L2_attn_b_proj"])
    maps = []
    for core in range(NC):
        b, h = core // 2, core % 2
        m = dict(shared)
        own = x[b, h * TOK:(h + 1) * TOK]
        other = x[b, (1 - h) * TOK:(2 - h) * TOK]
        m["xf"] = fmajor(np.concatenate([other, own], axis=0))
        m["xo_in"] = fmajor(own)
        maps.append(m)
    x = assemble_fm(_run(build_l2_attn(), maps))

    shared = host_l2_ffn(inputs["L2_ffn_W_r"], inputs["L2_ffn_W_k"], inputs["L2_ffn_W_v"],
                         inputs["L2_ffn_mu_r"], inputs["L2_ffn_mu_k"])
    maps = []
    for core in range(NC):
        b, h = core // 2, core % 2
        m = dict(shared)
        m["xi"] = fmajor(x[b, h * TOK:(h + 1) * TOK])
        halo = np.zeros((1, D), np.float32) if h == 0 else x[b, h * TOK - 1:h * TOK]
        m["xh"] = fmajor(halo)
        maps.append(m)
    x = assemble_fm(_run(build_l2_ffn(), maps))

    shared = host_l3_attn(inputs["L3_attn_qkv_w"], inputs["L3_attn_qkv_b"],
                          inputs["L3_attn_proj_w"], inputs["L3_attn_proj_b"])
    maps = []
    for core in range(NC):
        m = dict(shared)
        m.update(host_l3_attn_core(x, core))
        maps.append(m)
    x = assemble_fm(_run(build_l3_attn(), maps))

    shared = host_l3_ffn(inputs["L3_ffn_rw1"], inputs["L3_ffn_rb1"], inputs["L3_ffn_rw2"],
                         inputs["L3_ffn_rb2"], inputs["L3_ffn_W1"], inputs["L3_ffn_b1"],
                         inputs["L3_ffn_W2"], inputs["L3_ffn_b2"])
    maps = []
    for core in range(NC):
        b, h = core // 2, core % 2
        m = dict(shared)
        m["xi"] = fmajor(x[b, h * TOK:(h + 1) * TOK])
        maps.append(m)
    x = assemble_fm(_run(build_l3_ffn(), maps))
    return x


# version 08
# speedup vs baseline: 1.0523x; 1.0523x over previous
import math
from contextlib import ExitStack
import numpy as np
import ml_dtypes
import concourse.bass as bass
import concourse.mybir as mybir
from concourse.ap import AP
from concourse.bass_utils import run_bass_kernel_spmd

F32 = mybir.dt.float32
BF16 = mybir.dt.bfloat16
AF = mybir.ActivationFunctionType
ALU = mybir.AluOpType
AX = mybir.AxisListType
BF = ml_dtypes.bfloat16
NEG = -1.0e30

D = 1024
S = 4096
B = 4
TOK = 2048
NC = 8
FC = D // 128
EPS = 1e-5


class Sched:
    ENGINES = ("sync", "scalar", "tensor", "vector", "gpsimd")

    def __init__(self, nc):
        self.nc = nc
        self.ops = {e: [] for e in self.ENGINES}
        self.counts = {}
        self.chain = {}

    def enable_chain(self, **eng_sems):
        self.chain.update(eng_sems)

    def val(self, sem):
        return self.counts.get(id(sem), 0)

    def _bump(self, sem, n):
        v = self.counts.get(id(sem), 0) + n
        self.counts[id(sem)] = v
        return v

    def call(self, engine, method, *args, wait=(), inc=None, inc_n=1, **kwargs):
        wait = list(wait)
        cs = self.chain.get(engine)
        if cs is not None and method != "dma_start":
            wait.append((cs, self.val(cs)))
            if inc is None:
                inc = cs
            else:
                assert inc is cs, f"chained engine {engine} op with inc != chain sem"
        tgt = self._bump(inc, inc_n) if inc is not None else None
        waits = tuple((s, v) for (s, v) in wait if v is not None)

        def thunk(eng, method=method, args=args, kwargs=kwargs, waits=waits,
                  inc=inc, inc_n=inc_n):
            for (s, v) in waits:
                eng.wait_ge(s, v)
            ins = getattr(eng, method)(*args, **kwargs)
            if inc is not None:
                ins.then_inc(inc, inc_n)
        self.ops[engine].append(thunk)
        return tgt

    def dma(self, engine, out_ap, in_ap, sem, wait=()):
        return self.call(engine, "dma_start", wait=wait, inc=sem, inc_n=16,
                         out=out_ap, in_=in_ap)

    def raw(self, engine, fn, wait=(), inc=None, inc_n=1):
        waits = tuple((s, v) for (s, v) in wait if v is not None)
        tgt = self._bump(inc, inc_n) if inc is not None else None

        def thunk(eng, fn=fn, waits=waits, inc=inc, inc_n=inc_n):
            for (s, v) in waits:
                eng.wait_ge(s, v)
            ins = fn(eng)
            if inc is not None:
                ins.then_inc(inc, inc_n)
        self.ops[engine].append(thunk)
        return tgt

    def barrier_wait(self, engine, sem, val):
        def thunk(eng, sem=sem, val=val):
            eng.wait_ge(sem, val)
        self.ops[engine].append(thunk)

    def run(self, block):
        for e in self.ENGINES:
            def body(eng, ops=self.ops[e]):
                for f in ops:
                    f(eng)
            getattr(block, e)(body)






class Mega:

    def __init__(self):
        self.nc = bass.Bass("TRN2", target_bir_lowering=False)
        self.sch = Sched(self.nc)
        self.ctx = ExitStack()
        self._sems = {}
        self.psum = [self.ctx.enter_context(
            self.nc.psum_tensor(f"mps{i}", [128, 512], F32)) for i in range(8)]
        self.consts = {}
        self.inmap = {}

    def sem(self, name):
        if name not in self._sems:
            self._sems[name] = self.ctx.enter_context(self.nc.semaphore(name))
        return self._sems[name]

    def sems(self, *names):
        return [self.sem(n) for n in names]

    def barrier(self):
        vals = [(s, self.sch.val(s)) for s in self._sems.values()
                if self.sch.val(s) > 0]
        for e in Sched.ENGINES:
            for (s, v) in vals:
                self.sch.barrier_wait(e, s, v)

    def inp(self, name, shape, dtype, data_fn):
        ap = self.nc.dram_tensor(name, shape, dtype, kind="ExternalInput").ap()
        self.inmap[name] = data_fn
        return ap

    def scratch(self, name, shape, dtype):
        return self.nc.dram_tensor(name, shape, dtype, kind="Internal").ap()




class BlockEnv:

    def __init__(self, mega, pfx):
        self.mega = mega
        self.pfx = pfx
        if mega is None:
            self.nc = bass.Bass("TRN2", target_bir_lowering=False)
            self.sch = Sched(self.nc)
            self._psum_i = 0
        else:
            self.nc = mega.nc
            self.sch = mega.sch
            self._psum_i = 0
        self.ctx = ExitStack()
        self._local_sems = {}

    def nm(self, name):
        return name if self.mega is None else self.pfx + name

    def dram_in(self, name, shape, dtype):
        if self.mega is None:
            return self.nc.dram_tensor(name, shape, dtype, kind="ExternalInput").ap()
        return self.nc.dram_tensor(self.pfx + name, shape, dtype,
                                   kind="ExternalInput").ap()

    def dram_out(self, name, shape, dtype):
        if self.mega is None:
            return self.nc.dram_tensor(name, shape, dtype, kind="ExternalOutput").ap()
        return self.mega.scratch(self.pfx + name, shape, dtype)

    def dram_int(self, name, shape, dtype):
        nm = name if self.mega is None else self.pfx + name
        return self.nc.dram_tensor(nm, shape, dtype, kind="Internal").ap()

    def sb(self, name, shape, dtype):
        return self.ctx.enter_context(self.nc.sbuf_tensor(name if self.mega is None
                                                          else self.pfx + name,
                                                          shape, dtype))

    def ps(self):
        if self.mega is None:
            h = self.ctx.enter_context(
                self.nc.psum_tensor(f"ps{self._psum_i}", [128, 512], F32))
        else:
            h = self.mega.psum[self._psum_i]
        self._psum_i += 1
        return h

    def sem(self, name):
        if self.mega is not None:
            return self.mega.sem(name)
        if name not in self._local_sems:
            self._local_sems[name] = self.ctx.enter_context(self.nc.semaphore(name))
        return self._local_sems[name]

    def sem_list(self, name, n):
        return [self.sem(f"{name}{i}") for i in range(n)]

    def start(self):
        if self.mega is None:
            self.sch.enable_chain(vector=self.sem("s_dv"), scalar=self.sem("s_ev"))
        else:
            self.mega.sch.chain = {"vector": self.sem("s_dv"),
                                   "scalar": self.sem("s_ev")}

    def finish(self):
        if self.mega is None:
            with self.nc.Block() as block:
                self.sch.run(block)
            self.ctx.close()
            return self.nc
        self.ctx.close()
        self.mega.barrier()
        return None


def emit_ln_fm(sch, env, ntok):
    nc = sch.nc
    xn = env["xn"]
    ps_sum, ps_sq = env["ps_sum"], env["ps_sq"]
    ps_ba, ps_bb = env["ps_ba"], env["ps_bb"]
    s_pe, s_ev, s_dv = env["s_pe"], env["s_ev"], env["s_dv"]
    stat, stat2, stat3 = env["stat"], env["stat2"], env["stat3"]
    xsq = env["xsq"]
    first_wait = list(env.get("xn_ready", []))
    dv_apply_done = env.get("ln_prev_dv")
    prev_bcast_pe = env.get("ln_prev_bcast_pe")
    xsq_pe = env.get("ln_xsq_pe", [None, None])

    if "chunk_fn" in env:
        def xnc(c, cols, _f=env["chunk_fn"]):
            return _f(c)[:, cols]
    else:
        def xnc(c, cols):
            return xn[:, c * ntok:(c + 1) * ntok][:, cols]

    NTC = ntok // 512
    for tc in range(NTC):
        cols = slice(tc * 512, (tc + 1) * 512)
        for c in range(FC):
            slot = c % 2
            xq = xsq[:, slot * 512:slot * 512 + 512]
            w = list(first_wait) if tc == 0 and c == 0 else []
            w.append((s_pe, xsq_pe[slot]))
            ev = sch.call("scalar", "activation", xq, xnc(c, cols), AF.Square,
                          wait=w, inc=s_ev)
            sch.call("tensor", "matmul", ps_sum[0:1, 0:512], env["onescb"][:, 0:1],
                     xnc(c, cols), start=(c == 0), stop=(c == FC - 1),
                     wait=(list(first_wait) if tc == 0 and c == 0 else [])
                     + ([(s_dv, dv_apply_done)] if c == 0 else []))
            xsq_pe[slot] = sch.call("tensor", "matmul", ps_sq[0:1, 0:512],
                                    env["onescb"][:, 0:1], xq,
                                    start=(c == 0), stop=(c == FC - 1),
                                    wait=[(s_ev, ev)], inc=s_pe)
        pe_stats = sch.val(s_pe)
        r_mean = stat[0:1, :]
        r_var = stat2[0:1, :]
        dv = sch.call("vector", "tensor_scalar_mul", r_mean, ps_sum[0:1, 0:512],
                      1.0 / D, wait=[(s_pe, pe_stats), (s_pe, prev_bcast_pe)],
                      inc=s_dv)
        dv = sch.call("vector", "tensor_scalar_mul", r_var, ps_sq[0:1, 0:512],
                      1.0 / D, wait=[(s_dv, dv)], inc=s_dv)
        dv = sch.call("vector", "tensor_mul", stat3[0:1, :], r_mean, r_mean,
                      wait=[(s_dv, dv)], inc=s_dv)
        dv = sch.call("vector", "tensor_sub", r_var, r_var, stat3[0:1, :],
                      wait=[(s_dv, dv)], inc=s_dv)
        ev = sch.call("scalar", "activation", stat3[0:1, :], r_var, AF.Sqrt,
                      bias=env["epsc"][0:1, 0:1], wait=[(s_dv, dv)], inc=s_ev)
        dv = sch.call("vector", "reciprocal", r_var, stat3[0:1, :],
                      wait=[(s_ev, ev)], inc=s_dv)
        dv2 = sch.call("vector", "scalar_tensor_tensor", r_mean, r_mean, -1.0,
                       r_var, ALU.mult, ALU.mult, wait=[(s_dv, dv)], inc=s_dv)
        sch.call("tensor", "matmul", ps_ba[:, 0:512], env["ones1f"][0:1, :],
                 r_var, start=True, stop=True,
                 wait=[(s_dv, dv2), (s_dv, dv_apply_done)])
        pe_b = sch.call("tensor", "matmul", ps_bb[:, 0:512], env["ones1f"][0:1, :],
                        r_mean, start=True, stop=True, inc=s_pe)
        prev_bcast_pe = pe_b
        for c in range(FC):
            sch.call("vector", "tensor_mul", xnc(c, cols), xnc(c, cols),
                     ps_ba[:, 0:512], wait=([(s_pe, pe_b)] if c == 0 else []))
            nd = sch.call("vector", "tensor_add", xnc(c, cols), xnc(c, cols),
                          ps_bb[:, 0:512], inc=(s_dv if c == FC - 1 else None))
            dv_apply_done = nd or dv_apply_done
    env["ln_prev_dv"] = dv_apply_done
    env["ln_prev_bcast_pe"] = prev_bcast_pe
    env["ln_xsq_pe"] = xsq_pe
    return dv_apply_done


def fmajor(a):
    return np.ascontiguousarray(np.asarray(a, dtype=np.float32).T)


def _ident128():
    return np.eye(128, dtype=BF)


def _ones_row(dtype=np.float32):
    return np.ones((1, 128), dtype=dtype)


def causal_cmask(qc_w=256):
    j = np.arange(128)[:, None]
    i = np.arange(qc_w)[None, :]
    out = np.stack([np.where(j + 0 <= i, 0.0, NEG), np.where(j + 128 <= i, 0.0, NEG)])
    return out.astype(BF)


def chunk_cols(x_1d, p=128):
    n = np.asarray(x_1d).shape[0] // p
    return np.ascontiguousarray(np.asarray(x_1d, np.float32).reshape(n, p).T)


L0_QC = 256
L0_NQC = TOK // L0_QC
NKT = S // 128


def build_l0_attn(mega=None, io=None):
    E = BlockEnv(mega, "l0a_")
    nc = E.nc
    if mega is None:
        xf = nc.dram_tensor("xf", [D, S], F32, kind="ExternalInput").ap()
        xo_in = nc.dram_tensor("xo_in", [D, TOK], F32, kind="ExternalInput").ap()
        xo = nc.dram_tensor("xo", [D, TOK], F32, kind="ExternalOutput").ap()
    else:
        xf = io["xf"]
        xo_in = io["xo_in"]
        xo = io["xo"]
    wqT = E.dram_in("wqT", [D, D], F32)
    wkT = E.dram_in("wkT", [D, D], F32)
    wvT = E.dram_in("wvT", [D, D], F32)
    bqc = E.dram_in("bqc", [128, FC], F32)
    bvc = E.dram_in("bvc", [128, FC], F32)
    kmask = E.dram_in("kmask", [128, NKT], F32)
    cmask = E.dram_in("cmask", [2, 128, L0_QC], BF16)
    ident_in = E.dram_in("ident_in", [128, 128], BF16)
    onesb_in = E.dram_in("onesb_in", [1, 128], BF16)
    onesf_in = E.dram_in("onesf_in", [1, 128], F32)
    onescb_in = E.dram_in("onescb_in", [128, 1], BF16)
    onescf_in = E.dram_in("onescf_in", [128, 1], F32)
    epsc_in = E.dram_in("epsc_in", [1, 1], F32)
    ksc = E.dram_int("ksc", [FC, 128, S], BF16)
    vsc = E.dram_int("vsc", [NKT, 128, D], BF16)

    if True:
        _e = E.ctx.enter_context
        xn = E.sb("xn", [128, FC * S], BF16)
        kst = E.sb("kst", [128, 4 * D], BF16)
        vst = E.sb("vst", [128, 4 * D], BF16)
        kevac = E.sb("kevac", [128, 2 * 512], BF16)
        vevac = E.sb("vevac", [128, 2 * D], BF16)
        qq = E.sb("qq", [128, FC * TOK], BF16)
        wq = E.sb("wq", [128, FC * D], BF16)
        wk = E.sb("wk", [128, FC * D], BF16)
        wv = E.sb("wv", [128, FC * D], BF16)
        xsq = E.sb("xsq", [128, 2 * 512], BF16)
        stat = E.sb("stat", [1, 512], F32)
        stat2 = E.sb("stat2", [1, 512], F32)
        stat3 = E.sb("stat3", [1, 512], F32)
        bqv = E.sb("bqv", [128, 2 * FC], F32)
        kms = E.sb("kms", [128, NKT], F32)
        cms = E.sb("cms", [128, 2 * L0_QC], BF16)
        ident = E.sb("ident", [128, 128], BF16)
        ones1 = E.sb("ones1", [1, 128], BF16)
        ones1f = E.sb("ones1f", [1, 128], F32)
        onescb = E.sb("onescb", [128, 1], BF16)
        onescf = E.sb("onescf", [128, 1], F32)
        epsc = E.sb("epsc", [1, 1], F32)
        expt = E.sb("expt", [128, 4 * L0_QC], BF16)
        zacc = E.sb("zacc", [128, L0_QC], F32)
        rz = E.sb("rz", [1, L0_QC], F32)
        rzb = E.sb("rzb", [128, L0_QC], F32)
        xdm = E.sb("xdm", [128, FC * L0_QC], F32)
        xob = E.sb("xob", [128, FC * L0_QC], F32)
        uu = E.sb("uu", [128, FC * L0_QC], F32)
        ps_a = E.ps()
        ps_b = E.ps()
        pc0 = E.ps()
        pc1 = E.ps()
        pc2 = E.ps()
        pc3 = E.ps()
        ps_st = E.ps()
        ps_z = E.ps()
        s_ld = E.sem("s_ld")
        s_lg = E.sem("s_lg")
        s_pe = E.sem("s_pe")
        s_ev = E.sem("s_ev")
        s_dv = E.sem("s_dv")
        s_xd = E.sem("s_xd")
        s_kv = E.sem_list("s_kv", 4)
        s_sc = E.sem_list("s_sc", 4)
        s_st = E.sem("s_st")
        sch = E.sch
        E.start()
        PC = (pc0, pc1, pc2, pc3)

        sch.dma("sync", ident[:], ident_in[:, :], s_ld)
        sch.dma("sync", ones1[:], onesb_in[:, :], s_ld)
        sch.dma("sync", ones1f[:], onesf_in[:, :], s_ld)
        sch.dma("sync", onescb[:], onescb_in[:, :], s_ld)
        sch.dma("sync", onescf[:], onescf_in[:, :], s_ld)
        sch.dma("sync", epsc[:], epsc_in[:, :], s_ld)
        sch.dma("sync", cms[:, 0:L0_QC], cmask[0, :, :], s_ld)
        sch.dma("sync", cms[:, L0_QC:], cmask[1, :, :], s_ld)
        sch.dma("sync", kms[:], kmask[:, :], s_ld)
        sch.dma("sync", bqv[:, 0:FC], bqc[:, :], s_ld)
        sch.dma("sync", bqv[:, FC:], bvc[:, :], s_ld)
        for (wt, wsrc) in ((wq, wqT), (wk, wkT), (wv, wvT)):
            for c in range(FC):
                sch.dma("gpsimd", wt[:, c * D:(c + 1) * D],
                        wsrc[c * 128:(c + 1) * 128, :], s_lg)
        for c in range(FC):
            sch.dma("gpsimd", xn[:, c * S:(c + 1) * S],
                    xf[c * 128:(c + 1) * 128, :], s_lg)
        all_ld = sch.val(s_ld)
        all_lg = sch.val(s_lg)

        def xnc(c, cols):
            return xn[:, c * S:(c + 1) * S][:, cols]

        NTC = S // 512
        dv_apply_done = None
        prev_bcast_pe = None
        xsq_pe = [None, None]
        for tc in range(NTC):
            cols = slice(tc * 512, (tc + 1) * 512)
            for c in range(FC):
                slot = c % 2
                xq = xsq[:, slot * 512:slot * 512 + 512]
                w = []
                if tc == 0 and c == 0:
                    w.append((s_ld, all_ld)); w.append((s_lg, all_lg))
                w.append((s_pe, xsq_pe[slot]))
                ev = sch.call("scalar", "activation", xq, xnc(c, cols), AF.Square,
                              wait=w, inc=s_ev)
                sch.call("tensor", "matmul", ps_st[0:1, 0:512], onescb[:, 0:1],
                         xnc(c, cols), start=(c == 0), stop=(c == FC - 1),
                         wait=([(s_ld, all_ld), (s_lg, all_lg)] if tc == 0 and c == 0 else [])
                         + ([(s_dv, dv_apply_done)] if c == 0 else []))
                xsq_pe[slot] = sch.call("tensor", "matmul", ps_z[0:1, 0:512],
                                        onescb[:, 0:1], xq,
                                        start=(c == 0), stop=(c == FC - 1),
                                        wait=[(s_ev, ev)], inc=s_pe)
            pe_stats = sch.val(s_pe)
            r_mean = stat[0:1, :]
            r_var = stat2[0:1, :]
            sch.call("vector", "tensor_scalar_mul", r_mean, ps_st[0:1, 0:512],
                     1.0 / D, wait=[(s_pe, pe_stats), (s_pe, prev_bcast_pe)])
            sch.call("vector", "tensor_scalar_mul", r_var, ps_z[0:1, 0:512], 1.0 / D)
            sch.call("vector", "tensor_mul", stat3[0:1, :], r_mean, r_mean)
            dv = sch.call("vector", "tensor_sub", r_var, r_var, stat3[0:1, :],
                          inc=s_dv)
            ev = sch.call("scalar", "activation", stat3[0:1, :], r_var, AF.Sqrt,
                          bias=epsc[0:1, 0:1], wait=[(s_dv, dv)], inc=s_ev)
            dv = sch.call("vector", "reciprocal", r_var, stat3[0:1, :],
                          wait=[(s_ev, ev)], inc=s_dv)
            dv2 = sch.call("vector", "scalar_tensor_tensor", r_mean, r_mean, -1.0,
                           r_var, ALU.mult, ALU.mult, wait=[(s_dv, dv)], inc=s_dv)
            sch.call("tensor", "matmul", ps_a[:, 0:512], ones1f[0:1, :],
                     r_var, start=True, stop=True,
                     wait=[(s_dv, dv2), (s_dv, dv_apply_done)])
            pe_b = sch.call("tensor", "matmul", ps_b[:, 0:512], ones1f[0:1, :],
                            r_mean, start=True, stop=True, inc=s_pe)
            prev_bcast_pe = pe_b
            for c in range(FC):
                sch.call("vector", "tensor_mul", xnc(c, cols), xnc(c, cols),
                         ps_a[:, 0:512], wait=([(s_pe, pe_b)] if c == 0 else []))
                dv_apply_done = sch.call("vector", "tensor_add", xnc(c, cols),
                                         xnc(c, cols), ps_b[:, 0:512],
                                         inc=(s_dv if c == FC - 1 else None)) \
                    or dv_apply_done
        ln_done_dv = dv_apply_done

        pp = [ps_a, ps_b]
        pi = 0
        prev_ev = [(s_ev, None), (s_ev, None)]
        kev_dma = [None, None]
        vev_dma = [None, None]

        NTB = S // 512
        for tb in range(NTB):
            cols = slice(tb * 512, (tb + 1) * 512)
            for fo in range(FC):
                bank = pp[pi % 2]
                slot = pi % 2
                pi += 1
                w = [(s_dv, ln_done_dv), prev_ev[slot]]
                for c in range(FC):
                    sch.call("tensor", "matmul", bank[:, 0:512],
                             wk[:, c * D + fo * 128: c * D + fo * 128 + 128],
                             xnc(c, cols), start=(c == 0), stop=(c == FC - 1),
                             wait=(w if c == 0 else []),
                             inc=(s_pe if c == FC - 1 else None))
                pe = sch.val(s_pe)
                kslot = (tb * FC + fo) % 2
                ev = sch.call("scalar", "activation",
                              kevac[:, kslot * 512:(kslot + 1) * 512],
                              bank[:, 0:512], AF.Copy,
                              wait=[(s_pe, pe), (s_sc[kslot], kev_dma[kslot])], inc=s_ev)
                prev_ev[slot] = (s_ev, ev)
                kev_dma[kslot] = sch.dma("gpsimd", ksc[fo, :, cols],
                                         kevac[:, kslot * 512:(kslot + 1) * 512],
                                         s_sc[kslot], wait=[(s_ev, ev)])
            if tb >= 4:
                for fo in range(FC):
                    bank = pp[pi % 2]
                    slot = pi % 2
                    pi += 1
                    w = [(s_dv, ln_done_dv), prev_ev[slot]]
                    for c in range(FC):
                        sch.call("tensor", "matmul", bank[:, 0:512],
                                 wq[:, c * D + fo * 128: c * D + fo * 128 + 128],
                                 xnc(c, cols), start=(c == 0), stop=(c == FC - 1),
                                 wait=(w if c == 0 else []),
                                 inc=(s_pe if c == FC - 1 else None))
                    pe = sch.val(s_pe)
                    qdv = sch.call(
                        "vector", "tensor_scalar",
                        qq[:, fo * TOK + (tb - 4) * 512: fo * TOK + (tb - 3) * 512],
                        bank[:, 0:512], bqv[:, fo:fo + 1], None, ALU.add,
                        wait=[(s_pe, pe)], inc=s_dv)
                    prev_ev[slot] = (s_dv, qdv)
            for tt in range(4):
                kt = tb * 4 + tt
                tcol = slice(kt * 128, (kt + 1) * 128)
                vslot = kt % 2
                for half in range(2):
                    bank = pp[pi % 2]
                    slot = pi % 2
                    pi += 1
                    w = [(s_dv, ln_done_dv), prev_ev[slot]]
                    for c in range(FC):
                        sch.call("tensor", "matmul", bank[:, 0:512],
                                 xnc(c, tcol),
                                 wv[:, c * D + half * 512: c * D + (half + 1) * 512],
                                 start=(c == 0), stop=(c == FC - 1),
                                 wait=(w if c == 0 else []),
                                 inc=(s_pe if c == FC - 1 else None))
                    pe = sch.val(s_pe)
                    ev = sch.call("scalar", "activation",
                                  vevac[:, vslot * D + half * 512: vslot * D + (half + 1) * 512],
                                  bank[:, 0:512], AF.Copy,
                                  wait=[(s_pe, pe)]
                                  + ([(s_sc[2 + vslot], vev_dma[vslot])] if half == 0 else []),
                                  inc=s_ev)
                    prev_ev[slot] = (s_ev, ev)
                vev_dma[vslot] = sch.dma("gpsimd", vsc[kt, :, :],
                                         vevac[:, vslot * D:(vslot + 1) * D],
                                         s_sc[2 + vslot], wait=[(s_ev, ev)])
        scr_done = [(x, sch.val(x)) for x in s_sc]
        proj_all_ev = sch.val(s_ev)
        proj_all_dv = sch.val(s_dv)

        sp = [ps_a, ps_b]
        sprev = [(s_ev, proj_all_ev), (s_ev, proj_all_ev)]
        kv_slot_pe = [None, None, None, None]
        zacc_dv = [None, None, None, None]
        kv_ld = [None, None, None, None]
        ctx_read_dv = None
        si = 0
        for qc in range(L0_NQC):
            def qcols(c, qc=qc):
                return slice(c * TOK + qc * L0_QC, c * TOK + (qc + 1) * L0_QC)
            n_kt = 16 + 2 * (qc + 1)
            for c in range(FC):
                sch.dma("sync", xdm[:, c * L0_QC:(c + 1) * L0_QC],
                        xo_in[c * 128:(c + 1) * 128, qc * L0_QC:(qc + 1) * L0_QC],
                        s_xd, wait=([(s_dv, ctx_read_dv)] if c == 0 else []))
            xd_ld = sch.val(s_xd)
            for c in range(FC):
                sch.call("vector", "tensor_scalar", xob[:, c * L0_QC:(c + 1) * L0_QC],
                         xdm[:, c * L0_QC:(c + 1) * L0_QC],
                         bqv[:, FC + c:FC + c + 1], None, ALU.add,
                         wait=([(s_xd, xd_ld), (s_dv, ctx_read_dv)] if c == 0 else []))
            last_zacc_dv = None
            for ki in range(n_kt):
                kt = ki
                eslot = ki % 4
                lw = [(s_pe, kv_slot_pe[eslot])]
                if qc == 0 and ki == 0:
                    lw += scr_done
                ksrc = AP(tensor=ksc.tensor, offset=kt * 128,
                          ap=[[S, 128], [128 * S, FC], [1, 128]])
                sch.dma("gpsimd",
                        kst[:, eslot * D:(eslot + 1) * D].rearrange("p (f t) -> p f t", f=FC),
                        ksrc, s_kv[eslot], wait=lw)
                kv_ld[eslot] = sch.dma("gpsimd", vst[:, eslot * D:(eslot + 1) * D],
                                       vsc[kt, :, :], s_kv[eslot])
                bank = sp[si % 2]
                slot = si % 2
                si += 1
                dvar = None
                if kt == 16 + 2 * qc:
                    dvar = 0
                elif kt == 16 + 2 * qc + 1:
                    dvar = 1
                w = [sprev[slot], (s_kv[eslot], kv_ld[eslot])]
                if qc == 0 and ki == 0:
                    w.append((s_dv, proj_all_dv))
                pe_sc = None
                for c in range(FC):
                    last_qk = (c == FC - 1) and dvar is None
                    pe_sc = sch.call("tensor", "matmul", bank[:, 0:L0_QC],
                                     kst[:, eslot * D + c * 128: eslot * D + (c + 1) * 128],
                                     qq[:, qcols(c)], start=(c == 0), stop=last_qk,
                                     wait=(w if c == 0 else []),
                                     inc=(s_pe if last_qk else None)) or pe_sc
                if dvar is not None:
                    pe_sc = sch.call("tensor", "matmul", bank[:, 0:L0_QC], ident[:],
                                     cms[:, dvar * L0_QC:(dvar + 1) * L0_QC],
                                     start=False, stop=True, inc=s_pe)
                et = expt[:, eslot * L0_QC:(eslot + 1) * L0_QC]
                ev = sch.call("scalar", "activation", et, bank[:, 0:L0_QC], AF.Exp,
                              scale=1.0 / 32.0, bias=kms[:, kt:kt + 1],
                              wait=[(s_pe, pe_sc), (s_dv, zacc_dv[eslot])],
                              inc=s_ev)
                sprev[slot] = (s_ev, ev)
                zw = [(s_ev, ev)]
                if ki == 0:
                    zw.append((s_dv, ctx_read_dv))
                    last_zacc_dv = sch.call("vector", "tensor_copy", zacc[:], et,
                                            wait=zw, inc=s_dv)
                else:
                    last_zacc_dv = sch.call("vector", "tensor_add", zacc[:],
                                            zacc[:], et, wait=zw, inc=s_dv)
                zacc_dv[eslot] = last_zacc_dv
                avw = [(s_ev, ev)]
                if ki == 0:
                    avw.append((s_dv, ctx_read_dv))
                pe_av = None
                for fc in range(FC):
                    bankc = PC[fc // 2]
                    reg = slice((fc % 2) * L0_QC, (fc % 2 + 1) * L0_QC)
                    pe_av = sch.call("tensor", "matmul", bankc[:, reg],
                                     vst[:, eslot * D + fc * 128: eslot * D + (fc + 1) * 128],
                                     et, start=(ki == 0), stop=(ki == n_kt - 1),
                                     wait=(avw if fc == 0 else []),
                                     inc=(s_pe if fc == FC - 1 else None)) or pe_av
                kv_slot_pe[eslot] = pe_av
            av_done_pe = sch.val(s_pe)
            pe_z = sch.call("tensor", "matmul", ps_z[0:1, 0:L0_QC], onescf[:, 0:1],
                            zacc[:], start=True, stop=True,
                            wait=[(s_dv, last_zacc_dv)], inc=s_pe)
            dv_rz = sch.call("vector", "reciprocal", rz[0:1, :], ps_z[0:1, 0:L0_QC],
                             wait=[(s_pe, pe_z)], inc=s_dv)
            pe_rb = sch.call("tensor", "matmul", ps_z[:, L0_QC:2 * L0_QC],
                             ones1f[0:1, :], rz[0:1, :], start=True, stop=True,
                             wait=[(s_dv, dv_rz)], inc=s_pe)
            ev_rb = sch.call("scalar", "activation", rzb[:], ps_z[:, L0_QC:2 * L0_QC],
                             AF.Copy, wait=[(s_pe, pe_rb)], inc=s_ev)
            for c in range(FC):
                bankc = PC[c // 2]
                reg = slice((c % 2) * L0_QC, (c % 2 + 1) * L0_QC)
                uc = uu[:, c * L0_QC:(c + 1) * L0_QC]
                sch.call("vector", "tensor_mul", uc, bankc[:, reg], rzb[:],
                         wait=([(s_ev, ev_rb), (s_pe, av_done_pe)] if c == 0 else []))
                ctx_read_dv = sch.call("vector", "tensor_add", uc, uc,
                                       xob[:, c * L0_QC:(c + 1) * L0_QC],
                                       inc=(s_dv if c == FC - 1 else None)) \
                    or ctx_read_dv
            for c in range(FC):
                sch.dma("sync", xo[c * 128:(c + 1) * 128, qc * L0_QC:(qc + 1) * L0_QC],
                        uu[:, c * L0_QC:(c + 1) * L0_QC], s_st,
                        wait=([(s_dv, ctx_read_dv)] if c == 0 else []))
        st_all = sch.val(s_st)
        sch.barrier_wait("sync", s_st, st_all)
    return E.finish()


def host_l0_attn(x, W_qkv, b_qkv):
    wq = np.ascontiguousarray(np.asarray(W_qkv, np.float32)[0:D, :].T)
    wk = np.ascontiguousarray(np.asarray(W_qkv, np.float32)[D:2 * D, :].T)
    wv = np.ascontiguousarray(np.asarray(W_qkv, np.float32)[2 * D:3 * D, :].T)
    b_qkv = np.asarray(b_qkv, np.float32)
    bq = chunk_cols(b_qkv[0:D])
    bv = chunk_cols(b_qkv[2 * D:3 * D])
    ident = _ident128()
    onesb = _ones_row(BF)
    onesf = _ones_row(np.float32)
    cmask = causal_cmask(L0_QC)
    in_maps = []
    for core in range(NC):
        b, h = core // 2, core % 2
        own = x[b, h * TOK:(h + 1) * TOK]
        other = x[b, (1 - h) * TOK:(2 - h) * TOK]
        xcat = np.concatenate([other, own], axis=0)
        km = np.zeros((128, NKT), np.float32)
        if h == 0:
            km[:, 0:16] = NEG
        in_maps.append({
            "xf": fmajor(xcat), "xo_in": fmajor(own),
            "wqT": wq, "wkT": wk, "wvT": wv,
            "bqc": bq, "bvc": bv, "kmask": km,
            "cmask": cmask, "ident_in": ident, "onesb_in": onesb, "onesf_in": onesf,
            "onescb_in": np.ascontiguousarray(onesb.T), "onescf_in": np.ascontiguousarray(onesf.T),
            "epsc_in": np.full((1, 1), EPS, np.float32),
        })
    return in_maps


def assemble_fm(results, key="xo"):
    out = np.empty((B, S, D), np.float32)
    for core in range(NC):
        b, h = core // 2, core % 2
        out[b, h * TOK:(h + 1) * TOK] = np.asarray(results[core][key]).T
    return out





TCH = 512
NTCH = TOK // TCH
NE = 8


def build_l0_ffn(debug=False, mega=None, io=None):
    E = BlockEnv(mega, "l0f_")
    nc = E.nc
    if mega is None:
        xi = nc.dram_tensor("xi", [D, TOK], F32, kind="ExternalInput").ap()
        xo = nc.dram_tensor("xo", [D, TOK], F32, kind="ExternalOutput").ap()
    else:
        xi = io["xi"]
        xo = io["xo"]
    rw_in = E.dram_in("rw_in", [D, NE], F32)
    gup_in = E.dram_in("gup_in", [NE, D, 2 * D], F32)
    gub_in = E.dram_in("gub_in", [1, NE * 2 * D], F32)
    dwn_in = E.dram_in("dwn_in", [NE, D, D], F32)
    dwb_in = E.dram_in("dwb_in", [1, NE * D], F32)
    onesb_in = E.dram_in("onesb_in", [1, 128], BF16)
    onesf_in = E.dram_in("onesf_in", [1, 128], F32)
    onescb_in = E.dram_in("onescb_in", [128, 1], BF16)
    onest_in = E.dram_in("onest_in", [1, TCH], BF16)
    identf_in = E.dram_in("identf_in", [128, 128], F32)
    epsc_in = E.dram_in("epsc_in", [1, 1], F32)
    if debug:
        dbg_lg = nc.dram_tensor("dbg_lg", [128, 128], F32, kind="ExternalOutput").ap()
        dbg_ww = nc.dram_tensor("dbg_ww", [128, 128], F32, kind="ExternalOutput").ap()
        dbg_gate = nc.dram_tensor("dbg_gate", [128, TCH], F32, kind="ExternalOutput").ap()
        dbg_act = nc.dram_tensor("dbg_act", [128, TCH], BF16, kind="ExternalOutput").ap()
        dbg_acc = nc.dram_tensor("dbg_acc", [128, D], F32, kind="ExternalOutput").ap()
        dbg_xn = nc.dram_tensor("dbg_xn", [128, TOK], F32, kind="ExternalOutput").ap()
        dbg_m = nc.dram_tensor("dbg_m", [128, 8], F32, kind="ExternalOutput").ap()
        dbg_eq = nc.dram_tensor("dbg_eq", [128, 16], F32, kind="ExternalOutput").ap()

    with ExitStack() as ctx:
        _e = ctx.enter_context
        xn = E.sb("xn", [128, FC * TOK], BF16)

        gu = E.sb("gu", [128, 2 * FC * 2 * D], BF16)
        dw = E.sb("dw", [128, FC * D], BF16)
        gatef = E.sb("gatef", [128, FC * TCH], F32)
        upf = E.sb("upf", [128, FC * TCH], BF16)
        act = E.sb("act", [128, FC * TCH], BF16)
        slu = E.sb("slu", [128, 2 * TCH], F32)
        acc = E.sb("acc", [128, 4 * D], F32)
        xs = E.sb("xs", [128, FC * TCH], F32)
        rw = E.sb("rw", [128, FC * NE], BF16)
        lg = E.sb("lg", [128, 16 * NE], F32)
        ww = E.sb("ww", [128, 16 * NE], F32)
        m1c = E.sb("m1c", [128, 8], F32)
        eqb = E.sb("eqb", [128, 2 * NE], F32)
        gub = E.sb("gub", [1, 2 * 2 * D], BF16)
        dwb = E.sb("dwb", [1, 2 * D], BF16)
        onesb = E.sb("onesb", [1, 128], BF16)
        ones1f = E.sb("ones1f", [1, 128], F32)
        onescb = E.sb("onescb", [128, 1], BF16)
        onest = E.sb("onest", [1, TCH], BF16)
        identf = E.sb("identf", [128, 128], F32)
        epsc = E.sb("epsc", [1, 1], F32)
        stat = E.sb("stat", [1, 512], F32)
        stat2 = E.sb("stat2", [1, 512], F32)
        stat3 = E.sb("stat3", [1, 512], F32)
        xsq = E.sb("xsq", [128, 2 * 512], BF16)
        ps_a = E.ps()
        ps_b = E.ps()
        ps_c = E.ps()
        ps_d = E.ps()
        ps_sum = E.ps()
        ps_sq = E.ps()
        ps_ba = E.ps()
        ps_bb = E.ps()
        s_ld = E.sem("s_ld")
        s_lg = E.sem("s_lg")
        s_pe = E.sem("s_pe")
        s_ev = E.sem("s_ev")
        s_dv = E.sem("s_dv")
        s_wl = E.sem_list("s_wl", 2)
        s_xs = E.sem("s_xs")
        s_st = E.sem("s_st")

        sch = E.sch
        E.start()
        sch.dma("sync", onesb[:], onesb_in[:, :], s_lg)
        sch.dma("sync", ones1f[:], onesf_in[:, :], s_ld)
        sch.dma("sync", onescb[:], onescb_in[:, :], s_ld)
        sch.dma("sync", onest[:], onest_in[:, :], s_ld)
        sch.dma("sync", identf[:], identf_in[:, :], s_ld)
        sch.dma("sync", epsc[:], epsc_in[:, :], s_ld)

        for c in range(FC):
            sch.dma("gpsimd", rw[:, c * NE:(c + 1) * NE],
                    rw_in[c * 128:(c + 1) * 128, :], s_lg)
        for c in range(FC):
            sch.dma("gpsimd", xn[:, c * TOK:(c + 1) * TOK],
                    xi[c * 128:(c + 1) * 128, :], s_lg)
        all_ld = sch.val(s_ld)
        all_lg = sch.val(s_lg)

        env = {"xn": xn, "onescb": onescb, "ones1f": ones1f, "epsc": epsc,
               "stat": stat, "stat2": stat2, "stat3": stat3, "xsq": xsq,
               "ps_sum": ps_sum, "ps_sq": ps_sq, "ps_ba": ps_ba, "ps_bb": ps_bb,
               "s_pe": s_pe, "s_ev": s_ev, "s_dv": s_dv,
               "xn_ready": [(s_ld, all_ld), (s_lg, all_lg)]}
        ln_dv = emit_ln_fm(sch, env, TOK)

        def xnc(c, cols):
            return xn[:, c * TOK:(c + 1) * TOK][:, cols]

        for ti in range(16):
            tcol = slice(ti * 128, (ti + 1) * 128)
            for c in range(FC):
                sch.call("tensor", "matmul", ps_bb[:, ti * NE:(ti + 1) * NE],
                         xnc(c, tcol), rw[:, c * NE:(c + 1) * NE],
                         start=(c == 0), stop=(c == FC - 1),
                         wait=([(s_dv, ln_dv)] if ti == 0 and c == 0 else []),
                         inc=(s_pe if ti == 15 and c == FC - 1 else None))
        pe_lg = sch.val(s_pe)
        dv = sch.call("vector", "tensor_copy", lg[:], ps_bb[:, 0:16 * NE],
                      wait=[(s_pe, pe_lg)], inc=s_dv)
        w_dv = None
        for ti in range(16):
            l8 = lg[:, ti * NE:(ti + 1) * NE]
            m1 = m1c[:, 0:1]; m2 = m1c[:, 1:2]; dd = m1c[:, 2:3]
            s1 = m1c[:, 3:4]; s2 = m1c[:, 4:5]
            eq1 = eqb[:, 0:NE]; eq2 = eqb[:, NE:2 * NE]
            c_ = sch.call("vector", "reduce_max", m1, l8, AX.X,
                          wait=([(s_dv, dv)] if ti == 0 else []), inc=s_dv)
            c_ = sch.call("vector", "tensor_scalar", eq1, l8, m1, None,
                          ALU.is_equal, wait=[(s_dv, c_)], inc=s_dv)
            c_ = sch.call("vector", "scalar_tensor_tensor", eq2, eq1, NEG, l8,
                          ALU.mult, ALU.add, wait=[(s_dv, c_)], inc=s_dv)
            c_ = sch.call("vector", "reduce_max", m2, eq2, AX.X,
                          wait=[(s_dv, c_)], inc=s_dv)
            c_ = sch.call("vector", "tensor_sub", dd, m1, m2,
                          wait=[(s_dv, c_)], inc=s_dv)
            dv2 = sch.call("vector", "tensor_scalar", eq2, l8, m2, None,
                           ALU.is_equal, wait=[(s_dv, c_)], inc=s_dv)
            ev = sch.call("scalar", "activation", s1, dd, AF.Sigmoid,
                          wait=[(s_dv, dv2)], inc=s_ev)
            c_ = sch.call("vector", "tensor_scalar", s2, s1, -1.0, 1.0, ALU.mult,
                          ALU.add, wait=[(s_ev, ev)], inc=s_dv)
            c_ = sch.call("vector", "tensor_scalar", eq1, eq1, s1, None, ALU.mult,
                          wait=[(s_dv, c_)], inc=s_dv)
            w_dv = sch.call("vector", "scalar_tensor_tensor",
                            ww[:, ti * NE:(ti + 1) * NE], eq2, s2, eq1,
                            ALU.mult, ALU.add, wait=[(s_dv, c_)], inc=s_dv)

        pp = [ps_a, ps_b]
        dp = [ps_c, ps_d]
        pi = 0
        di = 0
        gu_ld = [None, None]
        gu_pe = [None, None]
        dw_ld = [None, None]
        dw_lh = [None, None]
        dw_pe = [None, None]
        s_dwh = E.sem("s_dwh")
        prev_gu_ev = [(s_dv, None), (s_dv, None)]
        prev_dn_dv = [None, None]
        acc_tr_pe = None
        prev_st = None
        st_all = None
        for tc in range(NTCH):
            cols = slice(tc * TCH, (tc + 1) * TCH)
            for c in range(FC):
                sch.dma("sync", xs[:, c * TCH:(c + 1) * TCH],
                        xi[c * 128:(c + 1) * 128, cols], s_xs,
                        wait=([(s_st, prev_st)] if c == 0 else []))
            xs_ld = sch.val(s_xs)
            for e in range(NE):
                gslot = (tc * NE + e) % 2
                gu_ld[gslot] = sch.dma(
                    "gpsimd", gu[:, gslot * FC * 2 * D:(gslot + 1) * FC * 2 * D]
                    .rearrange("p (c f) -> p c f", c=FC),
                    gup_in[e, :, :].rearrange("(c p) f -> p c f", p=128),
                    s_wl[gslot], wait=[(s_pe, gu_pe[gslot])])
                dw_lh[gslot] = sch.dma(
                    "gpsimd", dw[:, :].rearrange("p (c f) -> p c f", c=FC),
                    dwn_in[e, :, :].rearrange("(c p) f -> p c f", p=128),
                    s_dwh, wait=[(s_pe, dw_pe[0]), (s_pe, dw_pe[1])])
                sch.dma("gpsimd", gub[0:1, gslot * 2 * D:(gslot + 1) * 2 * D],
                        gub_in[0:1, e * 2 * D:(e + 1) * 2 * D], s_wl[gslot])
                full = sch.dma("gpsimd", dwb[0:1, gslot * D:(gslot + 1) * D],
                               dwb_in[0:1, e * D:(e + 1) * D], s_wl[gslot])
                gu_ld[gslot] = full
                dw_ld[gslot] = full
                for side in range(2):
                    dst = gatef if side == 0 else upf
                    for gf in range(FC):
                        bank = pp[pi % 2]
                        slot = pi % 2
                        pi += 1
                        w = [(s_wl[gslot], gu_ld[gslot]), prev_gu_ev[slot],
                             (s_ld, all_ld), (s_lg, all_lg)]
                        if tc == 0 and e == 0 and side == 0 and gf == 0:
                            w.append((s_dv, w_dv))
                        gub_base = gslot * FC * 2 * D
                        for c in range(FC):
                            gu3 = gu[:, gub_base:gub_base + FC * 2 * D] \
                                .rearrange("p (cc f) -> p cc f", cc=FC)
                            sch.call("tensor", "matmul", bank[:, 0:TCH],
                                     gu3[:, c, side * D + gf * 128: side * D + (gf + 1) * 128]
                                     if False else gu3[:, c, :][:, side * D + gf * 128: side * D + (gf + 1) * 128],
                                     xnc(c, cols), start=(c == 0), stop=False,
                                     wait=(w if c == 0 else []))
                        gu_pe[gslot] = sch.call(
                            "tensor", "matmul", bank[:, 0:TCH],
                            gub[0:1, gslot * 2 * D + side * D + gf * 128:
                                gslot * 2 * D + side * D + (gf + 1) * 128],
                            onest[0:1, :], start=False, stop=True, inc=s_pe)
                        pe = sch.val(s_pe)
                        if side == 0:
                            prev_gu_ev[slot] = (s_dv, sch.call(
                                "vector", "tensor_scalar",
                                dst[:, gf * TCH:(gf + 1) * TCH], bank[:, 0:TCH],
                                7.0, None, ALU.min, wait=[(s_pe, pe)], inc=s_dv))
                        else:
                            prev_gu_ev[slot] = (s_dv, sch.call(
                                "vector", "tensor_scalar",
                                dst[:, gf * TCH:(gf + 1) * TCH], bank[:, 0:TCH],
                                7.0, -7.0, ALU.min, ALU.max,
                                wait=[(s_pe, pe)], inc=s_dv))
                gu_all_dv = sch.val(s_dv)
                act_dv = None
                slu_dv = [None, None]
                for gf in range(FC):
                    gch = gatef[:, gf * TCH:(gf + 1) * TCH]
                    uch = upf[:, gf * TCH:(gf + 1) * TCH]
                    ach = act[:, gf * TCH:(gf + 1) * TCH]
                    sch_ = slu[:, (gf % 2) * TCH:(gf % 2 + 1) * TCH]
                    ev = sch.call("scalar", "activation", sch_, gch, AF.Silu,
                                  scale=1.702,
                                  wait=([(s_dv, gu_all_dv)] if gf == 0 else [])
                                  + [(s_dv, slu_dv[gf % 2])],
                                  inc=s_ev)
                    sch.call("vector", "tensor_scalar", ach, uch,
                             1.0 / 1.702, 1.0 / 1.702, ALU.mult, ALU.add)
                    act_dv = sch.call("vector", "tensor_mul", ach, ach, sch_,
                                      wait=[(s_ev, ev)], inc=s_dv)
                    slu_dv[gf % 2] = act_dv
                if debug and tc == 0 and e == 0:
                    sch.dma("sync", dbg_gate[:, :], gatef[:, 0:TCH], s_st,
                            wait=[(s_dv, act_dv)])
                    sch.dma("sync", dbg_act[:, :], act[:, 0:TCH], s_st,
                            wait=[(s_dv, act_dv)])
                for tt in range(4):
                    trow = slice(tt * 128, (tt + 1) * 128)
                    wcol = ww[:, (tc * 4 + tt) * NE + e:(tc * 4 + tt) * NE + e + 1]
                    for half in range(2):
                        bank = dp[di % 2]
                        slot = di % 2
                        di += 1
                        w = [(s_wl[gslot], dw_ld[gslot]),
                             (s_dwh, dw_lh[gslot]), (s_dv, act_dv),
                             (s_dv, prev_dn_dv[slot])]
                        for c in range(FC):
                            sch.call("tensor", "matmul", bank[:, 0:512],
                                     act[:, c * TCH:(c + 1) * TCH][:, trow],
                                     dw[:, :].rearrange("p (cc f) -> p cc f", cc=FC)
                                     [:, c, half * 512:(half + 1) * 512],
                                     start=(c == 0), stop=False,
                                     wait=(w if c == 0 else []))
                        dw_pe[gslot] = sch.call(
                            "tensor", "matmul", bank[:, 0:512],
                            onesb[0:1, 0:128],
                            dwb[0:1, gslot * D + half * 512: gslot * D + (half + 1) * 512],
                            start=False, stop=True, inc=s_pe)
                        pe = sch.val(s_pe)
                        aslice = acc[:, tt * D + half * 512: tt * D + (half + 1) * 512]
                        if e == 0:
                            prev_dn_dv[slot] = sch.call(
                                "vector", "tensor_scalar", aslice, bank[:, 0:512],
                                wcol, None, ALU.mult,
                                wait=[(s_pe, pe), (s_pe, acc_tr_pe)], inc=s_dv)
                        else:
                            prev_dn_dv[slot] = sch.call(
                                "vector", "scalar_tensor_tensor", aslice,
                                bank[:, 0:512], wcol, aslice, ALU.mult, ALU.add,
                                wait=[(s_pe, pe)], inc=s_dv)
            if debug and tc == 0:
                sch.dma("sync", dbg_acc[:, :], acc[:, 0:D], s_st,
                        wait=[(s_dv, sch.val(s_dv))])
            acc_dv = sch.val(s_dv)
            uev = None
            for tt in range(4):
                for fo in range(FC):
                    pe_t = sch.call("tensor", "matmul", ps_ba[:, 0:128],
                                    acc[:, tt * D + fo * 128: tt * D + (fo + 1) * 128],
                                    identf[:, :], start=True, stop=True,
                                    wait=[(s_dv, acc_dv), (s_dv, uev)], inc=s_pe)
                    xsl = xs[:, fo * TCH + tt * 128: fo * TCH + (tt + 1) * 128]
                    uev = sch.call("vector", "tensor_add", xsl, ps_ba[:, 0:128],
                                   xsl, wait=[(s_pe, pe_t), (s_xs, xs_ld)], inc=s_dv)
            for c in range(FC):
                sch.dma("sync", xo[c * 128:(c + 1) * 128, cols],
                        xs[:, c * TCH:(c + 1) * TCH], s_st,
                        wait=([(s_dv, uev)] if c == 0 else []))
            st_all = sch.val(s_st)
            prev_st = st_all
            acc_tr_pe = sch.val(s_pe)
        if debug:
            sch.dma("sync", dbg_lg[:, :], lg[:], s_st, wait=[(s_dv, w_dv)])
            sch.dma("sync", dbg_m[:, :], m1c[:], s_st, wait=[(s_dv, w_dv)])
            sch.dma("sync", dbg_eq[:, :], eqb[:], s_st, wait=[(s_dv, w_dv)])
            sch.dma("sync", dbg_ww[:, :], ww[:], s_st, wait=[(s_dv, w_dv)])
            sch.dma("gpsimd", dbg_xn[:, :], xn[:, 0:TOK], s_st, wait=[(s_dv, w_dv)])
        st_all = sch.val(s_st)
        sch.barrier_wait("sync", s_st, st_all)
    return E.finish()


def host_l0_ffn(rw, gup, gub, dwn, dwb):
    rw = np.ascontiguousarray(np.asarray(rw, np.float32))
    gup = np.asarray(gup, np.float32)
    gup_r = np.ascontiguousarray(
        np.concatenate([gup[:, :, 0::2], gup[:, :, 1::2]], axis=2))
    gub = np.asarray(gub, np.float32)
    gub_r = np.concatenate([gub[:, 0::2], gub[:, 1::2]], axis=1).reshape(1, -1)
    dwn = np.ascontiguousarray(np.asarray(dwn, np.float32))
    dwb = np.asarray(dwb, np.float32).reshape(1, -1)
    onest = np.ones((1, TCH), BF)
    return {
        "rw_in": rw, "gup_in": gup_r, "gub_in": np.ascontiguousarray(gub_r),
        "dwn_in": dwn, "dwb_in": np.ascontiguousarray(dwb),
        "onesb_in": _ones_row(BF), "onesf_in": _ones_row(np.float32),
        "onescb_in": np.ascontiguousarray(_ones_row(BF).T),
        "onest_in": onest, "identf_in": np.eye(128, dtype=np.float32),
        "epsc_in": np.full((1, 1), EPS, np.float32),
    }


def build_l2_ffn(mega=None, io=None):
    E = BlockEnv(mega, "l2f_")
    nc = E.nc
    if mega is None:
        xi = nc.dram_tensor("xi", [D, TOK], F32, kind="ExternalInput").ap()
        xh = nc.dram_tensor("xh", [D, 1], F32, kind="ExternalInput").ap()
        xo = nc.dram_tensor("xo", [D, TOK], F32, kind="ExternalOutput").ap()
    else:
        xi = io["xi"]; xh = io.get("xh"); xo = io["xo"]
    wrT = E.dram_in("wrT", [D, D], F32)
    wkT = E.dram_in("wkT", [D, D], F32)
    wvT = E.dram_in("wvT", [D, D], F32)
    mur = E.dram_in("mur", [128, FC], F32)
    muk = E.dram_in("muk", [128, FC], F32)
    epsc_in = E.dram_in("epsc_in", [1, 1], F32)
    onescb_in = E.dram_in("onescb_in", [128, 1], BF16)
    onesf_in = E.dram_in("onesf_in", [1, 128], F32)

    TP = TOK + 1
    if True:
        _e = E.ctx.enter_context
        xs = E.sb("xs", [128, FC * 512], F32)
        xn = E.sb("xn", [128, FC * TP], BF16)
        xm = E.sb("xm", [128, FC * TOK], BF16)
        rr = E.sb("rr", [128, FC * TOK], BF16)
        ka = E.sb("ka", [128, FC * TOK], BF16)
        kr = E.sb("kr", [128, 2 * 512], BF16)
        utmp = E.sb("utmp", [128, 512], F32)
        wr = E.sb("wr", [128, FC * D], BF16)
        wk = E.sb("wk", [128, FC * D], BF16)
        wv = wr
        muc = E.sb("muc", [128, 2 * FC], F32)
        stat = E.sb("stat", [1, 512], F32)
        stat2 = E.sb("stat2", [1, 512], F32)
        stat3 = E.sb("stat3", [1, 512], F32)
        xsq = E.sb("xsq", [128, 2 * 512], BF16)
        epsc = E.sb("epsc", [1, 1], F32)
        onescb = E.sb("onescb", [128, 1], BF16)
        ones1f = E.sb("ones1f", [1, 128], F32)
        ps_a = E.ps()
        ps_b = E.ps()
        ps_c = E.ps()
        ps_d = E.ps()
        s_ld = E.sem("s_ld")
        s_lg = E.sem("s_lg")
        s_pe = E.sem("s_pe")
        s_ev = E.sem("s_ev")
        s_dv = E.sem("s_dv")
        s_xs = E.sem("s_xs")
        s_st = E.sem("s_st")
        sch = E.sch
        E.start()

        sch.dma("sync", muc[:, 0:FC], mur[:, :], s_ld)
        sch.dma("sync", muc[:, FC:], muk[:, :], s_ld)
        sch.dma("sync", epsc[:], epsc_in[:, :], s_ld)
        sch.dma("sync", onescb[:], onescb_in[:, :], s_ld)
        sch.dma("sync", ones1f[:], onesf_in[:, :], s_ld)
        for c in range(FC):
            if io is not None and "xh_load" in io:
                io["xh_load"](sch, c, xn[:, c * TP:c * TP + 1], s_lg)
            else:
                sch.dma("gpsimd", xn[:, c * TP:c * TP + 1],
                        xh[c * 128:(c + 1) * 128, :], s_lg)
            sch.dma("gpsimd", xn[:, c * TP + 1:(c + 1) * TP],
                    xi[c * 128:(c + 1) * 128, :], s_lg)
        for (wt, wsrc) in ((wr, wrT), (wk, wkT)):
            for c in range(FC):
                sch.dma("gpsimd", wt[:, c * D:(c + 1) * D],
                        wsrc[c * 128:(c + 1) * 128, :], s_lg)
        all_ld = sch.val(s_ld)
        all_lg = sch.val(s_lg)

        env = {"xn": xn, "onescb": onescb, "ones1f": ones1f, "epsc": epsc,
               "stat": stat, "stat2": stat2, "stat3": stat3, "xsq": xsq,
               "ps_sum": ps_a, "ps_sq": ps_b, "ps_ba": ps_c, "ps_bb": ps_d,
               "s_pe": s_pe, "s_ev": s_ev, "s_dv": s_dv,
               "xn_ready": [(s_ld, all_ld), (s_lg, all_lg)],
               "chunk_fn": lambda c: xn[:, c * TP + 1:(c + 1) * TP]}
        ln_dv = emit_ln_fm(sch, env, TOK)
        def hcol(c):
            return xn[:, c * TP:c * TP + 1]
        for c in range(FC):
            sch.call("tensor", "matmul", ps_a[0:1, 0:1], onescb[:, 0:1], hcol(c),
                     start=(c == 0), stop=(c == FC - 1),
                     wait=([(s_dv, ln_dv)] if c == 0 else []))
            sch.call("tensor", "matmul", ps_b[0:1, 0:1], hcol(c), hcol(c),
                     start=(c == 0), stop=(c == FC - 1),
                     inc=(s_pe if c == FC - 1 else None))
        hpe = sch.val(s_pe)
        hr = stat3[0:1, 0:4]
        d_ = sch.call("vector", "tensor_scalar_mul", hr[0:1, 0:1], ps_a[0:1, 0:1],
                      1.0 / D, wait=[(s_pe, hpe)], inc=s_dv)
        d_ = sch.call("vector", "tensor_scalar_mul", hr[0:1, 1:2], ps_b[0:1, 0:1],
                      1.0 / D, wait=[(s_dv, d_)], inc=s_dv)
        d_ = sch.call("vector", "tensor_mul", hr[0:1, 2:3], hr[0:1, 0:1],
                      hr[0:1, 0:1], wait=[(s_dv, d_)], inc=s_dv)
        d_ = sch.call("vector", "tensor_sub", hr[0:1, 1:2], hr[0:1, 1:2],
                      hr[0:1, 2:3], wait=[(s_dv, d_)], inc=s_dv)
        e_ = sch.call("scalar", "activation", hr[0:1, 2:3], hr[0:1, 1:2], AF.Sqrt,
                      bias=epsc[0:1, 0:1], wait=[(s_dv, d_)], inc=s_ev)
        d_ = sch.call("vector", "reciprocal", hr[0:1, 1:2], hr[0:1, 2:3],
                      wait=[(s_ev, e_)], inc=s_dv)
        d_ = sch.call("vector", "scalar_tensor_tensor", hr[0:1, 0:1], hr[0:1, 0:1],
                      -1.0, hr[0:1, 1:2], ALU.mult, ALU.mult,
                      wait=[(s_dv, d_)], inc=s_dv)
        sch.call("tensor", "matmul", ps_a[:, 0:1], ones1f[0:1, :], hr[0:1, 1:2],
                 start=True, stop=True, wait=[(s_dv, d_)])
        hbpe = sch.call("tensor", "matmul", ps_b[:, 0:1], ones1f[0:1, :],
                        hr[0:1, 0:1], start=True, stop=True, inc=s_pe)
        hdv = None
        for c in range(FC):
            d_ = sch.call("vector", "tensor_mul", hcol(c), hcol(c), ps_a[:, 0:1],
                          wait=([(s_pe, hbpe)] if c == 0 else [(s_dv, hdv)]),
                          inc=s_dv)
            hdv = sch.call("vector", "tensor_add", hcol(c), hcol(c), ps_b[:, 0:1],
                           wait=[(s_dv, d_)], inc=s_dv)
        ln_all_dv = hdv

        def mix2(mcol_off, wait_pe):
            dvs = None
            for c in range(FC):
                xcur = xn[:, c * TP + 1:(c + 1) * TP]
                xprv = xn[:, c * TP:(c + 1) * TP - 1]
                mcol = muc[:, mcol_off + c:mcol_off + c + 1]
                o = xm[:, c * TOK:(c + 1) * TOK]
                w = [(s_dv, ln_all_dv)] if c == 0 else []
                if wait_pe is not None and c == 0:
                    w.append((s_pe, wait_pe))
                d1 = sch.call("vector", "tensor_sub", o, xcur, xprv, wait=w,
                              inc=s_dv)
                d2 = sch.call("vector", "tensor_scalar", o, o, mcol, None,
                              ALU.mult, wait=[(s_dv, d1)], inc=s_dv)
                dvs = sch.call("vector", "tensor_add", o, o, xprv,
                               wait=[(s_dv, d2)], inc=s_dv)
            return dvs

        pp = [ps_a, ps_b]
        pi = 0
        prev_ev = [(s_ev, None), (s_ev, None)]

        def proj(dst_buf, wsb, act_fn, mix_dv):
            nonlocal pi
            evs = None
            for tb in range(TOK // 512):
                cols = slice(tb * 512, (tb + 1) * 512)
                for fo in range(FC):
                    bank = pp[pi % 2]
                    slot = pi % 2
                    pi += 1
                    w = [(s_dv, mix_dv), prev_ev[slot]]
                    for c in range(FC):
                        sch.call("tensor", "matmul", bank[:, 0:512],
                                 wsb[:, c * D + fo * 128: c * D + fo * 128 + 128],
                                 xm[:, c * TOK:(c + 1) * TOK][:, cols],
                                 start=(c == 0), stop=(c == FC - 1),
                                 wait=(w if c == 0 else []),
                                 inc=(s_pe if c == FC - 1 else None))
                    pe = sch.val(s_pe)
                    evs = sch.call("scalar", "activation",
                                   dst_buf[:, fo * TOK:(fo + 1) * TOK][:, cols],
                                   bank[:, 0:512], act_fn, wait=[(s_pe, pe)],
                                   inc=s_ev)
                    prev_ev[slot] = (s_ev, evs)
            return evs

        mdv = mix2(0, None)
        proj(rr, wr, AF.Sigmoid, mdv)
        r_pe = sch.val(s_pe)
        for c in range(FC):
            sch.dma("gpsimd", wv[:, c * D:(c + 1) * D],
                    wvT[c * 128:(c + 1) * 128, :], s_lg,
                    wait=([(s_pe, r_pe)] if c == 0 else []))
        wv_ld = sch.val(s_lg)
        sch.barrier_wait("tensor", s_lg, wv_ld)
        mdv = mix2(FC, r_pe)
        evs = None
        for tb in range(TOK // 512):
            cols = slice(tb * 512, (tb + 1) * 512)
            for fo in range(FC):
                bank = pp[pi % 2]
                slot = pi % 2
                pi += 1
                w = [(s_dv, mdv), prev_ev[slot]]
                for c in range(FC):
                    sch.call("tensor", "matmul", bank[:, 0:512],
                             wk[:, c * D + fo * 128: c * D + fo * 128 + 128],
                             xm[:, c * TOK:(c + 1) * TOK][:, cols],
                             start=(c == 0), stop=(c == FC - 1),
                             wait=(w if c == 0 else []),
                             inc=(s_pe if c == FC - 1 else None))
                pe = sch.val(s_pe)
                kslot = (tb * FC + fo) % 2
                ev1 = sch.call("scalar", "activation", kr[:, kslot * 512:(kslot + 1) * 512],
                               bank[:, 0:512], AF.Relu, wait=[(s_pe, pe)], inc=s_ev)
                prev_ev[slot] = (s_ev, ev1)
                evs = sch.call("vector", "tensor_mul",
                               ka[:, fo * TOK:(fo + 1) * TOK][:, cols],
                               kr[:, kslot * 512:(kslot + 1) * 512],
                               kr[:, kslot * 512:(kslot + 1) * 512],
                               wait=[(s_ev, ev1)], inc=s_dv)
        ka_dv = sch.val(s_dv)
        prev_st = None
        for tb in range(TOK // 512):
            cols = slice(tb * 512, (tb + 1) * 512)
            for c in range(FC):
                sch.dma("sync", xs[:, c * 512:(c + 1) * 512],
                        xi[c * 128:(c + 1) * 128, cols], s_xs,
                        wait=([(s_st, prev_st)] if c == 0 else []))
            xs_ld = sch.val(s_xs)
            for fo in range(FC):
                bank = pp[pi % 2]
                slot = pi % 2
                pi += 1
                w = [(s_dv, ka_dv), prev_ev[slot]]
                for c in range(FC):
                    sch.call("tensor", "matmul", bank[:, 0:512],
                             wv[:, c * D + fo * 128: c * D + fo * 128 + 128],
                             ka[:, c * TOK:(c + 1) * TOK][:, cols],
                             start=(c == 0), stop=(c == FC - 1),
                             wait=(w if c == 0 else []),
                             inc=(s_pe if c == FC - 1 else None))
                pe = sch.val(s_pe)
                xcur = xs[:, fo * 512:(fo + 1) * 512]
                d1 = sch.call("vector", "tensor_mul", utmp[:, 0:512],
                              bank[:, 0:512],
                              rr[:, fo * TOK:(fo + 1) * TOK][:, cols],
                              wait=[(s_pe, pe)] + ([(s_xs, xs_ld)] if fo == 0 else []),
                              inc=s_dv)
                prev_ev[slot] = (s_dv, d1)
                d2 = sch.call("vector", "tensor_add", xcur, utmp[:, 0:512], xcur,
                              wait=[(s_dv, d1)], inc=s_dv)
                sch.dma("sync", xo[fo * 128:(fo + 1) * 128, cols], xcur,
                        s_st, wait=[(s_dv, d2)])
            prev_st = sch.val(s_st)
        st_all = sch.val(s_st)
        sch.barrier_wait("sync", s_st, st_all)

    return E.finish()


def host_l2_ffn(W_r, W_k, W_v, mu_r, mu_k):
    return {
        "wrT": np.ascontiguousarray(np.asarray(W_r, np.float32).T),
        "wkT": np.ascontiguousarray(np.asarray(W_k, np.float32).T),
        "wvT": np.ascontiguousarray(np.asarray(W_v, np.float32).T),
        "mur": chunk_cols(mu_r), "muk": chunk_cols(mu_k),
        "epsc_in": np.full((1, 1), EPS, np.float32),
        "onescb_in": np.ones((128, 1), BF),
        "onesf_in": _ones_row(np.float32),
    }


def build_l3_attn(stop_after=None, mega=None, io=None):
    E = BlockEnv(mega, "l3a_")
    nc = E.nc
    HS = 64
    if mega is None:
        xf = nc.dram_tensor("xf", [D, S], F32, kind="ExternalInput").ap()
        xo_in = nc.dram_tensor("xo_in", [D, TOK], F32, kind="ExternalInput").ap()
        xo = nc.dram_tensor("xo", [D, TOK], F32, kind="ExternalOutput").ap()
    else:
        xf = io.get("xf"); xo_in = io["xo_in"]; xo = io["xo"]
    wqT = E.dram_in("wqT", [D, HS], F32)
    wkT = E.dram_in("wkT", [D, HS], F32)
    wvT = E.dram_in("wvT", [D, HS], F32)
    bqc = E.dram_in("bqc", [HS, 1], F32)
    bkc = E.dram_in("bkc", [HS, 1], F32)
    bvr = E.dram_in("bvr", [1, HS], BF16)
    pwT = E.dram_in("pwT", [D, D], F32)
    pbc = E.dram_in("pbc", [128, FC], F32)
    kvis = E.dram_in("kvis", [128, NKT], F32)
    cms01 = E.dram_in("cms01", [4, 128, 512], BF16)
    sel_in = E.dram_in("sel_in", [128, HS], BF16)
    onesb_in = E.dram_in("onesb_in", [1, 128], BF16)
    onesf_in = E.dram_in("onesf_in", [1, 128], F32)
    onescb_in = E.dram_in("onescb_in", [128, 1], BF16)
    epsc_in = E.dram_in("epsc_in", [1, 1], F32)

    NQ3 = 4
    if True:
        _e = E.ctx.enter_context
        xn = E.sb("xn", [128, FC * S], BF16)
        qq = E.sb("qq", [HS, TOK], BF16)
        kk = E.sb("kk", [HS, S], BF16)
        vv = E.sb("vv", [128, NKT * HS], BF16)
        wq3 = E.sb("wq3", [128, FC * 3 * HS], BF16)
        pw = E.sb("pw", [128, FC * D], BF16)
        weff = E.sb("weff", [HS, D], BF16)
        sel = E.sb("sel", [128, HS], BF16)
        wei = E.sb("wei", [128, 4 * 512], BF16)
        outh = E.sb("outh", [HS, 512], BF16)
        bq3 = E.sb("bq3", [HS, 2], F32)
        bvr_s = E.sb("bvr_s", [1, HS], BF16)
        pbc_s = E.sb("pbc_s", [128, FC], F32)
        kvi = E.sb("kvi", [128, NKT], F32)
        cm1 = E.sb("cm1", [128, 4 * 512], BF16)
        onesb = E.sb("onesb", [1, 128], BF16)
        ones1f = E.sb("ones1f", [1, 128], F32)
        onescb = E.sb("onescb", [128, 1], BF16)
        epsc = E.sb("epsc", [1, 1], F32)
        stat = E.sb("stat", [1, 512], F32)
        stat2 = E.sb("stat2", [1, 512], F32)
        stat3 = E.sb("stat3", [1, 512], F32)
        xsq = E.sb("xsq", [128, 2 * 512], BF16)
        xs = E.sb("xs", [128, FC * 512], F32)
        utm = E.sb("utm", [128, 512], F32)
        ps_a = E.ps()
        ps_b = E.ps()
        ps_c = E.ps()
        ps_d = E.ps()
        ps_o = E.ps()
        s_ld = E.sem("s_ld")
        s_lg = E.sem("s_lg")
        s_pe = E.sem("s_pe")
        s_ev = E.sem("s_ev")
        s_dv = E.sem("s_dv")
        s_xs = E.sem("s_xs")
        s_st = E.sem("s_st")
        sch = E.sch
        E.start()

        for (dst, src) in ((onesb, onesb_in), (ones1f, onesf_in), (onescb, onescb_in),
                           (epsc, epsc_in), (sel, sel_in), (bvr_s, bvr),
                           (pbc_s, pbc), (kvi, kvis)):
            sch.dma("sync", dst[:], src[:, :], s_ld)
        for v_ in range(4):
            sch.dma("sync", cm1[:, v_ * 512:(v_ + 1) * 512], cms01[v_, :, :], s_ld)
        sch.dma("gpsimd", bq3[:, 0:1], bqc[:, :], s_lg)
        sch.dma("gpsimd", bq3[:, 1:2], bkc[:, :], s_lg)
        for c in range(FC):
            sch.dma("gpsimd", wq3[:, c * 3 * HS:c * 3 * HS + HS],
                    wqT[c * 128:(c + 1) * 128, :], s_lg)
            sch.dma("gpsimd", wq3[:, c * 3 * HS + HS:c * 3 * HS + 2 * HS],
                    wkT[c * 128:(c + 1) * 128, :], s_lg)
            sch.dma("gpsimd", wq3[:, c * 3 * HS + 2 * HS:c * 3 * HS + 3 * HS],
                    wvT[c * 128:(c + 1) * 128, :], s_lg)
            sch.dma("gpsimd", pw[:, c * D:(c + 1) * D],
                    pwT[c * 128:(c + 1) * 128, :], s_lg)
        for c in range(FC):
            if io is not None and "xf_load" in io:
                io["xf_load"](sch, c, xn[:, c * S:(c + 1) * S], s_lg)
            else:
                sch.dma("gpsimd", xn[:, c * S:(c + 1) * S],
                        xf[c * 128:(c + 1) * 128, :], s_lg)
        all_ld = sch.val(s_ld)
        all_lg = sch.val(s_lg)

        env = {"xn": xn, "onescb": onescb, "ones1f": ones1f, "epsc": epsc,
               "stat": stat, "stat2": stat2, "stat3": stat3, "xsq": xsq,
               "ps_sum": ps_a, "ps_sq": ps_b, "ps_ba": ps_c, "ps_bb": ps_d,
               "s_pe": s_pe, "s_ev": s_ev, "s_dv": s_dv,
               "xn_ready": [(s_ld, all_ld), (s_lg, all_lg)]}
        ln_dv = emit_ln_fm(sch, env, S)

        def xnc(c, cols):
            return xn[:, c * S:(c + 1) * S][:, cols]

        def _finish():
            d = sch.call("vector", "tensor_copy", xs[:, 0:512], xn[:, 0:512],
                         wait=[(s_dv, sch.val(s_dv))], inc=s_dv)
            for c in range(FC):
                sch.dma("sync", xo[c * 128:(c + 1) * 128, 0:512], xs[:, 0:512],
                        s_st, wait=[(s_dv, d)])
            sch.barrier_wait("sync", s_st, sch.val(s_st))
            with nc.Block() as block:
                sch.run(block)
            return nc

        if stop_after == "ln":
            return _finish()

        weff_ev0 = None
        for half in range(2):
            for c in range(FC):
                w0 = []
                if half == 0 and c == 0:
                    w0 = [(s_ld, all_ld), (s_lg, all_lg), (s_dv, ln_dv)]
                if half == 1 and c == 0:
                    w0 = [(s_ev, weff_ev0)]
                sch.call("tensor", "matmul", ps_o[0:HS, 0:512], sel[:],
                         pw[:, c * D + half * 512: c * D + (half + 1) * 512],
                         start=(c == 0), stop=(c == FC - 1),
                         wait=w0, inc=(s_pe if c == FC - 1 else None))
            pe = sch.val(s_pe)
            weff_ev0 = sch.call("scalar", "activation",
                                weff[:, half * 512:(half + 1) * 512],
                                ps_o[0:HS, 0:512], AF.Copy, wait=[(s_pe, pe)],
                                inc=s_ev)
        weff_ev = sch.val(s_ev)
        if stop_after == "weff":
            return _finish()

        pp = [ps_a, ps_b]
        pi = 0
        prev = [(s_ev, None), (s_ev, None)]
        for tb in range(S // 512):
            cols = slice(tb * 512, (tb + 1) * 512)
            for (which, woff) in (("k", HS), ("q", 0)):
                if which == "q" and tb < 4:
                    continue
                bank = pp[pi % 2]
                slot = pi % 2
                pi += 1
                w = [(s_dv, ln_dv), prev[slot]]
                for c in range(FC):
                    sch.call("tensor", "matmul", bank[0:HS, 0:512],
                             wq3[:, c * 3 * HS + woff:c * 3 * HS + woff + HS],
                             xnc(c, cols), start=(c == 0), stop=(c == FC - 1),
                             wait=(w if c == 0 else []),
                             inc=(s_pe if c == FC - 1 else None))
                pe = sch.val(s_pe)
                if which == "q":
                    dst = qq[:, (tb - 4) * 512:(tb - 3) * 512]
                    bcol = bq3[:, 0:1]
                else:
                    dst = kk[:, cols]
                    bcol = bq3[:, 1:2]
                dv = sch.call("vector", "tensor_scalar", dst, bank[0:HS, 0:512],
                              bcol, None, ALU.add, wait=[(s_pe, pe)], inc=s_dv)
                prev[slot] = (s_dv, dv)
        for kt in range(NKT):
            tcol = slice(kt * 128, (kt + 1) * 128)
            bank = pp[pi % 2]
            slot = pi % 2
            pi += 1
            w = [(s_dv, ln_dv), prev[slot]]
            for c in range(FC):
                sch.call("tensor", "matmul", bank[:, 0:HS],
                         xnc(c, tcol), wq3[:, c * 3 * HS + 2 * HS:c * 3 * HS + 3 * HS],
                         start=(c == 0), stop=False, wait=(w if c == 0 else []))
            sch.call("tensor", "matmul", bank[:, 0:HS], onesb[0:1, :],
                     bvr_s[0:1, :], start=False, stop=True, inc=s_pe)
            pe = sch.val(s_pe)
            dv = sch.call("vector", "tensor_scalar", vv[:, kt * HS:(kt + 1) * HS],
                          bank[:, 0:HS], kvi[:, kt:kt + 1], None, ALU.mult,
                          wait=[(s_pe, pe)], inc=s_dv)
            prev[slot] = (s_dv, dv)
        proj_dv = sch.val(s_dv)
        if stop_after == "proj":
            return _finish()

        sprev = [(s_dv, proj_dv), (s_dv, proj_dv), (s_dv, proj_dv), (s_dv, proj_dv)]
        prev_st = None
        ep_d1 = None
        for qc in range(NQ3):
            qcols = slice(qc * 512, (qc + 1) * 512)
            n_kt = 16 + 4 * (qc + 1)
            for c in range(FC):
                sch.dma("sync", xs[:, c * 512:(c + 1) * 512],
                        xo_in[c * 128:(c + 1) * 128, qcols], s_xs,
                        wait=([(s_st, prev_st)] if c == 0 else []))
            xs_ld = sch.val(s_xs)
            outh_dv = None
            for ki in range(n_kt):
                kt = ki
                wslot = ki % 4
                wt = wei[:, wslot * 512:(wslot + 1) * 512]
                w = [sprev[wslot], (s_dv, proj_dv)]
                pe_sc = sch.call("tensor", "matmul", ps_c[:, 0:512],
                                 kk[:, kt * 128:(kt + 1) * 128], qq[:, qcols],
                                 start=True, stop=True, wait=w, inc=s_pe)
                ev = sch.call("scalar", "activation", wt, ps_c[:, 0:512], AF.Copy,
                              wait=[(s_pe, pe_sc)], inc=s_ev)
                lvar = kt - 16 - 4 * qc
                dep = (s_ev, ev)
                if 0 <= lvar < 4:
                    dvm = sch.call("vector", "tensor_mul", wt, wt,
                                   cm1[:, lvar * 512:(lvar + 1) * 512],
                                   wait=[(s_ev, ev)], inc=s_dv)
                    dep = (s_dv, dvm)
                pe_av = sch.call("tensor", "matmul", ps_o[0:HS, 0:512],
                                 vv[:, kt * HS:(kt + 1) * HS], wt,
                                 start=(ki == 0), stop=(ki == n_kt - 1),
                                 wait=[dep], inc=s_pe)
                sprev[wslot] = (s_pe, pe_av)
            ev_oh = sch.call("scalar", "activation", outh[:], ps_o[0:HS, 0:512],
                             AF.Copy, wait=[(s_pe, sch.val(s_pe)), (s_ev, weff_ev)],
                             inc=s_ev)
            for fo in range(FC):
                pe = sch.call("tensor", "matmul", ps_d[:, 0:512],
                              weff[:, fo * 128:(fo + 1) * 128], outh[:],
                              start=True, stop=True,
                              wait=[(s_ev, ev_oh), (s_dv, ep_d1)], inc=s_pe)
                d1 = sch.call("vector", "tensor_scalar", utm[:, 0:512],
                              ps_d[:, 0:512], pbc_s[:, fo:fo + 1], None, ALU.add,
                              wait=[(s_pe, pe)] + ([(s_xs, xs_ld)] if fo == 0 else []),
                              inc=s_dv)
                xsl = xs[:, fo * 512:(fo + 1) * 512]
                ep_d1 = d1
                d2 = sch.call("vector", "tensor_add", xsl, utm[:, 0:512], xsl,
                              wait=[(s_dv, d1)], inc=s_dv)
                sch.dma("sync", xo[fo * 128:(fo + 1) * 128, qcols], xsl, s_st,
                        wait=[(s_dv, d2)])
            prev_st = sch.val(s_st)
        sch.barrier_wait("sync", s_st, sch.val(s_st))
    return E.finish()


def host_l3_attn(qkv_w, qkv_b, proj_w, proj_b):
    HS = 64
    qkv_w = np.asarray(qkv_w, np.float32)
    qkv_b = np.asarray(qkv_b, np.float32)
    sel = np.tile(np.eye(HS, dtype=BF), (2, 1))
    j = np.arange(128)[:, None]
    i = np.arange(512)[None, :]
    cms01 = np.stack([(j + v * 128 <= i).astype(BF) for v in range(4)])
    shared = {
        "wqT": np.ascontiguousarray(qkv_w[0:HS].T),
        "wkT": np.ascontiguousarray(qkv_w[HS:2 * HS].T),
        "wvT": np.ascontiguousarray(qkv_w[2 * HS:3 * HS].T),
        "bqc": qkv_b[0:HS].reshape(HS, 1).copy(),
        "bkc": qkv_b[HS:2 * HS].reshape(HS, 1).copy(),
        "bvr": qkv_b[2 * HS:3 * HS].reshape(1, HS).astype(BF),
        "pwT": np.ascontiguousarray(np.asarray(proj_w, np.float32).T),
        "pbc": chunk_cols(proj_b),
        "cms01": cms01, "sel_in": np.ascontiguousarray(sel),
        "onesb_in": _ones_row(BF), "onesf_in": _ones_row(np.float32),
        "onescb_in": np.ones((128, 1), BF),
        "epsc_in": np.full((1, 1), EPS, np.float32),
    }
    return shared


def host_l3_attn_core(x, core):
    b, h = core // 2, core % 2
    own = x[b, h * TOK:(h + 1) * TOK]
    other = x[b, (1 - h) * TOK:(2 - h) * TOK]
    xcat = np.concatenate([other, own], axis=0)
    kv = np.ones((128, NKT), np.float32)
    if h == 0:
        kv[:, 0:16] = 0.0
    return {"xf": fmajor(xcat), "xo_in": fmajor(own), "kvis": kv}


def build_l2_attn(mega=None, io=None):
    E = BlockEnv(mega, "l2a_")
    nc = E.nc
    NHP = 8
    if mega is None:
        xf = nc.dram_tensor("xf", [D, S], F32, kind="ExternalInput").ap()
        xo_in = nc.dram_tensor("xo_in", [D, TOK], F32, kind="ExternalInput").ap()
        xo = nc.dram_tensor("xo", [D, TOK], F32, kind="ExternalOutput").ap()
    else:
        xf = io.get("xf"); xo_in = io["xo_in"]; xo = io["xo"]
    wqT = E.dram_in("wqT", [D, D], F32)
    wkT = E.dram_in("wkT", [D, D], F32)
    wvT = E.dram_in("wvT", [D, D], F32)
    wpT = E.dram_in("wpT", [D, D], F32)
    pbc = E.dram_in("pbc", [128, FC], F32)
    onesb_in = E.dram_in("onesb_in", [1, 128], BF16)
    onesf_in = E.dram_in("onesf_in", [1, 128], F32)
    onescb_in = E.dram_in("onescb_in", [128, 1], BF16)
    onescf_in = E.dram_in("onescf_in", [128, 1], F32)
    epsc_in = E.dram_in("epsc_in", [1, 1], F32)
    ksc = E.dram_int("ksc", [NHP, 128, S], BF16)
    vsc = E.dram_int("vsc", [NHP, NKT, 128, 128], BF16)

    if True:
        _e = E.ctx.enter_context
        qq = E.sb("qq", [128, NHP * TOK], BF16)
        cxs = E.sb("cxs", [128, FC * TOK], BF16)
        wp = E.sb("wp", [128, FC * D], BF16)
        pbs = E.sb("pbs", [128, FC], F32)
        onesb = E.sb("onesb", [1, 128], BF16)
        ones1f = E.sb("ones1f", [1, 128], F32)
        onescb = E.sb("onescb", [128, 1], BF16)
        onescf = E.sb("onescf", [128, 1], F32)
        epsc = E.sb("epsc", [1, 1], F32)
        stat = E.sb("stat", [1, 512], F32)
        stat2 = E.sb("stat2", [1, 512], F32)
        stat3 = E.sb("stat3", [1, 512], F32)
        xsq = E.sb("xsq", [128, 2 * 512], BF16)
        ps_a = E.ps()
        ps_b = E.ps()
        ps_c = E.ps()
        ps_d = E.ps()
        ps_e = E.ps()
        ps_f = E.ps()
        ps_z = E.ps()
        ps_g = E.ps()
        s_ld = E.sem("s_ld")
        s_lg = E.sem("s_lg")
        s_pe = E.sem("s_pe")
        s_ev = E.sem("s_ev")
        s_dv = E.sem("s_dv")
        s_sc = E.sem_list("s_sc", 4)
        s_kv = E.sem_list("s_kv", 2)
        s_xs = E.sem("s_xs")
        s_st = E.sem("s_st")
        sch = E.sch
        E.start()

        for (dst, src) in ((onesb, onesb_in), (ones1f, onesf_in), (onescb, onescb_in),
                           (onescf, onescf_in), (epsc, epsc_in), (pbs, pbc)):
            sch.dma("sync", dst[:], src[:, :], s_ld)
        for c in range(FC):
            sch.dma("gpsimd", wp[:, c * D:(c + 1) * D],
                    wpT[c * 128:(c + 1) * 128, :], s_lg)

        with ExitStack() as ctx2:
            _e2 = ctx2.enter_context
            xn = _e2(nc.sbuf_tensor(E.nm("xn"), [128, FC * S], BF16))
            wq = _e2(nc.sbuf_tensor(E.nm("wq"), [128, FC * D], BF16))
            wk = _e2(nc.sbuf_tensor(E.nm("wk"), [128, FC * D], BF16))
            wv = _e2(nc.sbuf_tensor(E.nm("wv"), [128, FC * D], BF16))
            kevac = _e2(nc.sbuf_tensor(E.nm("kevac"), [128, 2 * 512], BF16))
            vevac = _e2(nc.sbuf_tensor(E.nm("vevac"), [128, 2 * 128], BF16))
            for (wt, wsrc) in ((wq, wqT), (wk, wkT), (wv, wvT)):
                for c in range(FC):
                    sch.dma("gpsimd", wt[:, c * D:(c + 1) * D],
                            wsrc[c * 128:(c + 1) * 128, :], s_lg)
            for c in range(FC):
                if io is not None and "xf_load" in io:
                    io["xf_load"](sch, c, xn[:, c * S:(c + 1) * S], s_lg)
                else:
                    sch.dma("gpsimd", xn[:, c * S:(c + 1) * S],
                            xf[c * 128:(c + 1) * 128, :], s_lg)
            all_ld = sch.val(s_ld)
            all_lg = sch.val(s_lg)

            env = {"xn": xn, "onescb": onescb, "ones1f": ones1f, "epsc": epsc,
                   "stat": stat, "stat2": stat2, "stat3": stat3, "xsq": xsq,
                   "ps_sum": ps_a, "ps_sq": ps_b, "ps_ba": ps_c, "ps_bb": ps_d,
                   "s_pe": s_pe, "s_ev": s_ev, "s_dv": s_dv,
                   "xn_ready": [(s_ld, all_ld), (s_lg, all_lg)]}
            ln_dv = emit_ln_fm(sch, env, S)

            def xnc(c, cols):
                return xn[:, c * S:(c + 1) * S][:, cols]

            pp = [ps_a, ps_b]
            pi = 0
            prev = [(s_ev, None), (s_ev, None)]
            kev_dma = [None, None]
            vev_dma = [None, None]
            for tb in range(S // 512):
                cols = slice(tb * 512, (tb + 1) * 512)
                for fo in range(FC):
                    bank = pp[pi % 2]
                    slot = pi % 2
                    pi += 1
                    w = [(s_dv, ln_dv), prev[slot]]
                    for c in range(FC):
                        sch.call("tensor", "matmul", bank[:, 0:512],
                                 wk[:, c * D + fo * 128: c * D + fo * 128 + 128],
                                 xnc(c, cols), start=(c == 0), stop=(c == FC - 1),
                                 wait=(w if c == 0 else []),
                                 inc=(s_pe if c == FC - 1 else None))
                    pe = sch.val(s_pe)
                    kslot = (tb * FC + fo) % 2
                    ev = sch.call("scalar", "activation",
                                  kevac[:, kslot * 512:(kslot + 1) * 512],
                                  bank[:, 0:512], AF.Copy,
                                  wait=[(s_pe, pe), (s_sc[kslot], kev_dma[kslot])],
                                  inc=s_ev)
                    prev[slot] = (s_ev, ev)
                    kev_dma[kslot] = sch.dma("gpsimd", ksc[fo, :, cols],
                                             kevac[:, kslot * 512:(kslot + 1) * 512],
                                             s_sc[kslot], wait=[(s_ev, ev)])
                if tb >= 4:
                    for fo in range(FC):
                        bank = pp[pi % 2]
                        slot = pi % 2
                        pi += 1
                        w = [(s_dv, ln_dv), prev[slot]]
                        for c in range(FC):
                            sch.call("tensor", "matmul", bank[:, 0:512],
                                     wq[:, c * D + fo * 128: c * D + fo * 128 + 128],
                                     xnc(c, cols), start=(c == 0), stop=(c == FC - 1),
                                     wait=(w if c == 0 else []),
                                     inc=(s_pe if c == FC - 1 else None))
                        pe = sch.val(s_pe)
                        ev = sch.call("scalar", "activation",
                                      qq[:, fo * TOK + (tb - 4) * 512: fo * TOK + (tb - 3) * 512],
                                      bank[:, 0:512], AF.Copy, wait=[(s_pe, pe)],
                                      inc=s_ev)
                        prev[slot] = (s_ev, ev)
            for kt in range(NKT):
                tcol = slice(kt * 128, (kt + 1) * 128)
                for fo in range(FC):
                    bank = pp[pi % 2]
                    slot = pi % 2
                    pi += 1
                    w = [(s_dv, ln_dv), prev[slot]]
                    for c in range(FC):
                        sch.call("tensor", "matmul", bank[:, 0:128],
                                 xnc(c, tcol),
                                 wv[:, c * D + fo * 128: c * D + (fo + 1) * 128],
                                 start=(c == 0), stop=(c == FC - 1),
                                 wait=(w if c == 0 else []),
                                 inc=(s_pe if c == FC - 1 else None))
                    pe = sch.val(s_pe)
                    vslot = (kt * FC + fo) % 2
                    ev = sch.call("scalar", "activation",
                                  vevac[:, vslot * 128:(vslot + 1) * 128],
                                  bank[:, 0:128], AF.Copy,
                                  wait=[(s_pe, pe), (s_sc[2 + vslot], vev_dma[vslot])],
                                  inc=s_ev)
                    prev[slot] = (s_ev, ev)
                    vev_dma[vslot] = sch.dma("gpsimd", vsc[fo, kt, :, :],
                                             vevac[:, vslot * 128:(vslot + 1) * 128],
                                             s_sc[2 + vslot], wait=[(s_ev, ev)])
            proj_pe = sch.val(s_pe)
            proj_ev = sch.val(s_ev)
            scr_done = [(x, sch.val(x)) for x in s_sc]
        with ExitStack() as ctx3:
            _e3 = ctx3.enter_context
            kst = _e3(nc.sbuf_tensor(E.nm("kst"), [128, 2 * S], BF16))
            vst = _e3(nc.sbuf_tensor(E.nm("vst"), [128, 2 * NKT * 128], BF16))
            zacc = _e3(nc.sbuf_tensor(E.nm("zacc"), [128, 2 * 512], F32))
            rzr = _e3(nc.sbuf_tensor(E.nm("rzr"), [1, 2 * 512], F32))
            rzb = _e3(nc.sbuf_tensor(E.nm("rzb"), [128, 512], F32))
            expt = _e3(nc.sbuf_tensor(E.nm("expt"), [128, 8 * 512], BF16))
            kv_ld = [None, None]
            kv_pe = [None, None]
            sprev2 = [(s_ev, None), (s_ev, None)]
            prev_dvc = None
            prev_evz = None
            prev_d2 = None
            prev_pz2 = None
            QCW = 512
            NQC2 = TOK // QCW
            first = True
            for hp in range(NHP):
                hslot = hp % 2
                lw = [(s_pe, kv_pe[hslot])]
                if first:
                    lw += scr_done + [(s_ev, proj_ev)]
                sch.dma("gpsimd", kst[:, hslot * S:(hslot + 1) * S],
                        ksc[hp, :, :], s_kv[hslot], wait=lw)
                kv_ld[hslot] = sch.dma(
                    "gpsimd", vst[:, hslot * NKT * 128:(hslot + 1) * NKT * 128]
                    .rearrange("p (n t) -> p n t", n=NKT),
                    vsc[hp, :, :, :].rearrange("n p t -> p n t"), s_kv[hslot])
                first = False
                for qc in range(NQC2):
                    qcols = slice(qc * QCW, (qc + 1) * QCW)
                    e_dv = [None] * 8
                    for kt in range(NKT):
                        eslot = kt % 4
                        for hh in range(2):
                            sb_ = [ps_a, ps_b][hh]
                            et = expt[:, (eslot * 2 + hh) * 512:(eslot * 2 + hh + 1) * 512]
                            qkw = [sprev2[hh], (s_kv[hslot], kv_ld[hslot])]
                            pe_sc = sch.call(
                                "tensor", "matmul", sb_[:, 0:512],
                                kst[:, hslot * S + kt * 128: hslot * S + (kt + 1) * 128][hh * 64:(hh + 1) * 64, :],
                                qq[:, hp * TOK:(hp + 1) * TOK][hh * 64:(hh + 1) * 64, qcols],
                                start=True, stop=True,
                                tile_position=(hh * 64, 0),
                                wait=qkw, inc=s_pe)
                            ev = sch.call("scalar", "activation", et, sb_[:, 0:512],
                                          AF.Exp, scale=0.125,
                                          wait=[(s_pe, pe_sc),
                                                (s_dv, e_dv[eslot * 2 + hh])],
                                          inc=s_ev)
                            sprev2[hh] = (s_ev, ev)
                            zw = [(s_ev, ev)]
                            if kt == 0:
                                zw.append((s_pe, prev_pz2))
                                zdv = sch.call("vector", "tensor_copy",
                                               zacc[:, hh * 512:(hh + 1) * 512], et,
                                               wait=zw, inc=s_dv)
                            else:
                                zdv = sch.call("vector", "tensor_add",
                                               zacc[:, hh * 512:(hh + 1) * 512],
                                               zacc[:, hh * 512:(hh + 1) * 512], et,
                                               wait=zw, inc=s_dv)
                            e_dv[eslot * 2 + hh] = zdv
                            avw = [(s_ev, ev)]
                            if kt == 0:
                                avw.append((s_dv, prev_dvc))
                            ctxb = [ps_e, ps_f][hh]
                            sch.call("tensor", "matmul",
                                     ctxb[hh * 64:(hh + 1) * 64, 0:512],
                                     vst[:, hslot * NKT * 128 + kt * 128 + hh * 64:
                                         hslot * NKT * 128 + kt * 128 + (hh + 1) * 64],
                                     et, start=(kt == 0), stop=(kt == NKT - 1),
                                     tile_position=(0, hh * 64),
                                     wait=avw,
                                     inc=(s_pe if kt == NKT - 1 and hh == 1 else None))
                    av_pe = sch.val(s_pe)
                    z_dv = sch.val(s_dv)
                    pz = sch.call("tensor", "matmul", ps_z[0:1, 0:512], onescf[:, 0:1],
                                  zacc[:, 0:512], start=True, stop=True,
                                  wait=[(s_dv, z_dv), (s_ev, prev_evz)], inc=s_pe)
                    pz2 = sch.call("tensor", "matmul", ps_g[0:1, 0:512], onescf[:, 0:1],
                                   zacc[:, 512:1024], start=True, stop=True,
                                   wait=[(s_dv, prev_d2)], inc=s_pe)
                    prev_pz2 = pz2
                    d1 = sch.call("vector", "reciprocal", rzr[0:1, 0:512],
                                  ps_z[0:1, 0:512], wait=[(s_pe, pz)], inc=s_dv)
                    d2 = sch.call("vector", "reciprocal", rzr[0:1, 512:1024],
                                  ps_g[0:1, 0:512], wait=[(s_pe, pz2)], inc=s_dv)
                    prev_d2 = d2
                    bp1 = sch.call("tensor", "matmul", ps_z[0:64, 0:512],
                                   ones1f[0:1, 0:64], rzr[0:1, 0:512],
                                   start=True, stop=True,
                                   wait=[(s_dv, d1), (s_ev, prev_evz)], inc=s_pe)
                    bp2 = sch.call("tensor", "matmul", ps_z[64:128, 0:512],
                                   ones1f[0:1, 0:64], rzr[0:1, 512:1024],
                                   start=True, stop=True, wait=[(s_dv, d2)],
                                   tile_position=(0, 64), inc=s_pe)
                    evz = sch.call("scalar", "activation", rzb[:], ps_z[:, 0:512],
                                   AF.Copy, wait=[(s_pe, bp2)], inc=s_ev)
                    prev_evz = evz
                    cxsl = cxs[:, hp * TOK:(hp + 1) * TOK][:, qcols]
                    sch.call("vector", "tensor_mul", cxsl[0:64, :],
                             ps_e[0:64, 0:512], rzb[0:64, :],
                             wait=[(s_ev, evz), (s_pe, av_pe)], inc=s_dv)
                    dvc = sch.call("vector", "tensor_mul", cxsl[64:128, :],
                                   ps_f[64:128, 0:512], rzb[64:128, :], inc=s_dv)
                    prev_dvc = dvc
                kv_pe[hslot] = sch.val(s_pe)
            ctx_all_dv = sch.val(s_dv)
        with ExitStack() as ctx4:
            _e4 = ctx4.enter_context
            xs = _e4(nc.sbuf_tensor(E.nm("xs"), [128, FC * 512], F32))
            utm = _e4(nc.sbuf_tensor(E.nm("utm"), [128, 512], F32))
            pp2 = [ps_a, ps_b]
            pi2 = 0
            oprev = [(s_dv, None), (s_dv, None)]
            prev_st = None
            scope_gate = [(s_dv, ctx_all_dv), (s_pe, sch.val(s_pe)),
                          (s_kv[0], sch.val(s_kv[0])), (s_kv[1], sch.val(s_kv[1])),
                          (s_ev, sch.val(s_ev))]
            for qc in range(4):
                qcols = slice(qc * 512, (qc + 1) * 512)
                for c in range(FC):
                    sch.dma("sync", xs[:, c * 512:(c + 1) * 512],
                            xo_in[c * 128:(c + 1) * 128, qcols], s_xs,
                            wait=((scope_gate if qc == 0 else [(s_st, prev_st)]) if c == 0 else []))
                xs_ld = sch.val(s_xs)
                for fo in range(FC):
                    bank = pp2[pi2 % 2]
                    slot = pi2 % 2
                    pi2 += 1
                    w = [(s_dv, ctx_all_dv), oprev[slot]]
                    for c in range(FC):
                        sch.call("tensor", "matmul", bank[:, 0:512],
                                 wp[:, c * D + fo * 128: c * D + fo * 128 + 128],
                                 cxs[:, c * TOK:(c + 1) * TOK][:, qcols],
                                 start=(c == 0), stop=(c == FC - 1),
                                 wait=(w if c == 0 else []),
                                 inc=(s_pe if c == FC - 1 else None))
                    pe = sch.val(s_pe)
                    d1 = sch.call("vector", "tensor_scalar", utm[:, 0:512],
                                  bank[:, 0:512], pbs[:, fo:fo + 1], None, ALU.add,
                                  wait=[(s_pe, pe)] + ([(s_xs, xs_ld)] if fo == 0 else []),
                                  inc=s_dv)
                    oprev[slot] = (s_dv, d1)
                    xsl = xs[:, fo * 512:(fo + 1) * 512]
                    d2 = sch.call("vector", "tensor_add", xsl, utm[:, 0:512], xsl,
                                  wait=[(s_dv, d1)], inc=s_dv)
                    sch.dma("sync", xo[fo * 128:(fo + 1) * 128, qcols], xsl, s_st,
                            wait=[(s_dv, d2)])
                    if io is not None and "col_hook" in io and qc == 3:
                        io["col_hook"](sch, fo, xsl[:, 511:512])
                prev_st = sch.val(s_st)
            sch.barrier_wait("sync", s_st, sch.val(s_st))
    return E.finish()


def host_l2_attn(w_qkv, w_proj, b_proj):
    w_qkv = np.asarray(w_qkv, np.float32)
    return {
        "wqT": np.ascontiguousarray(w_qkv[0:D].T),
        "wkT": np.ascontiguousarray(w_qkv[D:2 * D].T),
        "wvT": np.ascontiguousarray(w_qkv[2 * D:3 * D].T),
        "wpT": np.ascontiguousarray(np.asarray(w_proj, np.float32).T),
        "pbc": chunk_cols(b_proj),
        "onesb_in": _ones_row(BF), "onesf_in": _ones_row(np.float32),
        "onescb_in": np.ones((128, 1), BF),
        "onescf_in": np.ones((128, 1), np.float32),
        "epsc_in": np.full((1, 1), EPS, np.float32),
    }


DFF = 4096
NFF = DFF // 128


def build_l3_ffn(mega=None, io=None):
    E = BlockEnv(mega, "l3f_")
    nc = E.nc
    if mega is None:
        xi = nc.dram_tensor("xi", [D, TOK], F32, kind="ExternalInput").ap()
        xo = nc.dram_tensor("xo", [D, TOK], F32, kind="ExternalOutput").ap()
    else:
        xi = io["xi"]; xo = io["xo"]
    rw1_in = E.dram_in("rw1_in", [D, DFF], F32)
    rb1_in = E.dram_in("rb1_in", [128, NFF], F32)
    rw2_in = E.dram_in("rw2_in", [DFF, NE], F32)
    rb2_in = E.dram_in("rb2_in", [1, NE], BF16)
    if io is not None and "w1b" in io:
        w1_in = io["w1b"]
        w2_in = io["w2b"]
    else:
        w1_in = E.dram_in("w1_in", [NE, D, DFF], F32)
        w2_in = E.dram_in("w2_in", [NE, DFF, D], F32)
    wpc = io.get("wpc") if io is not None else None
    b1_in = E.dram_in("b1_in", [NE, 128, NFF], F32)
    b2_in = E.dram_in("b2_in", [1, NE * D], F32)
    onesb_in = E.dram_in("onesb_in", [1, 128], BF16)
    onesf_in = E.dram_in("onesf_in", [1, 128], F32)
    onescb_in = E.dram_in("onescb_in", [128, 1], BF16)
    identf_in = E.dram_in("identf_in", [128, 128], F32)
    epsc_in = E.dram_in("epsc_in", [1, 1], F32)

    if True:
        _e = E.ctx.enter_context
        xn = E.sb("xn", [128, FC * TOK], BF16)
        w1 = E.sb("w1", [128, FC * DFF], BF16)
        w2 = E.sb("w2", [128, NFF * D], BF16)
        TC3 = 256
        NTC3 = TOK // TC3
        hh = E.sb("hh", [128, NFF * TC3], BF16)
        acc = E.sb("acc", [128, 2 * D], F32)
        xs = E.sb("xs", [128, 2 * 128], F32)
        rw2 = E.sb("rw2", [128, NFF * NE], BF16)
        rb1c = E.sb("rb1c", [128, NFF], F32)
        b1c = E.sb("b1c", [128, 2 * NFF], F32)
        b2r = E.sb("b2r", [1, 2 * D], BF16)
        rb2r = E.sb("rb2r", [1, NE], BF16)
        lg = E.sb("lg", [128, 16 * NE], F32)
        ww = E.sb("ww", [128, 16 * NE], F32)
        m1c = E.sb("m1c", [128, 8], F32)
        eqb = E.sb("eqb", [128, 2 * NE], F32)
        onesb = E.sb("onesb", [1, 128], BF16)
        ones1f = E.sb("ones1f", [1, 128], F32)
        onescb = E.sb("onescb", [128, 1], BF16)
        identf = E.sb("identf", [128, 128], F32)
        epsc = E.sb("epsc", [1, 1], F32)
        stat = E.sb("stat", [1, 512], F32)
        stat2 = E.sb("stat2", [1, 512], F32)
        stat3 = E.sb("stat3", [1, 512], F32)
        xsq = E.sb("xsq", [128, 2 * 512], BF16)
        ps_a = E.ps()
        ps_b = E.ps()
        ps_c = E.ps()
        ps_d = E.ps()
        ps_e = E.ps()
        ps_f = E.ps()
        s_ld = E.sem("s_ld")
        s_lg = E.sem("s_lg")
        s_pe = E.sem("s_pe")
        s_ev = E.sem("s_ev")
        s_dv = E.sem("s_dv")
        s_w1 = E.sem("s_w1")
        s_w2 = E.sem("s_w2")
        s_w2s = E.sem("s_w2s")
        s_xs = E.sem("s_xs")
        s_st = E.sem("s_st")
        sch = E.sch
        E.start()

        for (dst, src) in ((onesb, onesb_in), (ones1f, onesf_in), (onescb, onescb_in),
                           (identf, identf_in), (epsc, epsc_in), (rb1c, rb1_in),
                           (rb2r, rb2_in)):
            sch.dma("sync", dst[:], src[:, :], s_ld)
        for c in range(NFF):
            sch.dma("gpsimd", rw2[:, c * NE:(c + 1) * NE],
                    rw2_in[c * 128:(c + 1) * 128, :], s_lg)
        for c in range(FC):
            sch.dma("gpsimd", xn[:, c * TOK:(c + 1) * TOK],
                    xi[c * 128:(c + 1) * 128, :], s_lg)
        rw1_ld = sch.dma("gpsimd", w1[:, :].rearrange("p (c f) -> p c f", c=FC),
                         rw1_in[:, :].rearrange("(c p) f -> p c f", p=128), s_w1)
        all_ld = sch.val(s_ld)
        all_lg = sch.val(s_lg)

        env = {"xn": xn, "onescb": onescb, "ones1f": ones1f, "epsc": epsc,
               "stat": stat, "stat2": stat2, "stat3": stat3, "xsq": xsq,
               "ps_sum": ps_a, "ps_sq": ps_b, "ps_ba": ps_c, "ps_bb": ps_d,
               "s_pe": s_pe, "s_ev": s_ev, "s_dv": s_dv,
               "xn_ready": [(s_ld, all_ld), (s_lg, all_lg)]}
        ln_dv = emit_ln_fm(sch, env, TOK)

        def xnc(c, cols):
            return xn[:, c * TOK:(c + 1) * TOK][:, cols]

        pp = [ps_a, ps_b]
        pi = 0
        hprev = [(s_dv, None), (s_dv, None)]

        def h_phase(e_idx, tc, wsem, wld, bcol_fn, dst_dv_list):
            nonlocal pi
            cols = slice(tc * TC3, (tc + 1) * TC3)
            last = None
            for gf in range(NFF):
                bank = pp[pi % 2]
                slot = pi % 2
                pi += 1
                w = [(wsem, wld), hprev[slot], (s_dv, ln_dv)]
                for c in range(FC):
                    sch.call("tensor", "matmul", bank[:, 0:TC3],
                             w1[:, c * DFF + gf * 128: c * DFF + gf * 128 + 128],
                             xnc(c, cols), start=(c == 0), stop=(c == FC - 1),
                             wait=(w if c == 0 else []),
                             inc=(s_pe if c == FC - 1 else None))
                pe = sch.val(s_pe)
                d = sch.call("vector", "tensor_scalar", hh[:, gf * TC3:(gf + 1) * TC3],
                             bank[:, 0:TC3], bcol_fn(gf), 0.0, ALU.add, ALU.max,
                             wait=[(s_pe, pe)], inc=s_dv)
                hprev[slot] = (s_dv, d)
                last = d
            return last

        w_dv = None
        score_pe = None
        for tc in range(NTC3):
            hdv = h_phase(None, tc, s_w1, rw1_ld, lambda gf: rb1c[:, gf:gf + 1], None)
            for tt in range(2):
                trow = slice(tt * 128, (tt + 1) * 128)
                ti = tc * 2 + tt
                for c in range(NFF):
                    sch.call("tensor", "matmul", ps_e[:, ti * NE:(ti + 1) * NE],
                             hh[:, c * TC3:(c + 1) * TC3][:, trow],
                             rw2[:, c * NE:(c + 1) * NE],
                             start=(c == 0), stop=False,
                             wait=([(s_dv, hdv)] if c == 0 else []))
                score_pe = sch.call("tensor", "matmul",
                                    ps_e[:, ti * NE:(ti + 1) * NE],
                                    onesb[0:1, :], rb2r[0:1, :],
                                    start=False, stop=True, inc=s_pe)
        dv = sch.call("vector", "tensor_copy", lg[:], ps_e[:, 0:16 * NE],
                      wait=[(s_pe, score_pe)], inc=s_dv)
        for ti in range(16):
            l8 = lg[:, ti * NE:(ti + 1) * NE]
            m1 = m1c[:, 0:1]; m2 = m1c[:, 1:2]; dd = m1c[:, 2:3]
            s1 = m1c[:, 3:4]; s2 = m1c[:, 4:5]
            eq1 = eqb[:, 0:NE]; eq2 = eqb[:, NE:2 * NE]
            c_ = sch.call("vector", "reduce_max", m1, l8, AX.X, inc=s_dv)
            c_ = sch.call("vector", "tensor_scalar", eq1, l8, m1, None,
                          ALU.is_equal, inc=s_dv)
            c_ = sch.call("vector", "scalar_tensor_tensor", eq2, eq1, NEG, l8,
                          ALU.mult, ALU.add, inc=s_dv)
            c_ = sch.call("vector", "reduce_max", m2, eq2, AX.X, inc=s_dv)
            c_ = sch.call("vector", "tensor_sub", dd, m1, m2, inc=s_dv)
            dv2 = sch.call("vector", "tensor_scalar", eq2, l8, m2, None,
                           ALU.is_equal, inc=s_dv)
            ev = sch.call("scalar", "activation", s1, dd, AF.Sigmoid,
                          wait=[(s_dv, dv2)], inc=s_ev)
            c_ = sch.call("vector", "tensor_scalar", s2, s1, -1.0, 1.0, ALU.mult,
                          ALU.add, wait=[(s_ev, ev)], inc=s_dv)
            c_ = sch.call("vector", "tensor_scalar", eq1, eq1, s1, None, ALU.mult,
                          inc=s_dv)
            w_dv = sch.call("vector", "scalar_tensor_tensor",
                            ww[:, ti * NE:(ti + 1) * NE], eq2, s2, eq1,
                            ALU.mult, ALU.add, inc=s_dv)

        dp = [ps_c, ps_d]
        di = 0
        dn_prev = [(s_dv, None), (s_dv, None)]
        w1_pe = None
        w2_pe = [None, None]
        b1_dv = [None, None]
        b2_pe = [None, None]
        acc_tr_pe = None
        prev_st = None
        w1_ld = rw1_ld
        for tc in range(NTC3):
            for e in range(NE):
                bslot = e % 2
                w1_ld = sch.dma("gpsimd", w1[:, :].rearrange("p (c f) -> p c f", c=FC),
                                w1_in[e, :, :].rearrange("(c p) f -> p c f", p=128),
                                s_w1, wait=[(s_pe, w1_pe), (s_dv, w_dv)]
                                + ([wpc] if wpc else []))
                w2_ld = sch.dma("sync", w2[:, :].rearrange("p (c f) -> p c f", c=NFF),
                                w2_in[e, :, :].rearrange("(c p) f -> p c f", p=128),
                                s_w2s, wait=[(s_pe, w2_pe[0]), (s_pe, w2_pe[1])]
                                + ([wpc] if wpc else []))
                sch.dma("gpsimd", b1c[:, bslot * NFF:(bslot + 1) * NFF],
                        b1_in[e, :, :], s_w1, wait=[(s_dv, b1_dv[bslot])])
                b1_ld = sch.val(s_w1)
                b2_ld = sch.dma("gpsimd", b2r[0:1, bslot * D:(bslot + 1) * D],
                                b2_in[0:1, e * D:(e + 1) * D], s_w2,
                                wait=[(s_pe, b2_pe[bslot])])
                hdv = h_phase(e, tc, s_w1, b1_ld,
                              lambda gf, bslot=bslot: b1c[:, bslot * NFF + gf:bslot * NFF + gf + 1],
                              None)
                b1_dv[bslot] = hdv
                w1_pe = sch.val(s_pe)
                for tt in range(2):
                    trow = slice(tt * 128, (tt + 1) * 128)
                    wcol = ww[:, (tc * 2 + tt) * NE + e:(tc * 2 + tt) * NE + e + 1]
                    for half in range(2):
                        bank = dp[di % 2]
                        slot = di % 2
                        di += 1
                        w = [(s_w2, b2_ld), (s_w2s, w2_ld), (s_dv, hdv),
                             dn_prev[slot]]
                        for c in range(NFF):
                            sch.call("tensor", "matmul", bank[:, 0:512],
                                     hh[:, c * TC3:(c + 1) * TC3][:, trow],
                                     w2[:, c * D + half * 512: c * D + (half + 1) * 512],
                                     start=(c == 0), stop=False,
                                     wait=(w if c == 0 else []))
                        pe2 = sch.call("tensor", "matmul", bank[:, 0:512],
                                       onesb[0:1, 0:128],
                                       b2r[0:1, bslot * D + half * 512: bslot * D + (half + 1) * 512],
                                       start=False, stop=True, inc=s_pe)
                        w2_pe[slot] = pe2
                        b2_pe[bslot] = pe2
                        aslice = acc[:, tt * D + half * 512: tt * D + (half + 1) * 512]
                        if e == 0:
                            d = sch.call("vector", "tensor_scalar", aslice,
                                         bank[:, 0:512], wcol, None, ALU.mult,
                                         wait=[(s_pe, pe2), (s_pe, acc_tr_pe)],
                                         inc=s_dv)
                        else:
                            d = sch.call("vector", "scalar_tensor_tensor", aslice,
                                         bank[:, 0:512], wcol, aslice,
                                         ALU.mult, ALU.add,
                                         wait=[(s_pe, pe2)], inc=s_dv)
                        dn_prev[slot] = (s_dv, d)
            acc_dv = sch.val(s_dv)
            uev = None
            xss = None
            for tt in range(2):
                for fo in range(FC):
                    xslot = (tt * FC + fo) % 2
                    xss = sch.dma("sync", xs[:, xslot * 128:(xslot + 1) * 128],
                                  xi[fo * 128:(fo + 1) * 128,
                                     tc * TC3 + tt * 128: tc * TC3 + (tt + 1) * 128],
                                  s_xs, wait=[(s_st, prev_st)])
                    pe_t = sch.call("tensor", "matmul", ps_f[:, 0:128],
                                    acc[:, tt * D + fo * 128: tt * D + (fo + 1) * 128],
                                    identf[:, :], start=True, stop=True,
                                    wait=[(s_dv, acc_dv), (s_dv, uev)], inc=s_pe)
                    xsl = xs[:, xslot * 128:(xslot + 1) * 128]
                    uev = sch.call("vector", "tensor_add", xsl, ps_f[:, 0:128], xsl,
                                   wait=[(s_pe, pe_t), (s_xs, xss)], inc=s_dv)
                    sch.dma("sync", xo[fo * 128:(fo + 1) * 128,
                                       tc * TC3 + tt * 128: tc * TC3 + (tt + 1) * 128],
                            xsl, s_st, wait=[(s_dv, uev)])
                    prev_st = sch.val(s_st)
            acc_tr_pe = sch.val(s_pe)
        sch.barrier_wait("sync", s_st, sch.val(s_st))
    return E.finish()


def host_l3_ffn(rw1, rb1, rw2, rb2, W1, b1, W2, b2):
    b1 = np.asarray(b1, np.float32)
    b1c = np.stack([chunk_cols(b1[e]) for e in range(NE)])
    return {
        "rw1_in": np.ascontiguousarray(np.asarray(rw1, np.float32)),
        "rb1_in": chunk_cols(rb1),
        "rw2_in": np.ascontiguousarray(np.asarray(rw2, np.float32)),
        "rb2_in": np.asarray(rb2, np.float32).reshape(1, -1).astype(BF),
        "w1_in": np.ascontiguousarray(np.asarray(W1, np.float32)),
        "b1_in": np.ascontiguousarray(b1c),
        "w2_in": np.ascontiguousarray(np.asarray(W2, np.float32)),
        "b2_in": np.asarray(b2, np.float32).reshape(1, -1).copy(),
        "onesb_in": _ones_row(BF), "onesf_in": _ones_row(np.float32),
        "onescb_in": np.ones((128, 1), BF),
        "identf_in": np.eye(128, dtype=np.float32),
        "epsc_in": np.full((1, 1), EPS, np.float32),
    }


L1P = 640
WTAB = 3328
HS1 = 64


def build_l1_attn(mega=None, io=None, pfx="l1a_"):
    E = BlockEnv(mega, pfx)
    nc = E.nc
    if mega is None:
        xf = nc.dram_tensor("xf", [D, S], F32, kind="ExternalInput").ap()
        co = nc.dram_tensor("co", [256, S], F32, kind="ExternalOutput").ap()
    else:
        xf = io.get("xf"); co = io["co"]
    wqT = E.dram_in("wqT", [D, 256], F32)
    wkT = E.dram_in("wkT", [D, 256], F32)
    wvT = E.dram_in("wvT", [D, 256], F32)
    bqc = E.dram_in("bqc", [128, 2], F32)
    bkc = E.dram_in("bkc", [128, 2], F32)
    bvr = E.dram_in("bvr", [1, 256], BF16)
    ept2 = E.dram_in("ept2", [128, WTAB], F32)
    eptr2 = E.dram_in("eptr2", [128, WTAB], F32)
    onesb_in = E.dram_in("onesb_in", [1, 128], BF16)
    onesf_in = E.dram_in("onesf_in", [1, 128], F32)
    onescf_in = E.dram_in("onescf_in", [128, 1], F32)
    onescb_in = E.dram_in("onescb_in", [128, 1], BF16)
    onestb_in = E.dram_in("onestb_in", [1, 512], BF16)
    ident_in = E.dram_in("ident_in", [128, 128], BF16)
    epsc_in = E.dram_in("epsc_in", [1, 1], F32)
    qdsc = E.dram_int("qdsc", [4, S, WTAB], BF16)
    kdsc = E.dram_int("kdsc", [4, S, WTAB], BF16)

    NLC = S // 512
    NTC7 = 7
    if True:
        _e = E.ctx.enter_context
        xn = E.sb("xn", [128, FC * S], BF16)
        qhh = E.sb("qhh", [128, 2 * S], BF16)
        khh = E.sb("khh", [128, 2 * S], BF16)
        vhh = E.sb("vhh", [128, 2 * S], BF16)
        epb2 = E.sb("epb2", [128, WTAB], BF16)
        eprb2 = E.sb("eprb2", [128, WTAB], BF16)
        bq3 = E.sb("bq3", [128, 4], F32)
        bvrs = E.sb("bvrs", [1, 256], BF16)
        onesb = E.sb("onesb", [1, 128], BF16)
        ones1f = E.sb("ones1f", [1, 128], F32)
        onescf = E.sb("onescf", [128, 1], F32)
        onescb = E.sb("onescb", [128, 1], BF16)
        onestb = E.sb("onestb", [1, 512], BF16)
        ident = E.sb("ident", [128, 128], BF16)
        epsc = E.sb("epsc", [1, 1], F32)
        stat = E.sb("stat", [1, 512], F32)
        stat2 = E.sb("stat2", [1, 512], F32)
        stat3 = E.sb("stat3", [1, 512], F32)
        xsq = E.sb("xsq", [128, 2 * 512], BF16)
        ps_a = E.ps()
        ps_b = E.ps()
        ps_c = E.ps()
        ps_d = E.ps()
        ps_s = E.ps()
        ps_t = E.ps()
        ps_x = E.ps()
        ps_z = E.ps()
        s_ld = E.sem("s_ld")
        s_lg = E.sem("s_lg")
        s_pe = E.sem("s_pe")
        s_ev = E.sem("s_ev")
        s_dv = E.sem("s_dv")
        s_tb = E.sem_list("s_tb", 2)
        s_ga = E.sem("s_ga")
        s_gb = E.sem_list("s_gb", 4)
        s_st = E.sem("s_st")
        sch = E.sch
        E.start()

        sch.dma("sync", onesb[:], onesb_in[:, :], s_ld)
        sch.dma("sync", ones1f[:], onesf_in[:, :], s_ld)
        sch.dma("sync", onescf[:], onescf_in[:, :], s_ld)
        sch.dma("sync", onescb[:], onescb_in[:, :], s_ld)
        sch.dma("sync", onestb[:], onestb_in[:, :], s_ld)
        sch.dma("sync", ident[:], ident_in[:, :], s_ld)
        sch.dma("sync", epsc[:], epsc_in[:, :], s_ld)
        sch.dma("sync", bq3[:, 0:2], bqc[:, :], s_ld)
        sch.dma("sync", bq3[:, 2:4], bkc[:, :], s_ld)
        sch.dma("gpsimd", bvrs[:], bvr[:, :], s_lg)
        sch.dma("gpsimd", epb2[:], ept2[:, :], s_lg)
        sch.dma("gpsimd", eprb2[:], eptr2[:, :], s_lg)
        ctx2 = E.ctx.enter_context(ExitStack())
        _e2 = ctx2.enter_context
        wts = _e2(nc.sbuf_tensor(E.nm("wts"), [128, FC * 3 * 256], BF16))
        tevac = _e2(nc.sbuf_tensor(E.nm("tevac"), [128, 4 * WTAB], BF16))
        for (wdst, wsrc) in ((0, wqT), (1, wkT), (2, wvT)):
            for c in range(FC):
                sch.dma("gpsimd", wts[:, c * 768 + wdst * 256: c * 768 + wdst * 256 + 256],
                        wsrc[c * 128:(c + 1) * 128, :], s_lg)
        for c in range(FC):
            if io is not None and "xf_load" in io:
                io["xf_load"](sch, c, xn[:, c * S:(c + 1) * S], s_lg)
            else:
                sch.dma("gpsimd", xn[:, c * S:(c + 1) * S],
                        xf[c * 128:(c + 1) * 128, :], s_lg)
        all_ld = sch.val(s_ld)
        all_lg = sch.val(s_lg)

        env = {"xn": xn, "onescb": onescb, "ones1f": ones1f, "epsc": epsc,
               "stat": stat, "stat2": stat2, "stat3": stat3, "xsq": xsq,
               "ps_sum": ps_a, "ps_sq": ps_b, "ps_ba": ps_c, "ps_bb": ps_d,
               "s_pe": s_pe, "s_ev": s_ev, "s_dv": s_dv,
               "xn_ready": [(s_ld, all_ld), (s_lg, all_lg)]}
        ln_dv = emit_ln_fm(sch, env, S)

        def xnc(c, cols):
            return xn[:, c * S:(c + 1) * S][:, cols]

        pp = [ps_a, ps_b]
        pi = 0
        prev = [(s_ev, None), (s_ev, None)]
        for tb in range(S // 512):
            cols = slice(tb * 512, (tb + 1) * 512)
            for hp in range(2):
                for which in range(2):
                    bank = pp[pi % 2]
                    slot = pi % 2
                    pi += 1
                    w = [(s_dv, ln_dv), prev[slot]]
                    for c in range(FC):
                        sch.call("tensor", "matmul", bank[:, 0:512],
                                 wts[:, c * 768 + which * 256 + hp * 128:
                                     c * 768 + which * 256 + (hp + 1) * 128],
                                 xnc(c, cols), start=(c == 0), stop=(c == FC - 1),
                                 wait=(w if c == 0 else []),
                                 inc=(s_pe if c == FC - 1 else None))
                    pe = sch.val(s_pe)
                    dst = (qhh if which == 0 else khh)[:, hp * S:(hp + 1) * S][:, cols]
                    dv = sch.call("vector", "tensor_scalar", dst, bank[:, 0:512],
                                  bq3[:, which * 2 + hp:which * 2 + hp + 1], None,
                                  ALU.add, wait=[(s_pe, pe)], inc=s_dv)
                    prev[slot] = (s_dv, dv)
        for kt in range(NKT):
            tcol = slice(kt * 128, (kt + 1) * 128)
            for hp in range(2):
                bank = pp[pi % 2]
                slot = pi % 2
                pi += 1
                w = [(s_dv, ln_dv), prev[slot]]
                for c in range(FC):
                    sch.call("tensor", "matmul", bank[:, 0:128],
                             xnc(c, tcol),
                             wts[:, c * 768 + 512 + hp * 128: c * 768 + 512 + (hp + 1) * 128],
                             start=(c == 0), stop=False,
                             wait=(w if c == 0 else []))
                sch.call("tensor", "matmul", bank[:, 0:128], onesb[0:1, :],
                         bvrs[0:1, hp * 128:(hp + 1) * 128],
                         start=False, stop=True, inc=s_pe)
                pe = sch.val(s_pe)
                ev = sch.call("scalar", "activation",
                              vhh[:, (kt * 2 + hp) * 128:(kt * 2 + hp + 1) * 128],
                              bank[:, 0:128], AF.Copy, wait=[(s_pe, pe)], inc=s_ev)
                prev[slot] = (s_ev, ev)
        proj_dv = sch.val(s_dv)
        proj_ev = sch.val(s_ev)

        NTC7 = 7
        tprev = [(s_ev, None), (s_ev, None)]
        tev_dma = [None, None]
        bank_ev = None
        for hp in range(2):
            for (tbl, src, scr) in ((eprb2, qhh, qdsc), (epb2, khh, kdsc)):
                for lt in range(NKT):
                    tcol = slice(lt * 128, (lt + 1) * 128)
                    dslot = lt % 2
                    w0 = [(s_dv, proj_dv), (s_ev, proj_ev), tprev[dslot],
                          (s_ev, bank_ev), (s_tb[dslot], tev_dma[dslot])]
                    sA = tevac[:, (2 * dslot) * WTAB:(2 * dslot + 1) * WTAB]
                    sB = tevac[:, (2 * dslot + 1) * WTAB:(2 * dslot + 2) * WTAB]
                    last_ev = None
                    for ch in range(NTC7):
                        cw = 512 if ch < 6 else WTAB - 6 * 512
                        for hh in range(2):
                            bank = [ps_s, ps_t][hh]
                            wch = []
                            if ch == 0 and hh == 0:
                                wch = list(w0)
                            elif hh == 0 and last_ev is not None:
                                wch = [(s_ev, last_ev)]
                            sch.call("tensor", "matmul", bank[:, 0:cw],
                                     src[:, hp * S:(hp + 1) * S][:, tcol][hh * 64:(hh + 1) * 64, :],
                                     tbl[hh * 64:(hh + 1) * 64, ch * 512:ch * 512 + cw],
                                     start=True, stop=True,
                                     tile_position=(hh * 64, 0),
                                     wait=wch, inc=(s_pe if hh == 1 else None))
                        pe = sch.val(s_pe)
                        sch.call("scalar", "activation", sA[:, ch * 512:ch * 512 + cw],
                                 ps_s[:, 0:cw], AF.Copy, wait=[(s_pe, pe)], inc=s_ev)
                        last_ev = sch.call("scalar", "activation",
                                           sB[:, ch * 512:ch * 512 + cw],
                                           ps_t[:, 0:cw], AF.Copy, inc=s_ev)
                    tprev[dslot] = (s_ev, last_ev)
                    bank_ev = last_ev
                    sch.dma("gpsimd", scr[2 * hp, lt * 128:(lt + 1) * 128, :],
                            sA[:, :], s_tb[dslot], wait=[(s_ev, last_ev)])
                    tev_dma[dslot] = sch.dma(
                        "gpsimd", scr[2 * hp + 1, lt * 128:(lt + 1) * 128, :],
                        sB[:, :], s_tb[dslot])
        tab_done = [(s_tb[0], sch.val(s_tb[0])), (s_tb[1], sch.val(s_tb[1]))]
        ctx2.close()
        scope_gate = tab_done + [(s_pe, sch.val(s_pe)), (s_ev, sch.val(s_ev)),
                                 (s_dv, sch.val(s_dv))]
        gat = E.sb("gat", [128, 4 * 2816], BF16)
        gbt = E.sb("gbt", [128, 4 * 512], BF16)
        expt = E.sb("expt", [128, 4 * 512], BF16)
        zacc = E.sb("zacc", [128, 512], F32)
        rz = E.sb("rz", [1, 512], F32)
        rzb = E.sb("rzb", [128, 512], F32)
        cov = E.sb("cov", [64, 512], F32)
        qer = E.sb("qer", [1, 2 * S], BF16)
        ker = E.sb("ker", [1, 2 * S], BF16)

        sp = [ps_s, ps_t]
        si = 0
        sprev = [(s_ev, sch.val(s_ev)), (s_ev, sch.val(s_ev))]
        ga_pe = None
        gb_pe = [None] * 4
        gb_ld = [None] * 4
        e_cons = [None] * 4
        e_zdv = [None] * 4
        prev_pz = None
        prev_cov = None
        prev_cost = None
        prev_rzb_ev = None
        far_pe = None
        for hloc in range(4):
            hp, hh = hloc // 2, hloc % 2
            erow_dv = None
            for (dst, src) in ((qer, qhh), (ker, khh)):
                for side in range(2):
                    ecol = epb2[hh * 64:(hh + 1) * 64,
                                (0 if side == 0 else WTAB - 1):(1 if side == 0 else WTAB)]
                    for ch8 in range(8):
                        ccols = slice(ch8 * 512, (ch8 + 1) * 512)
                        pe = sch.call("tensor", "matmul", ps_z[0:1, 0:512], ecol,
                                      src[:, hp * S:(hp + 1) * S][:, ccols][hh * 64:(hh + 1) * 64, :],
                                      start=True, stop=True,
                                      tile_position=(hh * 64, 0),
                                      wait=[(s_dv, erow_dv)]
                                      + ([(s_pe, far_pe), (s_dv, prev_cov)]
                                         + (scope_gate if hloc == 0 else [])
                                         if ch8 == 0 and side == 0 and dst is qer else []),
                                      inc=s_pe)
                        erow_dv = sch.call(
                            "vector", "tensor_copy",
                            dst[0:1, side * S + ch8 * 512: side * S + (ch8 + 1) * 512],
                            ps_z[0:1, 0:512], wait=[(s_pe, pe)], inc=s_dv)
            erows_dv = sch.val(s_dv)
            for lc in range(NLC):
                l0c = lc * 512
                rt_lo = max(0, 4 * lc - 9)
                rt_hi = min(NKT - 1, 4 * lc + 12)
                rmin = rt_lo * 128
                wr = (rt_hi - rt_lo + 1) * 128
                ga_ld = None
                for j in range(4):
                    lt = 4 * lc + j
                    l0 = lt * 128
                    asrc = AP(tensor=qdsc.tensor,
                              offset=(2 * hp + hh) * S * WTAB + l0 * (WTAB - 1) + rmin + 1664,
                              ap=[[WTAB - 1, 128], [1, wr]])
                    ga_ld = sch.dma("gpsimd", gat[:, j * 2816:j * 2816 + wr], asrc,
                                    s_ga, wait=([(s_pe, ga_pe)] + tab_done if j == 0 else []))
                for rt in range(NKT):
                    r0 = rt * 128
                    bank = sp[si % 2]
                    slot = si % 2
                    si += 1
                    gathered = rt_lo <= rt <= rt_hi
                    w = [sprev[slot], (s_dv, erows_dv)]
                    pe_sc = sch.call("tensor", "matmul", bank[:, 0:512],
                                     khh[:, hp * S + r0: hp * S + r0 + 128][hh * 64:(hh + 1) * 64, :],
                                     qhh[:, hp * S + l0c: hp * S + l0c + 512][hh * 64:(hh + 1) * 64, :],
                                     start=True, stop=False,
                                     tile_position=(hh * 64, 0),
                                     wait=w, inc=None)
                    if gathered:
                        bslot = rt % 4
                        gb_ld[bslot] = sch.dma(
                            "gpsimd", gbt[:, bslot * 512:(bslot + 1) * 512],
                            AP(tensor=kdsc.tensor,
                               offset=(2 * hp + hh) * S * WTAB + r0 * (WTAB - 1) + l0c + 1663,
                               ap=[[WTAB - 1, 128], [1, 512]]),
                            s_gb[bslot], wait=[(s_pe, gb_pe[bslot])])
                        for j in range(4):
                            sch.call("tensor", "matmul", bank[:, j * 128:(j + 1) * 128],
                                     gat[:, j * 2816 + (rt - rt_lo) * 128:
                                         j * 2816 + (rt - rt_lo) * 128 + 128],
                                     ident[:], start=False, stop=False,
                                     wait=([(s_ga, ga_ld)] if j == 0 else []))
                        gb_pe[bslot] = sch.call(
                            "tensor", "matmul", bank[:, 0:512], ident[:],
                            gbt[:, bslot * 512:(bslot + 1) * 512],
                            start=False, stop=True,
                            wait=[(s_gb[bslot], gb_ld[bslot])], inc=s_pe)
                        pe_sc = gb_pe[bslot]
                        ga_pe = pe_sc
                    else:
                        side = 1 if rt < rt_lo else 0
                        sch.call("tensor", "matmul", bank[:, 0:512], onesb[0:1, :],
                                 qer[0:1, side * S + l0c: side * S + l0c + 512],
                                 start=False, stop=False)
                        pe_sc = sch.call("tensor", "matmul", bank[:, 0:512],
                                         ker[0:1, side * S + r0: side * S + r0 + 128],
                                         onestb[0:1, :], start=False, stop=True,
                                         inc=s_pe)
                        far_pe = pe_sc
                    eslot = si % 4
                    et = expt[:, eslot * 512:(eslot + 1) * 512]
                    ev = sch.call("scalar", "activation", et, bank[:, 0:512],
                                  AF.Exp, scale=0.125,
                                  wait=[(s_pe, pe_sc), (s_pe, e_cons[eslot]),
                                        (s_dv, e_zdv[eslot])], inc=s_ev)
                    sprev[slot] = (s_ev, ev)
                    zw = [(s_ev, ev)]
                    if rt == 0:
                        zw += [(s_pe, prev_pz)]
                        zdv = sch.call("vector", "tensor_copy", zacc[:], et,
                                       wait=zw, inc=s_dv)
                    else:
                        zdv = sch.call("vector", "tensor_add", zacc[:], zacc[:], et,
                                       wait=zw, inc=s_dv)
                    e_zdv[eslot] = zdv
                    avw = [(s_ev, ev)]
                    if rt == 0:
                        avw.append((s_dv, prev_cov))
                    e_cons[eslot] = sch.call(
                        "tensor", "matmul", ps_x[0:64, 0:512],
                        vhh[:, (rt * 2 + hp) * 128 + hh * 64:(rt * 2 + hp) * 128 + (hh + 1) * 64],
                        et, start=(rt == 0), stop=(rt == NKT - 1),
                        wait=avw, inc=s_pe)
                av_pe = sch.val(s_pe)
                last_zdv = sch.val(s_dv)
                pz = sch.call("tensor", "matmul", ps_z[0:1, 0:512], onescf[:, 0:1],
                              zacc[:], start=True, stop=True,
                              wait=[(s_dv, last_zdv), (s_dv, prev_cov)], inc=s_pe)
                prev_pz = pz
                d1 = sch.call("vector", "reciprocal", rz[0:1, :], ps_z[0:1, 0:512],
                              wait=[(s_pe, pz)], inc=s_dv)
                bp = sch.call("tensor", "matmul", ps_c[0:64, 0:512],
                              ones1f[0:1, 0:64], rz[0:1, :], start=True, stop=True,
                              wait=[(s_dv, d1), (s_ev, prev_rzb_ev)], inc=s_pe)
                evz = sch.call("scalar", "activation", rzb[0:64, :], ps_c[0:64, 0:512],
                               AF.Copy, wait=[(s_pe, bp)], inc=s_ev)
                prev_rzb_ev = evz
                cov_dv = sch.call("vector", "tensor_mul", cov[:], ps_x[0:64, 0:512],
                                  rzb[0:64, :],
                                  wait=[(s_ev, evz), (s_pe, av_pe), (s_st, prev_cost)],
                                  inc=s_dv)
                prev_cov = cov_dv
                prev_cost = sch.dma("sync", co[hloc * 64:(hloc + 1) * 64, l0c:l0c + 512],
                                    cov[:], s_st, wait=[(s_dv, cov_dv)])
        sch.barrier_wait("sync", s_st, sch.val(s_st))
    return E.finish()


def host_l1_attn(wq, wk, wv, bq, bk, bv, dist_emb):
    wqT = np.ascontiguousarray(np.asarray(wq, np.float32).T)
    wkT = np.ascontiguousarray(np.asarray(wk, np.float32).T)
    wvT = np.ascontiguousarray(np.asarray(wv, np.float32).T)
    bq = np.asarray(bq, np.float32)
    bk = np.asarray(bk, np.float32)
    bv = np.asarray(bv, np.float32)
    E = np.asarray(dist_emb, np.float32)
    idx = np.clip(np.arange(WTAB) - L1P, 0, 2046)
    ep = E[idx]
    ept2 = np.ascontiguousarray(np.vstack([ep.T, ep.T]))
    eptr2 = np.ascontiguousarray(ept2[:, ::-1])
    const = {
        "ept2": ept2, "eptr2": eptr2,
        "onesb_in": _ones_row(BF), "onesf_in": _ones_row(np.float32),
        "onescf_in": np.ones((128, 1), np.float32),
        "onescb_in": np.ones((128, 1), BF),
        "onestb_in": np.ones((1, 512), BF),
        "ident_in": _ident128(),
        "epsc_in": np.full((1, 1), EPS, np.float32),
    }

    def per(core, li):
        h2 = core % 2
        c0 = (h2 * 8 + li * 4) * 64
        m = dict(const)
        m["wqT"] = np.ascontiguousarray(wqT[:, c0:c0 + 256])
        m["wkT"] = np.ascontiguousarray(wkT[:, c0:c0 + 256])
        m["wvT"] = np.ascontiguousarray(wvT[:, c0:c0 + 256])
        m["bqc"] = chunk_cols(bq[c0:c0 + 256])
        m["bkc"] = chunk_cols(bk[c0:c0 + 256])
        m["bvr"] = bv[c0:c0 + 256].reshape(1, -1).astype(BF)
        return m
    return per


TV = 2560
TK = 2304
DI2 = 2816
FCA = DI2 // 128


def build_l1f_pre(mega=None, io=None):
    E = BlockEnv(mega, "l1fp_")
    nc = E.nc
    if mega is None:
        xi = nc.dram_tensor("xi", [D, TOK], F32, kind="ExternalInput").ap()
        ci = nc.dram_tensor("ci", [D, TOK], F32, kind="ExternalInput").ap()
        yo = nc.dram_tensor("yo", [D, TOK], F32, kind="ExternalOutput").ap()
    else:
        xi = io["xi"]; ci = io.get("ci"); yo = io["yo"]
    if True:
        _e = E.ctx.enter_context
        xa = E.sb("xa", [128, 2 * 512], F32)
        xb = E.sb("xb", [128, 2 * 512], F32)
        s_ld = E.sem("s_ld")
        s_dv = E.sem("s_dv")
        s_ev = E.sem("s_ev")
        s_st = E.sem("s_st")
        sch = E.sch
        E.start()
        prev_st = [None, None]
        prev_add = [None, None]
        for i in range(FC * 4):
            c = i // 4
            tb = i % 4
            cols = slice(tb * 512, (tb + 1) * 512)
            slot = i % 2
            w = [(s_st, prev_st[slot])]
            sch.dma("sync", xa[:, slot * 512:(slot + 1) * 512],
                    xi[c * 128:(c + 1) * 128, cols], s_ld, wait=w)
            if io is not None and "ci_load" in io:
                ldw = io["ci_load"](sch, c, tb, xb[:, slot * 512:(slot + 1) * 512],
                                    [(s_dv, prev_add[slot])])
            else:
                ldw = (s_ld, sch.dma("sync", xb[:, slot * 512:(slot + 1) * 512],
                                     ci[c * 128:(c + 1) * 128, cols], s_ld))
            d = sch.call("vector", "tensor_add", xa[:, slot * 512:(slot + 1) * 512],
                         xa[:, slot * 512:(slot + 1) * 512],
                         xb[:, slot * 512:(slot + 1) * 512],
                         wait=[(s_ld, sch.val(s_ld)), ldw], inc=s_dv)
            prev_add[slot] = d
            prev_st[slot] = sch.dma("sync", yo[c * 128:(c + 1) * 128, cols],
                                    xa[:, slot * 512:(slot + 1) * 512], s_st,
                                    wait=[(s_dv, d)])
        sch.barrier_wait("sync", s_st, sch.val(s_st))
    return E.finish()


def build_l1f_attn(ln_first=False, mega=None, io=None, pfx="l1fa_"):
    E = BlockEnv(mega, pfx)
    nc = E.nc
    if mega is None:
        xv = nc.dram_tensor("xv", [D, TV], F32, kind="ExternalInput").ap()
        xr = nc.dram_tensor("xr", [D, TOK], F32, kind="ExternalInput").ap()
        xo = nc.dram_tensor("xo", [D, TOK], F32, kind="ExternalOutput").ap()
    else:
        xv = io.get("xv"); xr = io["xr"]; xo = io["xo"]
    wqkv = E.dram_in("wqkv", [D, 3 * D], F32)
    woT = E.dram_in("woT", [D, D], F32)
    qs2 = E.dram_in("qs2", [128, 2], F32)
    ks2 = E.dram_in("ks2", [128, 2], F32)
    cst = E.dram_in("cst", [128, TK], F32)
    snt = E.dram_in("snt", [128, TK], F32)
    csti = E.dram_in("csti", [128, TK], F32)
    snti = E.dram_in("snti", [128, TK], F32)
    rotT = E.dram_in("rotT", [128, 128], BF16)
    wmask = E.dram_in("wmask", [4, 128, 256], BF16)
    kvis = E.dram_in("kvis", [128, 32], F32)
    onesb_in = E.dram_in("onesb_in", [1, 128], BF16)
    onesf_in = E.dram_in("onesf_in", [1, 128], F32)
    onescf_in = E.dram_in("onescf_in", [128, 1], F32)
    onescb_in = E.dram_in("onescb_in", [128, 1], BF16)
    ident_in = E.dram_in("ident_in", [128, 128], BF16)
    epsc_in = E.dram_in("epsc_in", [1, 1], F32)
    mum_in = E.dram_in("mum_in", [1, 1], F32)

    NCH = 5
    def chw(ch):
        return 512 if ch < 4 else 256
    if True:
        _e = E.ctx.enter_context
        xb = E.sb("xb", [128, FC * TV], BF16)
        wq3 = E.sb("wq3", [128, FC * 384], BF16)
        wo = E.sb("wo", [128, FC * D], BF16)
        cxs = E.sb("cxs", [128, FC * TOK], BF16)
        qp = E.sb("qp", [128, TK], BF16)
        kp = E.sb("kp", [128, TK], BF16)
        vp = E.sb("vp", [128, 18 * 128], BF16)
        qf = E.sb("qf", [128, TK], BF16)
        kf = E.sb("kf", [128, TK], BF16)
        sq2 = E.sb("sq2", [128, TK], BF16)
        ta = E.sb("ta", [128, TK], BF16)
        tb_ = E.sb("tb_", [128, TK], BF16)
        tc_ = E.sb("tc_", [128, TK], BF16)
        td_ = E.sb("td_", [128, TK], BF16)
        rot = E.sb("rot", [128, 128], BF16)
        nr = E.sb("nr", [1, 2048], F32)
        expt = E.sb("expt", [128, 4 * 256], BF16)
        zca = E.sb("zca", [128, 2 * 256], F32)
        rzr = E.sb("rzr", [1, 2 * 256], F32)
        rzb = E.sb("rzb", [128, 256], F32)
        msk = E.sb("msk", [128, 4 * 256], BF16)
        kvi = E.sb("kvi", [128, 32], F32)
        qsc = E.sb("qsc", [128, 4], F32)
        xs = E.sb("xs", [128, FC * 512], F32)
        utm = E.sb("utm", [128, 512], F32)
        onesb = E.sb("onesb", [1, 128], BF16)
        ones1f = E.sb("ones1f", [1, 128], F32)
        onescf = E.sb("onescf", [128, 1], F32)
        onescb = E.sb("onescb", [128, 1], BF16)
        ident = E.sb("ident", [128, 128], BF16)
        epsc = E.sb("epsc", [1, 1], F32)
        mum = E.sb("mum", [1, 1], F32)
        stat = E.sb("stat", [1, 512], F32)
        stat2 = E.sb("stat2", [1, 512], F32)
        stat3 = E.sb("stat3", [1, 512], F32)
        xsq = E.sb("xsq", [128, 2 * 512], BF16)
        ps_a = E.ps()
        ps_b = E.ps()
        ps_c = E.ps()
        ps_d = E.ps()
        ps_e = E.ps()
        ps_f = E.ps()
        ps_g = E.ps()
        ps_h = E.ps()
        s_ld = E.sem("s_ld")
        s_lg = E.sem("s_lg")
        s_wl = E.sem("s_wl")
        s_pe = E.sem("s_pe")
        s_ev = E.sem("s_ev")
        s_dv = E.sem("s_dv")
        s_xs = E.sem("s_xs")
        s_st = E.sem("s_st")
        sch = E.sch
        E.start()

        for (dst, src) in ((onesb, onesb_in), (ones1f, onesf_in), (onescf, onescf_in),
                           (onescb, onescb_in), (ident, ident_in), (epsc, epsc_in),
                           (mum, mum_in), (kvi, kvis), (rot, rotT),
                           (qsc[:, 0:2], qs2), (qsc[:, 2:4], ks2)):
            sch.dma("sync", dst if isinstance(dst, AP) else dst[:], src[:, :], s_ld)
        for v_ in range(4):
            sch.dma("sync", msk[:, v_ * 256:(v_ + 1) * 256], wmask[v_, :, :], s_ld)
        for c in range(FC):
            sch.dma("gpsimd", wo[:, c * D:(c + 1) * D],
                    woT[c * 128:(c + 1) * 128, :], s_lg)
        for (dst, src) in ((ta, cst), (tb_, snt), (tc_, csti), (td_, snti)):
            sch.dma("gpsimd", dst[:], src[:, :], s_lg)
        for c in range(FC):
            if io is not None and "xv_load" in io:
                io["xv_load"](sch, c, xb[:, c * TV:(c + 1) * TV], s_lg)
            else:
                sch.dma("gpsimd", xb[:, c * TV:(c + 1) * TV],
                        xv[c * 128:(c + 1) * 128, :], s_lg)
        all_ld = sch.val(s_ld)
        all_lg = sch.val(s_lg)
        tdv = None
        for (dst, sidx) in ((ta, 0), (tb_, 1), (tc_, 2), (td_, 3)):
            tdv = sch.call("vector", "tensor_scalar", dst[:], dst[:],
                           qsc[:, sidx:sidx + 1], None, ALU.mult,
                           wait=[(s_lg, all_lg), (s_ld, all_ld)], inc=s_dv)
        tab_dv = tdv

        ln_dv = None
        if ln_first:
            env = {"xn": xb, "onescb": onescb, "ones1f": ones1f, "epsc": epsc,
                   "stat": stat, "stat2": stat2, "stat3": stat3, "xsq": xsq,
                   "ps_sum": ps_a, "ps_sq": ps_b, "ps_ba": ps_c, "ps_bb": ps_d,
                   "s_pe": s_pe, "s_ev": s_ev, "s_dv": s_dv,
                   "xn_ready": [(s_ld, all_ld), (s_lg, all_lg)],
                   "chunk_fn": lambda c: xb[:, c * TV:(c + 1) * TV]}
            ln_dv = emit_ln_fm(sch, env, TV)

        def xsh(c, cols):
            off = 256 if c < 4 else 255
            return xb[:, c * TV + off: c * TV + off + TK][:, cols]

        wl_prev = None
        hp_pe = None
        pp = [ps_a, ps_b]
        pi = 0
        prev = [(s_ev, None), (s_ev, None)]
        sprev = [(s_ev, None), (s_ev, None)]
        e_cons = [None] * 4
        e_zdv = [None] * 4
        prev_ctx_dv = None
        prev_pz = None
        prev_rzb = None
        for hp in range(FC):
            wl_prev = sch.dma("gpsimd", wq3[:, :].rearrange("p (c x) -> p c x", c=FC),
                              wqkv[:, hp * 384:(hp + 1) * 384]
                              .rearrange("(c p) x -> p c x", p=128),
                              s_wl, wait=[(s_pe, hp_pe)])
            for (which, dst) in ((0, qp), (1, kp)):
                for ch in range(NCH):
                    cw = chw(ch)
                    cols = slice(ch * 512, ch * 512 + cw)
                    bank = pp[pi % 2]
                    slot = pi % 2
                    pi += 1
                    w = [(s_wl, wl_prev), prev[slot]]
                    if ln_first:
                        w.append((s_dv, ln_dv))
                    w.append((s_lg, all_lg))
                    for c in range(FC):
                        sch.call("tensor", "matmul", bank[:, 0:cw],
                                 wq3[:, c * 384 + which * 128: c * 384 + (which + 1) * 128],
                                 xsh(c, cols), start=(c == 0), stop=(c == FC - 1),
                                 wait=(w if c == 0 else []),
                                 inc=(s_pe if c == FC - 1 else None))
                    pe = sch.val(s_pe)
                    ev = sch.call("scalar", "activation", dst[:, cols], bank[:, 0:cw],
                                  AF.Copy, wait=[(s_pe, pe)], inc=s_ev)
                    prev[slot] = (s_ev, ev)
            for tt in range(18):
                tcols = slice(tt * 128, (tt + 1) * 128)
                bank = pp[pi % 2]
                slot = pi % 2
                pi += 1
                w = [(s_wl, wl_prev), prev[slot]]
                if ln_first:
                    w.append((s_dv, ln_dv))
                for c in range(FC):
                    sch.call("tensor", "matmul", bank[:, 0:128],
                             xsh(c, tcols),
                             wq3[:, c * 384 + 256: c * 384 + 384],
                             start=(c == 0), stop=(c == FC - 1),
                             wait=(w if c == 0 else []),
                             inc=(s_pe if c == FC - 1 else None))
                pe = sch.val(s_pe)
                ev = sch.call("scalar", "activation", vp[:, tt * 128:(tt + 1) * 128],
                              bank[:, 0:128], AF.Copy, wait=[(s_pe, pe)], inc=s_ev)
                prev[slot] = (s_ev, ev)
            pk_ev = sch.val(s_ev)
            for (src_, dst_, tA, tB) in ((qp, qf, ta, tb_), (kp, kf, tc_, td_)):
                sq_ev = sch.call("scalar", "activation", sq2[:, 0:TK], src_[:, 0:TK],
                                 AF.Square, wait=[(s_ev, pk_ev), (s_dv, sch.val(s_dv))],
                                 inc=s_ev)
                for ch in range(NCH):
                    cw = chw(ch)
                    cols = slice(ch * 512, ch * 512 + cw)
                    for hh in range(2):
                        pe = sch.call("tensor", "matmul", ps_g[0:1, 0:cw],
                                      onescb[hh * 64:(hh + 1) * 64, 0:1],
                                      sq2[hh * 64:(hh + 1) * 64, cols],
                                      start=True, stop=True,
                                      tile_position=(hh * 64, 0),
                                      wait=[(s_ev, sq_ev), (s_dv, sch.val(s_dv))],
                                      inc=s_pe)
                        pe_ = sch.val(s_pe)
                        rrow = nr[0:1, hh * 512: hh * 512 + cw]
                        trow = nr[0:1, 1024 + hh * 512: 1024 + hh * 512 + cw]
                        e2 = sch.call("scalar", "activation", trow, ps_g[0:1, 0:cw],
                                      AF.Sqrt, wait=[(s_pe, pe_)], inc=s_ev)
                        d2 = sch.call("vector", "tensor_scalar_max", trow, trow,
                                      1e-12, wait=[(s_ev, e2)], inc=s_dv)
                        d3 = sch.call("vector", "reciprocal", rrow, trow,
                                      wait=[(s_dv, d2)], inc=s_dv)
                        sch.call("tensor", "matmul",
                                 ps_f[hh * 64:(hh + 1) * 64, 0:cw],
                                 ones1f[0:1, 0:64], rrow, start=True, stop=True,
                                 tile_position=(0, hh * 64),
                                 wait=[(s_dv, d3)],
                                 inc=(s_pe if hh == 1 else None))
                    pe_b = sch.val(s_pe)
                    pe_r = sch.call("tensor", "matmul", ps_e[:, 0:cw], rot[:],
                                    src_[:, cols], start=True, stop=True,
                                    wait=[(s_dv, sch.val(s_dv))], inc=s_pe)
                    d1 = sch.call("vector", "tensor_mul", sq2[:, cols], src_[:, cols],
                                  tA[:, cols], wait=[(s_pe, pe_r)], inc=s_dv)
                    d2 = sch.call("vector", "tensor_tensor", dst_[:, cols],
                                  ps_e[:, 0:cw], tB[:, cols], ALU.mult,
                                  wait=[(s_dv, d1)], inc=s_dv)
                    d3 = sch.call("vector", "tensor_add", dst_[:, cols], dst_[:, cols],
                                  sq2[:, cols], wait=[(s_dv, d2)], inc=s_dv)
                    d4 = sch.call("vector", "tensor_mul", dst_[:, cols], dst_[:, cols],
                                  ps_f[:, 0:cw], wait=[(s_dv, d3), (s_pe, pe_b)],
                                  inc=s_dv)
            rot_dv = sch.val(s_dv)
            for w8 in range(8):
                q0 = 256 + w8 * 256
                k0 = w8 * 256
                for hh in range(2):
                    zinit = False
                    for kt4 in range(4):
                        bank = [ps_c, ps_d][si2 := (w8 * 8 + hh * 4 + kt4) % 2]
                        kcol = k0 + kt4 * 128
                        w = [sprev[si2], (s_dv, rot_dv)]
                        sch.call("tensor", "matmul", bank[:, 0:256],
                                 kf[hh * 64:(hh + 1) * 64, kcol:kcol + 128],
                                 qf[hh * 64:(hh + 1) * 64, q0:q0 + 256],
                                 start=True, stop=False,
                                 tile_position=(hh * 64, 0), wait=w)
                        pe_sc = sch.call("tensor", "matmul", bank[:, 0:256],
                                         ident[:], msk[:, kt4 * 256:(kt4 + 1) * 256],
                                         start=False, stop=True, inc=s_pe)
                        eslot = (w8 * 8 + hh * 4 + kt4) % 4
                        et = expt[:, eslot * 256:(eslot + 1) * 256]
                        ev = sch.call("scalar", "activation", et, bank[:, 0:256],
                                      AF.Exp, scale=8.0,
                                      bias=kvi[:, w8 * 4 + kt4:w8 * 4 + kt4 + 1],
                                      wait=[(s_pe, pe_sc), (s_pe, e_cons[eslot]),
                                            (s_dv, e_zdv[eslot])], inc=s_ev)
                        sprev[si2] = (s_ev, ev)
                        zsl = zca[:, hh * 256:(hh + 1) * 256]
                        if kt4 == 0:
                            zdv = sch.call("vector", "tensor_copy", zsl, et,
                                           wait=[(s_ev, ev), (s_pe, prev_pz)],
                                           inc=s_dv)
                        else:
                            zdv = sch.call("vector", "tensor_add", zsl, zsl, et,
                                           wait=[(s_ev, ev)], inc=s_dv)
                        e_zdv[eslot] = zdv
                        avw = [(s_ev, ev)]
                        if kt4 == 0:
                            avw.append((s_dv, prev_ctx_dv))
                        e_cons[eslot] = sch.call(
                            "tensor", "matmul",
                            [ps_e, ps_f][hh][hh * 64:(hh + 1) * 64, 0:256],
                            vp[:, (w8 + kt4 // 2) * 128 * 2:][:, 0:0] if False else
                            vp[:, (w8 * 2 + kt4) * 64 + hh * 64:][:, 0:0] if False else
                            vp[:, k0 // 128 * 128 + kt4 * 128: k0 // 128 * 128 + kt4 * 128][:, 0:0] if False else
                            vp[:, (k0 + kt4 * 128) // 128 * 128 + hh * 64:
                                (k0 + kt4 * 128) // 128 * 128 + hh * 64 + 64],
                            et, start=(kt4 == 0), stop=(kt4 == 3),
                            tile_position=(0, hh * 64),
                            wait=avw, inc=s_pe)
                    av_pe = sch.val(s_pe)
                    pz = sch.call("tensor", "matmul", ps_g[0:1, 0:256], onescf[:, 0:1],
                                  zca[:, hh * 256:(hh + 1) * 256], start=True,
                                  stop=True, wait=[(s_dv, e_zdv[(w8 * 8 + hh * 4 + 3) % 4])],
                                  inc=s_pe)
                    prev_pz = pz
                    d1 = sch.call("vector", "reciprocal", rzr[0:1, hh * 256:(hh + 1) * 256],
                                  ps_g[0:1, 0:256], wait=[(s_pe, pz)], inc=s_dv)
                    bp = sch.call("tensor", "matmul",
                                  ps_h[hh * 64:(hh + 1) * 64, 0:256],
                                  ones1f[0:1, 0:64],
                                  rzr[0:1, hh * 256:(hh + 1) * 256],
                                  start=True, stop=True, tile_position=(0, hh * 64),
                                  wait=[(s_dv, d1), (s_ev, prev_rzb)], inc=s_pe)
                    evz = sch.call("scalar", "activation", rzb[hh * 64:(hh + 1) * 64, 0:256],
                                   ps_h[hh * 64:(hh + 1) * 64, 0:256], AF.Copy,
                                   wait=[(s_pe, bp)], inc=s_ev)
                    prev_rzb = evz
                    ctx_dv = sch.call(
                        "vector", "tensor_mul",
                        cxs[:, hp * TOK + w8 * 256: hp * TOK + w8 * 256 + 256][hh * 64:(hh + 1) * 64, :],
                        [ps_e, ps_f][hh][hh * 64:(hh + 1) * 64, 0:256],
                        rzb[hh * 64:(hh + 1) * 64, 0:256],
                        wait=[(s_ev, evz), (s_pe, av_pe)], inc=s_dv)
                    prev_ctx_dv = ctx_dv
            hp_pe = sch.val(s_pe)
        ctx_all_dv = sch.val(s_dv)

        oprev = [(s_dv, None), (s_dv, None)]
        prev_st = None
        for qc in range(4):
            qcols = slice(qc * 512, (qc + 1) * 512)
            if not ln_first:
                for c in range(FC):
                    sch.dma("sync", xs[:, c * 512:(c + 1) * 512],
                            xr[c * 128:(c + 1) * 128, qcols], s_xs,
                            wait=([(s_st, prev_st)] if c == 0 else []))
            xs_ld = sch.val(s_xs)
            for fo in range(FC):
                bank = pp[pi % 2]
                slot = pi % 2
                pi += 1
                w = [(s_dv, ctx_all_dv), oprev[slot]]
                for c in range(FC):
                    sch.call("tensor", "matmul", bank[:, 0:512],
                             wo[:, c * D + fo * 128: c * D + fo * 128 + 128],
                             cxs[:, c * TOK:(c + 1) * TOK][:, qcols],
                             start=(c == 0), stop=(c == FC - 1),
                             wait=(w if c == 0 else []),
                             inc=(s_pe if c == FC - 1 else None))
                pe = sch.val(s_pe)
                if ln_first:
                    xres = xb[:, fo * TV + 512:(fo * TV) + 512 + TOK][:, qcols]
                    d2 = sch.call("vector", "tensor_add", utm[:, 0:512],
                                  bank[:, 0:512], xres,
                                  wait=[(s_pe, pe), (s_st, prev_st)], inc=s_dv)
                    oprev[slot] = (s_dv, d2)
                    prev_st = sch.dma("sync", xo[fo * 128:(fo + 1) * 128, qcols],
                                      utm[:, 0:512], s_st, wait=[(s_dv, d2)])
                    if io is not None and "col_hook" in io and qc == 3:
                        prev_st = io["col_hook"](sch, fo, utm[:, 511:512])
                else:
                    xsl = xs[:, fo * 512:(fo + 1) * 512]
                    d2 = sch.call("vector", "tensor_add", xsl, bank[:, 0:512], xsl,
                                  wait=[(s_pe, pe), (s_xs, xs_ld)], inc=s_dv)
                    oprev[slot] = (s_dv, d2)
                    prev_st = sch.dma("sync", xo[fo * 128:(fo + 1) * 128, qcols],
                                      xsl, s_st, wait=[(s_dv, d2)])
                    if io is not None and "col_hook" in io and qc == 3:
                        prev_st = io["col_hook"](sch, fo, xsl[:, 511:512])
        sch.barrier_wait("sync", s_st, sch.val(s_st))
    return E.finish()


def build_l1f_ffn(final=False, mega=None, io=None, pfx="l1ff_"):
    E = BlockEnv(mega, pfx)
    nc = E.nc
    TVF = TOK + 1
    if mega is None:
        xv = nc.dram_tensor("xv", [D, TVF], F32, kind="ExternalInput").ap()
        if final:
            ym = nc.dram_tensor("ym", [D, TOK], F32, kind="ExternalInput").ap()
        xo = nc.dram_tensor("xo", [D, TOK], F32, kind="ExternalOutput").ap()
    else:
        xv = io.get("xv"); xo = io["xo"]
        ym = io.get("ym")
    w1p_in = E.dram_in("w1p_in", [2, D, DI2], F32)
    b1r_in = E.dram_in("b1r_in", [1, 2 * DI2], BF16)
    w2p_in = E.dram_in("w2p_in", [DI2, D], F32)
    b2c_in = E.dram_in("b2c_in", [128, FC], F32)
    lng_in = E.dram_in("lng_in", [128, FC], F32)
    lnb_in = E.dram_in("lnb_in", [128, FC], F32)
    onesb_in = E.dram_in("onesb_in", [1, 128], BF16)
    onesf_in = E.dram_in("onesf_in", [1, 128], F32)
    onescb_in = E.dram_in("onescb_in", [128, 1], BF16)
    onest_in = E.dram_in("onest_in", [1, 512], BF16)
    epsc_in = E.dram_in("epsc_in", [1, 1], F32)

    if True:
        _e = E.ctx.enter_context
        xb = E.sb("xb", [128, FC * TVF], BF16)
        w1 = E.sb("w1", [128, FC * DI2], BF16)
        w2 = E.sb("w2", [128, FCA * D], BF16)
        ab = E.sb("ab", [128, FCA * 512], BF16)
        gst = E.sb("gst", [128, 2 * 512], BF16)
        b1r = E.sb("b1r", [1, 2 * DI2], BF16)
        b2c = E.sb("b2c", [128, FC], F32)
        lng = E.sb("lng", [128, FC], F32)
        lnb = E.sb("lnb", [128, FC], F32)
        xs = E.sb("xs", [128, FC * 512], F32)
        xs2 = E.sb("xs2", [128, FC * 512], F32) if final else None
        utm = E.sb("utm", [128, 512], F32)
        onesb = E.sb("onesb", [1, 128], BF16)
        ones1f = E.sb("ones1f", [1, 128], F32)
        onescb = E.sb("onescb", [128, 1], BF16)
        onest = E.sb("onest", [1, 512], BF16)
        epsc = E.sb("epsc", [1, 1], F32)
        stat = E.sb("stat", [1, 512], F32)
        stat2 = E.sb("stat2", [1, 512], F32)
        stat3 = E.sb("stat3", [1, 512], F32)
        xsq = E.sb("xsq", [128, 2 * 512], BF16)
        ps_a = E.ps()
        ps_b = E.ps()
        ps_c = E.ps()
        ps_d = E.ps()
        s_ld = E.sem("s_ld")
        s_lg = E.sem("s_lg")
        s_w1 = E.sem("s_w1")
        s_w2 = E.sem("s_w2")
        s_pe = E.sem("s_pe")
        s_ev = E.sem("s_ev")
        s_dv = E.sem("s_dv")
        s_xs = E.sem("s_xs")
        s_st = E.sem("s_st")
        sch = E.sch
        E.start()

        for (dst, src) in ((onesb, onesb_in), (ones1f, onesf_in), (onescb, onescb_in),
                           (onest, onest_in), (epsc, epsc_in), (b2c, b2c_in),
                           (lng, lng_in), (lnb, lnb_in)):
            sch.dma("sync", dst[:], src[:, :], s_ld)
        sch.dma("gpsimd", b1r[:], b1r_in[:, :], s_lg)
        for c in range(FC):
            if io is not None and "xv_load" in io:
                io["xv_load"](sch, c, xb[:, c * TVF:(c + 1) * TVF], s_lg)
            else:
                sch.dma("gpsimd", xb[:, c * TVF:(c + 1) * TVF],
                        xv[c * 128:(c + 1) * 128, :], s_lg)
        sch.dma("gpsimd", w2[:, :].rearrange("p (c f) -> p c f", c=FCA),
                w2p_in[:, :].rearrange("(c p) f -> p c f", p=128), s_w2)
        w2_ld = sch.val(s_w2)
        all_ld = sch.val(s_ld)
        all_lg = sch.val(s_lg)

        def chunk_shift(c):
            off = 1 if c < 4 else 0
            return xb[:, c * TVF + off: c * TVF + off + TOK]

        env = {"xn": xb, "onescb": onescb, "ones1f": ones1f, "epsc": epsc,
               "stat": stat, "stat2": stat2, "stat3": stat3, "xsq": xsq,
               "ps_sum": ps_a, "ps_sq": ps_b, "ps_ba": ps_c, "ps_bb": ps_d,
               "s_pe": s_pe, "s_ev": s_ev, "s_dv": s_dv,
               "xn_ready": [(s_ld, all_ld), (s_lg, all_lg)],
               "chunk_fn": chunk_shift}
        ln_dv = emit_ln_fm(sch, env, TOK)
        aff_dv = None
        for c in range(FC):
            aff_dv = sch.call("vector", "tensor_scalar", chunk_shift(c), chunk_shift(c),
                              lng[:, c:c + 1], lnb[:, c:c + 1], ALU.mult, ALU.add,
                              wait=([(s_dv, ln_dv), (s_ld, all_ld)] if c == 0 else []),
                              inc=s_dv)

        pp = [ps_a, ps_b]
        pi = 0
        prev = [(s_ev, None), (s_ev, None)]
        w1_pe = None
        prev_st = None
        g_dv = [None, None]
        for tc in range(4):
            cols = slice(tc * 512, (tc + 1) * 512)
            for side in range(2):
                w1_ld = sch.dma("gpsimd", w1[:, :].rearrange("p (c f) -> p c f", c=FC),
                                w1p_in[side, :, :].rearrange("(c p) f -> p c f", p=128),
                                s_w1, wait=[(s_pe, w1_pe)])
                for gf in range(FCA):
                    bank = pp[pi % 2]
                    slot = pi % 2
                    pi += 1
                    w = [(s_w1, w1_ld), prev[slot], (s_dv, aff_dv)]
                    for c in range(FC):
                        sch.call("tensor", "matmul", bank[:, 0:512],
                                 w1[:, c * DI2 + gf * 128: c * DI2 + gf * 128 + 128],
                                 chunk_shift(c)[:, cols], start=(c == 0), stop=False,
                                 wait=(w if c == 0 else []))
                    sch.call("tensor", "matmul", bank[:, 0:512],
                             b1r[0:1, side * DI2 + gf * 128: side * DI2 + gf * 128 + 128],
                             onest[0:1, :], start=False, stop=True, inc=s_pe)
                    pe = sch.val(s_pe)
                    if side == 0:
                        ev = sch.call("scalar", "activation",
                                      ab[:, gf * 512:(gf + 1) * 512], bank[:, 0:512],
                                      AF.Copy, wait=[(s_pe, pe)], inc=s_ev)
                        prev[slot] = (s_ev, ev)
                    else:
                        gslot = gf % 2
                        ev = sch.call("scalar", "activation",
                                      gst[:, gslot * 512:(gslot + 1) * 512],
                                      bank[:, 0:512], AF.Gelu,
                                      wait=[(s_pe, pe), (s_dv, g_dv[gslot])], inc=s_ev)
                        d = sch.call("vector", "tensor_mul",
                                     ab[:, gf * 512:(gf + 1) * 512],
                                     ab[:, gf * 512:(gf + 1) * 512],
                                     gst[:, gslot * 512:(gslot + 1) * 512],
                                     wait=[(s_ev, ev)], inc=s_dv)
                        g_dv[gslot] = d
                        prev[slot] = (s_dv, d)
                w1_pe = sch.val(s_pe)
            act_dv = sch.val(s_dv)
            for c in range(FC):
                if io is not None and "xvr" in io:
                    xsrc = io["xvr"][c * 128:(c + 1) * 128, tc * 512:(tc + 1) * 512]
                else:
                    xsrc = xv[c * 128:(c + 1) * 128, 1 + tc * 512:1 + (tc + 1) * 512]
                sch.dma("sync", xs[:, c * 512:(c + 1) * 512], xsrc,
                        s_xs, wait=([(s_st, prev_st)] if c == 0 else []))
                if final:
                    sch.dma("sync", xs2[:, c * 512:(c + 1) * 512],
                            ym[c * 128:(c + 1) * 128, cols], s_xs)
            xs_ld = sch.val(s_xs)
            for fo in range(FC):
                bank = pp[pi % 2]
                slot = pi % 2
                pi += 1
                w = [(s_dv, act_dv), prev[slot], (s_w2, w2_ld)]
                for c in range(FCA):
                    sch.call("tensor", "matmul", bank[:, 0:512],
                             w2[:, c * D + fo * 128: c * D + fo * 128 + 128],
                             ab[:, c * 512:(c + 1) * 512],
                             start=(c == 0), stop=(c == FCA - 1),
                             wait=(w if c == 0 else []),
                             inc=(s_pe if c == FCA - 1 else None))
                pe = sch.val(s_pe)
                d1 = sch.call("vector", "tensor_scalar", utm[:, 0:512],
                              bank[:, 0:512], b2c[:, fo:fo + 1], None, ALU.add,
                              wait=[(s_pe, pe)] + ([(s_xs, xs_ld)] if fo == 0 else []),
                              inc=s_dv)
                prev[slot] = (s_dv, d1)
                xsl = xs[:, fo * 512:(fo + 1) * 512]
                d2 = sch.call("vector", "tensor_add", xsl, utm[:, 0:512], xsl,
                              wait=[(s_dv, d1)], inc=s_dv)
                if final:
                    d2 = sch.call("vector", "tensor_add", xsl, xsl,
                                  xs2[:, fo * 512:(fo + 1) * 512],
                                  wait=[(s_dv, d2)], inc=s_dv)
                sch.dma("sync", xo[fo * 128:(fo + 1) * 128, cols], xsl, s_st,
                        wait=[(s_dv, d2)])
            prev_st = sch.val(s_st)
        sch.barrier_wait("sync", s_st, sch.val(s_st))
    return E.finish()


def host_l1f(Wqkv, q_scale, k_scale, Wo, ln_g, ln_b, W1, b1, W2, b2):
    DEPTH = 6
    HEADS = 16
    DH = 64
    WINDOW = 256
    DIM_INNER = 2730
    Wqkv = np.asarray(Wqkv, np.float32)
    Wo = np.asarray(Wo, np.float32)
    W1 = np.asarray(W1, np.float32)
    W2 = np.asarray(W2, np.float32)
    b1 = np.asarray(b1, np.float32)
    b2 = np.asarray(b2, np.float32)
    ln_g = np.asarray(ln_g, np.float32)
    ln_b = np.asarray(ln_b, np.float32)
    q_scale = np.asarray(q_scale, np.float32)
    k_scale = np.asarray(k_scale, np.float32)
    R = np.zeros((64, 64), np.float32)
    for d_ in range(32):
        R[d_, 32 + d_] = -1.0
        R[32 + d_, d_] = 1.0
    R2 = np.zeros((128, 128), np.float32)
    R2[0:64, 0:64] = R
    R2[64:128, 64:128] = R
    R2T = np.ascontiguousarray(R2.T).astype(BF)
    i_ = np.arange(256)[None, :]
    j_ = np.arange(128)[:, None]
    masks = []
    for kt4 in range(4):
        jj = kt4 * 128 + j_
        if kt4 < 2:
            keep = (i_ <= jj)
        else:
            keep = (i_ >= (jj - 256))
        masks.append(np.where(keep, 0.0, NEG))
    wmask = np.stack(masks).astype(BF)

    def rotary_tables(n=4096):
        half = np.arange(0, DH, 2, dtype=np.float32)
        inv_freq = 1.0 / 10000.0 ** (half / DH)
        t = np.arange(n, dtype=np.float32)
        freqs = t[:, None] * inv_freq[None, :]
        freqs = np.concatenate([freqs, freqs], -1)
        base_scale = (half + 0.4 * DH) / (1.4 * DH)
        power = (t - n // 2) / (WINDOW // 2)
        scale = base_scale[None, :] ** power[:, None]
        scale = np.concatenate([scale, scale], -1)
        return freqs, scale
    freqs, scale = rotary_tables()
    cosS = (np.cos(freqs) * scale).T
    sinS = (np.sin(freqs) * scale).T
    cosI = (np.cos(freqs) / scale).T
    sinI = (np.sin(freqs) / scale).T
    sigma = np.concatenate([np.arange(32, 64), np.arange(0, 32)])

    consts = {
        "rotT": R2T, "wmask": wmask,
        "onesb_in": _ones_row(BF), "onesf_in": _ones_row(np.float32),
        "onescf_in": np.ones((128, 1), np.float32),
        "onescb_in": np.ones((128, 1), BF),
        "ident_in": _ident128(),
        "epsc_in": np.full((1, 1), EPS, np.float32),
        "mum_in": np.full((1, 1), 1e-12, np.float32),
    }
    ffn_consts = {
        "onesb_in": _ones_row(BF), "onesf_in": _ones_row(np.float32),
        "onescb_in": np.ones((128, 1), BF),
        "onest_in": np.ones((1, 512), BF),
        "epsc_in": np.full((1, 1), EPS, np.float32),
    }

    def attn_layer_inputs(i, core):
        h2 = core % 2
        own0 = h2 * TOK
        trange = np.arange(own0 - 256, own0 + TOK)
        trange = np.clip(trange, 0, S - 1)
        m = dict(consts)
        Wq = Wqkv[i][:, 0:1024]
        Wk = Wqkv[i][:, 1024:2048]
        Wv = Wqkv[i][:, 2048:3072]
        blocks = []
        for hp in range(8):
            sl = slice(hp * 128, (hp + 1) * 128)
            blocks += [Wq[:, sl], Wk[:, sl], Wv[:, sl]]
        m["wqkv"] = np.ascontiguousarray(np.concatenate(blocks, axis=1))
        m["woT"] = np.ascontiguousarray(Wo[i])
        m["qs2"] = np.stack([np.tile(q_scale[i], 2), np.tile(q_scale[i][sigma], 2)], 1).copy()
        m["ks2"] = np.stack([np.tile(k_scale[i], 2), np.tile(k_scale[i][sigma], 2)], 1).copy()
        m["cst"] = np.ascontiguousarray(np.tile(cosS[:, trange], (2, 1)))
        m["snt"] = np.ascontiguousarray(np.tile(sinS[:, trange], (2, 1)))
        m["csti"] = np.ascontiguousarray(np.tile(cosI[:, trange], (2, 1)))
        m["snti"] = np.ascontiguousarray(np.tile(sinI[:, trange], (2, 1)))
        kv = np.zeros((128, 32), np.float32)
        if h2 == 0:
            kv[:, 0] = NEG
            kv[:, 1] = NEG
        m["kvis"] = kv
        return m

    def ffn_layer_inputs(i):
        m = dict(ffn_consts)
        a = W1[i][:, 0:DIM_INNER]
        g = W1[i][:, DIM_INNER:2 * DIM_INNER]
        pad = np.zeros((D, DI2 - DIM_INNER), np.float32)
        m["w1p_in"] = np.ascontiguousarray(np.stack([np.concatenate([a, pad], 1),
                                                     np.concatenate([g, pad], 1)]))
        b1a = np.concatenate([b1[i][0:DIM_INNER], np.zeros(DI2 - DIM_INNER, np.float32)])
        b1g = np.concatenate([b1[i][DIM_INNER:], np.zeros(DI2 - DIM_INNER, np.float32)])
        m["b1r_in"] = np.concatenate([b1a, b1g]).reshape(1, -1).astype(BF)
        m["w2p_in"] = np.ascontiguousarray(
            np.concatenate([W2[i], np.zeros((DI2 - DIM_INNER, D), np.float32)], 0))
        m["b2c_in"] = chunk_cols(b2[i])
        m["lng_in"] = chunk_cols(ln_g[i])
        m["lnb_in"] = chunk_cols(ln_b[i])
        return m
    return attn_layer_inputs, ffn_layer_inputs


def build_mega(upto=None):
    mega = Mega()
    nc = mega.nc
    sch = mega.sch
    out = nc.dram_tensor("out", [D, TOK], F32, kind="ExternalOutput").ap()
    cprm = nc.dram_tensor("cprm", [1, 64], mybir.dt.int32, kind="ExternalInput").ap()
    zcol = nc.dram_tensor("zcol", [D, 512], F32, kind="ExternalInput").ap()
    xc0 = mega.scratch("xc0", [D, TOK], F32)
    xc1 = mega.scratch("xc1", [D, TOK], F32)
    yo = mega.scratch("yo", [D, TOK], F32)
    co2 = mega.scratch("co2", [512, S], F32)
    agx = mega.scratch("agx", [2, D, TOK], F32)
    agc = mega.scratch("agc", [2, 512, S], F32)
    cna = mega.scratch("cna", [D, 1024], F32)
    aga = mega.scratch("aga", [2, D, 1024], F32)
    cnf = mega.scratch("cnf", [2, D], F32)
    agf = mega.scratch("agf", [2, 2, D], F32)
    ci_loc = mega.scratch("ci_loc", [D, TOK], F32)
    halo_loc = mega.scratch("halo_loc", [D, 512], F32)
    halo1_loc = mega.scratch("halo1_loc", [1, D], F32)
    xf_loc = mega.scratch("xf_loc", [D, S], F32)
    cprs = mega.ctx.enter_context(nc.sbuf_tensor("cprs", [1, 64], mybir.dt.int32))
    s_gl = mega.sem("s_gl")
    s_cc = mega.sem("s_cc")
    RG = [[0, 1], [2, 3], [4, 5], [6, 7]]
    regs = {}

    sch.dma("gpsimd", cprs[:], cprm[:, :], s_gl)
    sch.dma("gpsimd", cna[:, 512:1024], zcol[:, :], s_gl)
    sch.dma("gpsimd", cnf[1:2, 0:512], zcol[0:1, :], s_gl)
    sch.dma("gpsimd", cnf[1:2, 512:1024], zcol[1:2, :], s_gl)
    cpr_ld = sch.val(s_gl)

    def dyn_stage(k, dst, src_t, ap_dims, wait=()):
        def fn(g):
            rv = g.value_load(cprs[0:1, k:k + 1])
            src = AP(tensor=src_t.tensor, offset=rv, ap=ap_dims)
            ins = g.dma_start(out=dst, in_=src)
            g.free_register(rv.val)
            return ins
        return sch.raw("gpsimd", fn, wait=list(wait) + [(s_gl, cpr_ld)],
                       inc=s_gl, inc_n=16)

    def ag(src, dst, wait=()):
        def fn(g):
            return g.collective_compute("AllGather", mybir.AluOpType.bypass,
                                        RG, [src.opt()], [dst.opt()])
        return sch.raw("gpsimd", fn, wait=wait, inc=s_cc, inc_n=1)

    def col_hook(sch_, fo, tile):
        return sch_.dma("sync",
                        cnf[0:1, fo * 128:(fo + 1) * 128].rearrange("a b -> b a"),
                        tile, mega.sem("s_st"))

    def _dbg(state):
        v = sch.dma("gpsimd", out[:, :], state[:, :], s_gl)
        sch.barrier_wait("sync", s_gl, v)
        with nc.Block() as block:
            sch.run(block)
        mega.ctx.close()
        return nc

    l0a_xf = nc.dram_tensor("l0a_xf", [D, S], F32, kind="ExternalInput").ap()
    l0a_xr = nc.dram_tensor("l0a_xo_in", [D, TOK], F32, kind="ExternalInput").ap()
    build_l0_attn(mega=mega, io={"xf": l0a_xf, "xo_in": l0a_xr, "xo": xc0})
    if upto == "l0a":
        return _dbg(xc0)
    build_l0_ffn(mega=mega, io={"xi": xc0, "xo": xc1})
    if upto == "l0f":
        return _dbg(xc1)

    ag(xc1, agx)
    agx_cc = sch.val(s_cc)
    if upto in ("agx0", "agx1"):
        u = int(upto[-1])
        v = sch.dma("gpsimd", out[:, :], agx[u, :, :], s_gl,
                    wait=[(s_cc, agx_cc)])
        sch.barrier_wait("sync", s_gl, v)
        with nc.Block() as block:
            sch.run(block)
        mega.ctx.close()
        return nc

    def xf_interleave(sch_, c, dst, sem):
        sch_.dma("gpsimd", dst.rearrange("p (u v) -> p u v", u=2),
                 agx[:, c * 128:(c + 1) * 128, :].rearrange("u p v -> p u v"),
                 sem, wait=[(s_cc, agx_cc)])

    build_l1_attn(mega=mega, pfx="l1a0_",
                  io={"xf_load": xf_interleave, "co": co2[0:256, :]})
    build_l1_attn(mega=mega, pfx="l1a1_",
                  io={"xf_load": xf_interleave, "co": co2[256:512, :]})
    if upto == "co2":
        v = sch.dma("gpsimd", out[0:512, :], co2[:, 0:TOK], s_gl)
        v = sch.dma("gpsimd", out[512:1024, :], co2[:, TOK:S], s_gl)
        sch.barrier_wait("sync", s_gl, v)
        with nc.Block() as block:
            sch.run(block)
        mega.ctx.close()
        return nc
    ag(co2, agc)
    agc_cc = sch.val(s_cc)
    ci_stage = dyn_stage(0, ci_loc[:, :], agc, [[S, D], [1, TOK]],
                         wait=[(s_cc, agc_cc)])

    def ci_load(sch_, c, tb, dst, wait=()):
        v = sch_.dma("sync", dst,
                     ci_loc[c * 128:(c + 1) * 128, tb * 512:(tb + 1) * 512],
                     mega.sem("s_ld"),
                     wait=list(wait) + [(s_gl, ci_stage)])
        return (mega.sem("s_ld"), v)

    build_l1f_pre(mega=mega, io={"xi": xc1, "ci_load": ci_load, "yo": yo})
    if upto == "l1fp":
        return _dbg(yo)

    sA, tA, sF = yo, xc0, xc1
    for i in range(6):
        sch.dma("gpsimd", cna[:, 0:512], sA[:, TOK - 512:TOK], s_gl)
        ag(cna, aga, wait=[(s_gl, sch.val(s_gl))])
        a_cc = sch.val(s_cc)
        h_stage = dyn_stage(1, halo_loc[:, :], aga, [[1024, D], [1, 512]],
                            wait=[(s_cc, a_cc)])

        def xv_load(sch_, c, dst, sem, h_stage=h_stage, sA=sA):
            sch_.dma("gpsimd", dst[:, 0:512], halo_loc[c * 128:(c + 1) * 128, :],
                     sem, wait=[(s_gl, h_stage)])
            sch_.dma("gpsimd", dst[:, 512:TV], sA[c * 128:(c + 1) * 128, :], sem)

        build_l1f_attn(ln_first=(i == 0), mega=mega, pfx=f"l1fa{i}_",
                       io={"xv_load": xv_load, "xr": sA, "xo": tA,
                           "col_hook": col_hook})
        if upto == f"i{i}a":
            return _dbg(tA)
        ag(cnf, agf)
        f_cc = sch.val(s_cc)
        f_stage = dyn_stage(2, halo1_loc[0:1, :], agf, [[1, 1], [1, D]],
                            wait=[(s_cc, f_cc)])

        def xvf_load(sch_, c, dst, sem, f_stage=f_stage, tA=tA):
            sch_.dma("gpsimd", dst[:, 0:1],
                     halo1_loc[0:1, c * 128:(c + 1) * 128]
                     .rearrange("a b -> b a"), sem, wait=[(s_gl, f_stage)])
            sch_.dma("gpsimd", dst[:, 1:TOK + 1], tA[c * 128:(c + 1) * 128, :], sem)

        io_f = {"xv_load": xvf_load, "xvr": tA, "xo": sF}
        if i == 5:
            io_f["ym"] = yo
        build_l1f_ffn(final=(i == 5), mega=mega, pfx=f"l1ff{i}_", io=io_f)
        if upto == f"i{i}f":
            return _dbg(sF)
        sA, tA, sF = xc1, xc0, xc1

    ag(xc1, agx)
    x2_cc = sch.val(s_cc)
    dyn_stage(3, xf_loc[:, 0:TOK], agx, [[TOK, D], [1, TOK]],
              wait=[(s_cc, x2_cc)])
    x2_st = dyn_stage(4, xf_loc[:, TOK:S], agx, [[TOK, D], [1, TOK]])

    def xf_oo2(sch_, c, dst, sem, st=x2_st):
        sch_.dma("gpsimd", dst, xf_loc[c * 128:(c + 1) * 128, :], sem,
                 wait=[(s_gl, st)])

    build_l2_attn(mega=mega, io={"xf_load": xf_oo2, "xo_in": xc1, "xo": xc0,
                                 "col_hook": col_hook})
    if upto == "l2a":
        return _dbg(xc0)
    ag(cnf, agf)
    h2_cc = sch.val(s_cc)
    h2_st = dyn_stage(2, halo1_loc[0:1, :], agf, [[1, 1], [1, D]],
                      wait=[(s_cc, h2_cc)])

    def xh_load(sch_, c, dst, sem, st=h2_st):
        sch_.dma("gpsimd", dst,
                 halo1_loc[0:1, c * 128:(c + 1) * 128].rearrange("a b -> b a"),
                 sem, wait=[(s_gl, st)])

    w1f = nc.dram_tensor("l3f_w1_in", [NE, D, DFF], F32, kind="ExternalInput").ap()
    w2f = nc.dram_tensor("l3f_w2_in", [NE, DFF, D], F32, kind="ExternalInput").ap()
    w1b = mega.scratch("l3f_w1b", [NE, D, DFF], BF16)
    w2b = mega.scratch("l3f_w2b", [NE, DFF, D], BF16)
    s_pc = mega.sem("s_pc")
    for e in range(NE):
        sch.dma("gpsimd", w1b[e, :, :], w1f[e, :, :], s_pc)
        sch.dma("gpsimd", w2b[e, :, :], w2f[e, :, :], s_pc)
    pc_done = sch.val(s_pc)

    build_l2_ffn(mega=mega, io={"xi": xc0, "xh_load": xh_load, "xo": xc1})
    if upto == "l2f":
        return _dbg(xc1)

    ag(xc1, agx)
    x3_cc = sch.val(s_cc)
    dyn_stage(3, xf_loc[:, 0:TOK], agx, [[TOK, D], [1, TOK]],
              wait=[(s_cc, x3_cc)])
    x3_st = dyn_stage(4, xf_loc[:, TOK:S], agx, [[TOK, D], [1, TOK]])

    def xf_oo3(sch_, c, dst, sem, st=x3_st):
        sch_.dma("gpsimd", dst, xf_loc[c * 128:(c + 1) * 128, :], sem,
                 wait=[(s_gl, st)])

    build_l3_attn(mega=mega, io={"xf_load": xf_oo3, "xo_in": xc1, "xo": xc0})
    if upto == "l3a":
        return _dbg(xc0)
    build_l3_ffn(mega=mega, io={"xi": xc0, "xo": out, "w1b": w1b,
                                "w2b": w2b, "wpc": (s_pc, pc_done)})
    with nc.Block() as block:
        sch.run(block)
    mega.ctx.close()
    return nc


MEGA_PHASES = ["l0a", "l0f", "l1a0", "l1a1", "l1fp",
               "i0a", "i0f", "i1a", "i1f", "i2a", "i2f",
               "i3a", "i3f", "i4a", "i4f", "i5a", "i5f",
               "l2a", "l2f", "l3a", "l3f"]


def mega_maps(x, inputs, upto=None):
    nlim = MEGA_PHASES.index(upto) + 1 if upto else len(MEGA_PHASES)
    ph = MEGA_PHASES[:nlim]
    maps = [dict() for _ in range(NC)]

    def put(pfx, core, m):
        for k, v in m.items():
            maps[core][pfx + k] = v

    for core, m in enumerate(host_l0_attn(x, inputs["L0_attn_W_qkv"],
                                          inputs["L0_attn_b_qkv"])):
        put("l0a_", core, m)
    if "l0f" in ph:
        sh = host_l0_ffn(inputs["L0_ffn_router_weight"], inputs["L0_ffn_gate_up_proj"],
                         inputs["L0_ffn_gate_up_proj_bias"], inputs["L0_ffn_down_proj"],
                         inputs["L0_ffn_down_proj_bias"])
        for core in range(NC):
            put("l0f_", core, sh)
    if "l1a0" in ph:
        per = host_l1_attn(inputs["L1_attn_wq"], inputs["L1_attn_wk"],
                           inputs["L1_attn_wv"], inputs["L1_attn_bq"],
                           inputs["L1_attn_bk"], inputs["L1_attn_bv"],
                           inputs["L1_attn_dist_emb"])
        for li in range(2):
            if f"l1a{li}" in ph:
                for core in range(NC):
                    put(f"l1a{li}_", core, per(core, li))
    if "i0a" in ph:
        att_in, ffn_in = host_l1f(inputs["L1_ffn_Wqkv"], inputs["L1_ffn_q_scale"],
                                  inputs["L1_ffn_k_scale"], inputs["L1_ffn_Wo"],
                                  inputs["L1_ffn_ln_g"], inputs["L1_ffn_ln_b"],
                                  inputs["L1_ffn_W1"], inputs["L1_ffn_b1"],
                                  inputs["L1_ffn_W2"], inputs["L1_ffn_b2"])
        for i in range(6):
            if f"i{i}a" in ph:
                for core in range(NC):
                    put(f"l1fa{i}_", core, att_in(i, core))
            if f"i{i}f" in ph:
                sh = ffn_in(i)
                for core in range(NC):
                    put(f"l1ff{i}_", core, sh)
    if "l2a" in ph:
        sh = host_l2_attn(inputs["L2_attn_w_qkv"], inputs["L2_attn_w_proj"],
                          inputs["L2_attn_b_proj"])
        for core in range(NC):
            put("l2a_", core, sh)
    if "l2f" in ph:
        sh = host_l2_ffn(inputs["L2_ffn_W_r"], inputs["L2_ffn_W_k"],
                         inputs["L2_ffn_W_v"], inputs["L2_ffn_mu_r"],
                         inputs["L2_ffn_mu_k"])
        for core in range(NC):
            put("l2f_", core, sh)
    if "l3a" in ph:
        sh = host_l3_attn(inputs["L3_attn_qkv_w"], inputs["L3_attn_qkv_b"],
                          inputs["L3_attn_proj_w"], inputs["L3_attn_proj_b"])
        for core in range(NC):
            put("l3a_", core, sh)
            h = core % 2
            kv = np.ones((128, NKT), np.float32)
            if h == 0:
                kv[:, 0:16] = 0.0
            maps[core]["l3a_kvis"] = kv
    if "l3f" in ph:
        sh = host_l3_ffn(inputs["L3_ffn_rw1"], inputs["L3_ffn_rb1"],
                         inputs["L3_ffn_rw2"], inputs["L3_ffn_rb2"],
                         inputs["L3_ffn_W1"], inputs["L3_ffn_b1"],
                         inputs["L3_ffn_W2"], inputs["L3_ffn_b2"])
        for core in range(NC):
            put("l3f_", core, sh)
    zc = np.zeros((D, 512), np.float32)
    for core in range(NC):
        h = core % 2
        t = np.zeros((1, 64), np.int32)
        t[0, 0] = h * TOK
        t[0, 1] = (1 - h) * 512
        t[0, 2] = (1 - h) * D
        t[0, 3] = (1 - h) * D * TOK
        t[0, 4] = h * D * TOK
        maps[core]["cprm"] = t
        maps[core]["zcol"] = zc
    return maps


def _run(nc, maps):
    return run_bass_kernel_spmd(nc, maps, core_ids=list(range(NC))).results


def kernel(**inputs):
    x = np.asarray(inputs["L0_attn_x"], np.float32)
    nc = build_mega()
    maps = mega_maps(x, inputs)
    res = _run(nc, maps)
    return assemble_fm(res, key="out")


def kernel_unfused(**inputs):
    x = np.asarray(inputs["L0_attn_x"], np.float32)

    res = _run(build_l0_attn(),
               host_l0_attn(x, inputs["L0_attn_W_qkv"], inputs["L0_attn_b_qkv"]))
    x = assemble_fm(res)

    shared = host_l0_ffn(inputs["L0_ffn_router_weight"], inputs["L0_ffn_gate_up_proj"],
                         inputs["L0_ffn_gate_up_proj_bias"], inputs["L0_ffn_down_proj"],
                         inputs["L0_ffn_down_proj_bias"])
    maps = []
    for core in range(NC):
        b, h = core // 2, core % 2
        m = dict(shared)
        m["xi"] = fmajor(x[b, h * TOK:(h + 1) * TOK])
        maps.append(m)
    x = assemble_fm(_run(build_l0_ffn(), maps))

    per = host_l1_attn(inputs["L1_attn_wq"], inputs["L1_attn_wk"], inputs["L1_attn_wv"],
                       inputs["L1_attn_bq"], inputs["L1_attn_bk"], inputs["L1_attn_bv"],
                       inputs["L1_attn_dist_emb"])
    nc1 = build_l1_attn()
    ctx = np.empty((B, S, D), np.float32)
    for li in range(2):
        maps = []
        for core in range(NC):
            b = core // 2
            m = per(core, li)
            m["xf"] = fmajor(x[b])
            maps.append(m)
        res = _run(nc1, maps)
        for core in range(NC):
            b, h2 = core // 2, core % 2
            c0 = (h2 * 8 + li * 4) * 64
            ctx[b, :, c0:c0 + 256] = np.asarray(res[core]["co"]).T

    maps = []
    for core in range(NC):
        b, h = core // 2, core % 2
        maps.append({"xi": fmajor(x[b, h * TOK:(h + 1) * TOK]),
                     "ci": fmajor(ctx[b, h * TOK:(h + 1) * TOK])})
    y = assemble_fm(_run(build_l1f_pre(), maps), key="yo")

    att_in, ffn_in = host_l1f(inputs["L1_ffn_Wqkv"], inputs["L1_ffn_q_scale"],
                              inputs["L1_ffn_k_scale"], inputs["L1_ffn_Wo"],
                              inputs["L1_ffn_ln_g"], inputs["L1_ffn_ln_b"],
                              inputs["L1_ffn_W1"], inputs["L1_ffn_b1"],
                              inputs["L1_ffn_W2"], inputs["L1_ffn_b2"])
    nc_a0 = build_l1f_attn(ln_first=True)
    nc_a = build_l1f_attn(ln_first=False)
    nc_f = build_l1f_ffn(final=False)
    nc_ff = build_l1f_ffn(final=True)
    state = y.copy()
    for i in range(6):
        maps = []
        for core in range(NC):
            b, h2 = core // 2, core % 2
            m = att_in(i, core)
            own0 = h2 * TOK
            xvv = np.zeros((TV, D), np.float32)
            lo = own0 - 512
            src0 = max(0, lo)
            xvv[src0 - lo:, :] = state[b, src0:own0 + TOK]
            m["xv"] = fmajor(xvv)
            m["xr"] = fmajor(state[b, own0:own0 + TOK])
            maps.append(m)
        res = _run(nc_a0 if i == 0 else nc_a, maps)
        for core in range(NC):
            b, h2 = core // 2, core % 2
            state[b, h2 * TOK:(h2 + 1) * TOK] = np.asarray(res[core]["xo"]).T
        maps = []
        for core in range(NC):
            b, h2 = core // 2, core % 2
            m = ffn_in(i)
            own0 = h2 * TOK
            xvv = np.zeros((TOK + 1, D), np.float32)
            if own0 > 0:
                xvv[0] = state[b, own0 - 1]
            xvv[1:] = state[b, own0:own0 + TOK]
            m["xv"] = fmajor(xvv)
            if i == 5:
                m["ym"] = fmajor(y[b, own0:own0 + TOK])
            maps.append(m)
        res = _run(nc_ff if i == 5 else nc_f, maps)
        for core in range(NC):
            b, h2 = core // 2, core % 2
            state[b, h2 * TOK:(h2 + 1) * TOK] = np.asarray(res[core]["xo"]).T
    x = state

    shared = host_l2_attn(inputs["L2_attn_w_qkv"], inputs["L2_attn_w_proj"],
                          inputs["L2_attn_b_proj"])
    maps = []
    for core in range(NC):
        b, h = core // 2, core % 2
        m = dict(shared)
        own = x[b, h * TOK:(h + 1) * TOK]
        other = x[b, (1 - h) * TOK:(2 - h) * TOK]
        m["xf"] = fmajor(np.concatenate([other, own], axis=0))
        m["xo_in"] = fmajor(own)
        maps.append(m)
    x = assemble_fm(_run(build_l2_attn(), maps))

    shared = host_l2_ffn(inputs["L2_ffn_W_r"], inputs["L2_ffn_W_k"], inputs["L2_ffn_W_v"],
                         inputs["L2_ffn_mu_r"], inputs["L2_ffn_mu_k"])
    maps = []
    for core in range(NC):
        b, h = core // 2, core % 2
        m = dict(shared)
        m["xi"] = fmajor(x[b, h * TOK:(h + 1) * TOK])
        halo = np.zeros((1, D), np.float32) if h == 0 else x[b, h * TOK - 1:h * TOK]
        m["xh"] = fmajor(halo)
        maps.append(m)
    x = assemble_fm(_run(build_l2_ffn(), maps))

    shared = host_l3_attn(inputs["L3_attn_qkv_w"], inputs["L3_attn_qkv_b"],
                          inputs["L3_attn_proj_w"], inputs["L3_attn_proj_b"])
    maps = []
    for core in range(NC):
        m = dict(shared)
        m.update(host_l3_attn_core(x, core))
        maps.append(m)
    x = assemble_fm(_run(build_l3_attn(), maps))

    shared = host_l3_ffn(inputs["L3_ffn_rw1"], inputs["L3_ffn_rb1"], inputs["L3_ffn_rw2"],
                         inputs["L3_ffn_rb2"], inputs["L3_ffn_W1"], inputs["L3_ffn_b1"],
                         inputs["L3_ffn_W2"], inputs["L3_ffn_b2"])
    maps = []
    for core in range(NC):
        b, h = core // 2, core % 2
        m = dict(shared)
        m["xi"] = fmajor(x[b, h * TOK:(h + 1) * TOK])
        maps.append(m)
    x = assemble_fm(_run(build_l3_ffn(), maps))
    return x
